# version 01
# baseline (speedup 1.0000x reference)
import os

os.environ.setdefault("TRN_ALLOW_ICI_ALL_TO_ALL", "1")

import numpy as np
import ml_dtypes
from contextlib import ExitStack

import concourse.bass as bass
import concourse.mybir as mybir
from concourse.bass_utils import run_bass_kernel_spmd

F32 = mybir.dt.float32
BF16 = mybir.dt.bfloat16
AF = mybir.ActivationFunctionType
ALU = mybir.AluOpType
AX = mybir.AxisListType

NC = 8
T = 8192
TL = T // NC
D = 2048
NF = 16
H = 16
DH = 128
E = 8
II = 1024
R = 16
CH = 128
SC = 256
NSC = T // SC
CPS = SC // CH
QKS = 0.08838834764831845

ENGINES = ("sync", "scalar", "vector", "tensor", "gpsimd")


class Sem:
    __slots__ = ("h", "n", "name")

    def __init__(self, h, name):
        self.h = h
        self.n = 0
        self.name = name


class Builder:
    def __init__(self, nc, ctx):
        self.nc = nc
        self.ctx = ctx
        self.q = {e: [] for e in ENGINES}

    def sem(self, name):
        return Sem(self.ctx.enter_context(self.nc.semaphore(name)), name)

    def emit(self, eng, fn, waits=(), inc=None, amt=1):
        best = {}
        for t in waits:
            if t is None:
                continue
            s, v = t
            assert v <= s.n, f"wait {s.name}>={v} but only {s.n} emitted"
            if best.get(s.name, (None, -1))[1] < v:
                best[s.name] = (s, v)
        tok = None
        if inc is not None:
            inc.n += amt
            tok = (inc, inc.n)
        self.q[eng].append((fn, list(best.values()), (inc, amt) if inc else None))
        return tok

    def run(self, eng_name, eng):
        n = 0
        for fn, waits, inc in self.q[eng_name]:
            for s, v in waits:
                eng.wait_ge(s.h, v)
            inst = fn(eng)
            if inc is not None:
                s, amt = inc
                inst.then_inc(s.h, amt)
            n += 1
        if n == 0:
            eng.nop()


def build_nc(stage="full"):
    nc = bass.Bass("TRN2", target_bir_lowering=False, debug=False)
    ctx = ExitStack()
    b = Builder(nc, ctx)

    def din(name, shape, dt=F32):
        return nc.dram_tensor(name, shape, dt, kind="ExternalInput").ap()

    xT = din("xT", [D, TL])
    cT_d = din("cT", [128, 16])
    wmod = din("wmod", [D, 6 * D // NC])
    wq = din("wq", [D, 2 * DH])
    wk = din("wk", [D, 2 * DH])
    wv = din("wv", [D, 2 * DH])
    wb = din("wb", [D, H])
    onw = din("onw", [128, 1])
    wo = din("wo", [D, D])
    gwT = din("gwT", [D, E])
    gA = din("gA", [R, D])
    gBT = din("gBT", [R, E])
    gupw = din("gupw", [E, D, 2 * II])
    dnw = din("dnw", [E, II, D])
    guAT_all = din("guAT_all", [D, E * R])
    guBT = din("guBT", [E, R, 2 * II])
    dnAT = din("dnAT", [E, II, R])
    dnBT = din("dnBT", [E, R, D])
    identf = din("identf", [128, 128])
    identb = din("identb", [128, 128], BF16)
    ones_cb = din("ones_cb", [128, 1], BF16)
    ones_rb = din("ones_rb", [1, 128], BF16)
    ones_rf = din("ones_rf", [1, 128])
    mask_sl = din("mask_sl", [128, 128], BF16)
    mask_su = din("mask_su", [128, 128], BF16)
    mask_ul = din("mask_ul", [128, 128], BF16)
    cstc = din("cstc", [128, 16])

    outT = nc.dram_tensor("outT", [D, TL], F32, kind="ExternalOutput").ap()
    dbg = {}
    if stage == "h1":
        dbg["h1T_dbg"] = nc.dram_tensor("h1T_dbg", [D, TL], F32, kind="ExternalOutput").ap()
        dbg["bet_dbg"] = nc.dram_tensor("bet_dbg", [H, TL], F32, kind="ExternalOutput").ap()
        dbg["mod_dbg"] = nc.dram_tensor("mod_dbg", [128, 128], F32, kind="ExternalOutput").ap()

    SCAN_DBG = False
    if stage == "scan":
        dbg["o_dbg"] = nc.dram_tensor("o_dbg", [2 * DH, T], F32, kind="ExternalOutput").ap()
        if SCAN_DBG:
            dbg["scr_dbg"] = nc.dram_tensor("scr_dbg", [128, 9088], BF16, kind="ExternalOutput").ap()
            dbg["sm_dbg"] = nc.dram_tensor("sm_dbg", [128, 2 * DH], F32, kind="ExternalOutput").ap()
            dbg["bc_dbg"] = nc.dram_tensor("bc_dbg", [128, 4], F32, kind="ExternalOutput").ap()
    if stage == "attn":
        dbg["x2_dbg"] = nc.dram_tensor("x2_dbg", [D, TL], F32, kind="ExternalOutput").ap()
        dbg["h2_dbg"] = nc.dram_tensor("h2_dbg", [D, TL], F32, kind="ExternalOutput").ap()
        dbg["w_dbg"] = nc.dram_tensor("w_dbg", [E, TL], F32, kind="ExternalOutput").ap()

    mod_loc = nc.dram_tensor("mod_loc", [1, 6 * D // NC], F32).ap()
    mod_all = nc.dram_tensor("mod_all", [NC, 6 * D // NC], F32).ap()
    h1_loc = nc.dram_tensor("h1_loc", [D, TL], BF16).ap()
    h1_all = nc.dram_tensor("h1_all", [NC * D, TL], BF16).ap()
    bet_loc = nc.dram_tensor("bet_loc", [H, TL], F32).ap()
    bet_all = nc.dram_tensor("bet_all", [H, TL], F32).ap()
    o_loc = nc.dram_tensor("o_loc", [2 * DH * NC, TL], BF16).ap()
    o_all = nc.dram_tensor("o_all", [2 * DH * NC, TL], BF16).ap()
    tal_d = nc.dram_tensor("tal_d", [E * R, TL], BF16).ap()

    def sb(name, shape, dt=F32):
        return ctx.enter_context(nc.sbuf_tensor(name, shape, dt))

    big_a = sb("big_a", [128, NF * TL], F32)
    big_b = sb("big_b", [128, NF * TL], BF16)
    big_c = sb("big_c", [128, 8192], BF16)
    big_d = sb("big_d", [128, 9088], BF16)
    dwn_b = sb("dwn_b", [128, 8 * D], BF16)
    stage_f = sb("stage_f", [128, 2048], F32)
    rep1 = sb("rep1", [128, 512], F32)
    rep2 = sb("rep2", [128, 512], F32)
    rep_s = sb("rep_s", [128, 1024], F32)
    rh = sb("rh", [128, 1536], F32)
    rowb = sb("rowb", [1, 1024], BF16)
    c_idf = sb("c_idf", [128, 128], F32)
    c_idb = sb("c_idb", [128, 128], BF16)
    c_ocb = sb("c_ocb", [128, 1], BF16)
    c_orb = sb("c_orb", [1, 128], BF16)
    c_orf = sb("c_orf", [1, 128], F32)
    c_msl = sb("c_msl", [128, 128], BF16)
    c_msu = sb("c_msu", [128, 128], BF16)
    c_mul = sb("c_mul", [128, 128], BF16)
    c_onw = sb("c_onw", [128, 1], F32)
    cT_s = sb("cT_s", [128, 16], F32)
    cS = sb("cS", [128, 16], F32)
    mvec = sb("mvec", [128, 128], F32)
    bcols = sb("bcols", [128, 4], F32)
    
    S_m = sb("S_m", [128, 2 * DH], F32)
    S_b = sb("S_b", [128, 2 * DH], BF16)
    lo16 = sb("lo16", [128, 4096], BF16)
    lo16f = sb("lo16f", [64, 512], F32)
    weff = sb("weff", [128, NF * E], BF16)
    gw_b = sb("gw_b", [128, NF * E], BF16)
    wtok = sb("wtok", [128, 8 * E], F32)
    wrowb = sb("wrowb", [E, TL], BF16)
    dnA_s = sb("dnA_s", [128, 8 * R], BF16)

    R_MOD = lambda: rh[96:97, 0:1536]
    S1 = lambda: rep_s[:, 0:512]
    S2 = lambda: rep_s[:, 512:1024]
    L2A = lambda: rep_s[:, 640:896]
    L2B = lambda: rep_s[:, 0:256]
    ON1 = lambda: rep_s[:, 256:384]
    ON2 = lambda: rep_s[:, 896:1024]

    LNB1 = lambda par: big_c[:, par * 2048: par * 2048 + 1024]
    LNB2 = lambda par: big_c[:, par * 2048 + 1024: par * 2048 + 2048]
    LNF = lambda: rep_s[:, 0:512]
    QB = lambda sp, idx: slice(sp * 2304 + idx * 256, sp * 2304 + (idx + 1) * 256)
    SCR = lambda hs, slot: slice(4608 + hs * 2048 + slot * 128, 4608 + hs * 2048 + (slot + 1) * 128)
    SQ2 = slice(8704, 8960)
    SQ1 = slice(8960, 9088)
    (SL_A, SL_A2, SL_A4, SL_A8, SL_A16, SL_A32, SL_A64, SL_B, SL_MT, SL_W0,
     SL_U, SL_TT, SL_KT, SL_KB, SL_VB, SL_X) = range(16)
    oT_col = lambda hs, c: slice(hs * 8192 + c * 128, hs * 8192 + (c + 1) * 128)
    bm_sl = lambda hs: slice(384 + hs * 128, 384 + (hs + 1) * 128)

    cst = sb("cst", [128, 16], F32)
    C_ZERO = cst[:, 0:1]
    C_TWO = cst[:, 1:2]
    C_NEG1 = cst[:, 2:3]
    C_QTR = cst[:, 3:4]
    C_ONE = cst[:, 4:5]
    C_EPS = cst[:, 5:6]
    C_TINY = cst[:, 6:7]
    C_QKS = cst[:, 7:8]
    C_INVD = cst[:, 8:9]
    C_INVDH = cst[:, 9:10]
    C_NEG30 = cst[:, 10:11]
    assert nc.sbuf_bytes_remaining >= 0

    def pst(name, shape, dt=F32):
        return ctx.enter_context(nc.psum_tensor(name, shape, dt))

    pb = [pst(f"pb{i}", [128, 512]) for i in range(7)]
    ptb = pst("ptb", [128, 1024], BF16)
    PROJ, ROWS, REPS = pb[0], pb[1], pb[2]
    SCB_ = [(pb[3], pb[4]), (pb[5], pb[6])]

    sT = b.sem("sT")
    sV = b.sem("sV")
    sS = b.sem("sS")
    cc = b.sem("cc")
    sd_c = b.sem("sd_c")
    sd_x = b.sem("sd_x")
    sd_w = [b.sem("sd_w0"), b.sem("sd_w1")]
    sd_h1 = b.sem("sd_h1")
    sd_bet = b.sem("sd_bet")
    sd_mvec = b.sem("sd_mvec")
    sd_hin = [b.sem("sd_hin0"), b.sem("sd_hin1")]
    sd_bcol = b.sem("sd_bcol")
    sd_o = b.sem("sd_o")
    sd_oall = b.sem("sd_oall")
    sd_tal = b.sem("sd_tal")
    sd_te = b.sem("sd_te")
    sd_h1b = b.sem("sd_h1b")
    sd_wr = b.sem("sd_wr")
    sd_d = [b.sem("sd_d0"), b.sem("sd_d1")]
    sd_out = b.sem("sd_out")

    bank_tok = {id(p): None for p in pb}
    bank_tok[id(ptb)] = None

    def mm_group(bank, reg, pairs, waits, inc=None):
        n = len(pairs)
        tok = None
        for k, (lh, rhs_) in enumerate(pairs):
            w = []
            if k == 0:
                w = list(waits)
                if bank_tok[id(bank)] is not None:
                    w.append(bank_tok[id(bank)])
            ap = bank[reg[0], reg[1]] if isinstance(reg, tuple) else bank[:, reg]
            tok = b.emit("tensor", lambda e, lh=lh, rhs_=rhs_, k=k, ap=ap: e.matmul(
                ap, lh, rhs_, start=(k == 0), stop=(k == n - 1)),
                waits=w, inc=inc if k == n - 1 else None)
        return tok

    def bank_done(bank, tok):
        bank_tok[id(bank)] = tok

    def tr(bank, reg, in_, ident, waits, inc=None):
        w = list(waits)
        if bank_tok[id(bank)] is not None:
            w.append(bank_tok[id(bank)])
        ap = bank[reg[0], reg[1]] if isinstance(reg, tuple) else bank[:, reg]
        return b.emit("tensor", lambda e, ap=ap: e.transpose(ap, in_, ident), waits=w, inc=inc)

    stage_free = [None, None]
    stream_i = [0]
    stream_last = {"vector": None, "scalar": None}

    def stream_sync():
        return [stream_last["vector"], stream_last["scalar"]]

    def stream_piece(src_fn, dst_fn, extra_load_waits=(), extra_cv_waits=(), rows=128, ncols=1024, a=None):
        i = stream_i[0]
        stream_i[0] += 1
        sl = i % 2
        so = sl * 1024
        if a is None:
            out_ap = stage_f[0:rows, so:so + ncols]
        else:
            out_ap = stage_f[0:rows, so:so + ncols].rearrange("p (a c) -> p a c", a=a)
        tload = b.emit("sync", lambda e, out_ap=out_ap: e.dma_start(out=out_ap, in_=src_fn()),
                       waits=[stage_free[sl]] + list(extra_load_waits), inc=sd_w[sl], amt=16)
        eng = "vector" if i % 2 == 0 else "scalar"
        src_sb = stage_f[0:rows, so:so + ncols]
        if eng == "vector":
            fn = lambda e, src_sb=src_sb: e.tensor_copy(dst_fn(), src_sb)
        else:
            fn = lambda e, src_sb=src_sb: e.activation(dst_fn(), src_sb, AF.Copy)
        cv = b.emit(eng, fn, waits=[tload] + list(extra_cv_waits), inc=sV if eng == "vector" else sS)
        stage_free[sl] = cv
        stream_last[eng] = cv
        return cv

    tk_c = None
    for out, src in [(c_idf[:], identf), (c_idb[:], identb), (c_ocb[:], ones_cb),
                     (c_orb[:], ones_rb), (c_orf[:], ones_rf), (c_msl[:], mask_sl),
                     (c_msu[:], mask_su), (c_mul[:], mask_ul), (c_onw[:], onw),
                     (cT_s[:], cT_d), (cst[:], cstc)]:
        tk_c = b.emit("sync", lambda e, o=out, s=src: e.dma_start(out=o, in_=s[:, :]), inc=sd_c, amt=16)
    tk_x = b.emit("sync", lambda e: e.dma_start(
        out=big_a[:].rearrange("p (f t) -> p f t", f=NF),
        in_=xT.rearrange("(f p) t -> p f t", p=128)), inc=sd_x, amt=16)
    tk_cS = b.emit("scalar", lambda e: e.activation(cS[:], cT_s[:], AF.Silu), waits=[tk_c], inc=sS)

    MODB = [pb[3], pb[4], pb[5]]
    tk_modmm = None
    for f in range(16):
        for n in range(3):
            i = stream_i[0]
            stream_i[0] += 1
            sl = i % 2
            so = sl * 1024
            tload = b.emit("sync", lambda e, f=f, n=n, so=so: e.dma_start(
                out=stage_f[:, so:so + 512],
                in_=wmod[128 * f:128 * (f + 1), 512 * n:512 * (n + 1)]),
                waits=[stage_free[sl]], inc=sd_w[sl], amt=16)
            w = [tload]
            if f == 0:
                w += [tk_cS, bank_tok[id(MODB[n])]]
            tk_modmm = b.emit("tensor", lambda e, f=f, n=n, so=so: e.matmul(
                MODB[n][0:1, 0:512], cS[:, f:f + 1], stage_f[:, so:so + 512],
                start=(f == 0), stop=(f == 15)), waits=w, inc=sT)
            stage_free[sl] = tk_modmm
    tk_mode = None
    for n in range(3):
        tk_mode = b.emit("vector", lambda e, n=n: e.tensor_copy(
            R_MOD()[0:1, 512 * n:512 * (n + 1)], MODB[n][0:1, 0:512]), waits=[tk_modmm], inc=sV)
        bank_done(MODB[n], tk_mode)
    tk_modst = b.emit("scalar", lambda e: e.dma_start(out=mod_loc[:, :], in_=R_MOD()),
                      waits=[tk_mode], inc=sd_h1, amt=16)
    tk_ag1 = b.emit("gpsimd", lambda e: e.collective_compute(
        "AllGather", ALU.bypass, replica_groups=[list(range(NC))],
        ins=[mod_loc], outs=[mod_all]), waits=[tk_modst], inc=cc)
    tk_mvl = b.emit("sync", lambda e: e.dma_start(
        out=rep_s[0:96, 0:128],
        in_=mod_all.rearrange("a b -> (a b)").rearrange("(q r) -> q r", r=128)),
        waits=[tk_ag1], inc=sd_mvec, amt=16)
    tk_mvt = tr(REPS, slice(0, 96), rep_s[0:96, 0:128], c_idf[0:96, 0:96], [tk_mvl, tk_c], inc=sT)
    tk_mv = b.emit("vector", lambda e: e.tensor_copy(mvec[:, 0:96], REPS[:, 0:96]),
                   waits=[tk_mvt], inc=sV)
    bank_done(REPS, tk_mv)

    def clamp_emit(v, prev):
        sl = slice(v * 16, v * 16 + 16)
        t1 = b.emit("vector", lambda e: e.tensor_scalar(rep_s[:, 128:144], mvec[:, sl], C_ZERO, None, ALU.is_ge),
                    waits=[tk_mv, prev, tk_c], inc=sV)
        t2 = b.emit("vector", lambda e: e.tensor_scalar(rep_s[:, 160:176], rep_s[:, 128:144], C_TWO, C_NEG1, ALU.mult, ALU.add),
                    waits=[t1], inc=sV)
        t3 = b.emit("scalar", lambda e: e.activation(rep_s[:, 144:160], mvec[:, sl], AF.Abs), waits=[tk_mv, prev], inc=sS)
        t4 = b.emit("vector", lambda e: e.tensor_scalar(rep_s[:, 176:192], rep_s[:, 144:160], C_QTR, None, ALU.max),
                    waits=[t3], inc=sV)
        return b.emit("vector", lambda e: e.tensor_tensor(mvec[:, sl], rep_s[:, 160:176], rep_s[:, 176:192], ALU.mult),
                      waits=[t2, t4], inc=sV)

    tk_ga = clamp_emit(2, None)
    tk_gb = clamp_emit(5, tk_ga)
    tk_ca = b.emit("vector", lambda e: e.tensor_scalar(mvec[:, 96:112], mvec[:, 16:32], C_ONE, None, ALU.add),
                   waits=[tk_mv, tk_c], inc=sV)
    tk_cb = b.emit("vector", lambda e: e.tensor_scalar(mvec[:, 112:128], mvec[:, 64:80], C_ONE, None, ALU.add),
                   waits=[tk_mv, tk_c], inc=sV)

    def emit_ln(src, dst, cmod, smod, wait_stats, wait_apply, rowbanks):
        bsum = [rowbanks[0], rowbanks[1]]
        bsq = [rowbanks[2], rowbanks[3]]
        tk_mm = None
        mm_toks = [None] * NF
        for f in range(NF):
            par = f % 2
            w_ov = [mm_toks[f - 2]] if f >= 2 else []
            tb = b.emit("vector", lambda e, f=f, par=par: e.tensor_copy(LNB2(par), src(f)),
                        waits=wait_stats(f) + w_ov, inc=sV)
            ts_ = b.emit("scalar", lambda e, f=f, par=par: e.activation(LNB1(par), src(f), AF.Square),
                         waits=wait_stats(f) + w_ov, inc=sS)
            for half in range(2):
                sl = slice(half * 512, half * 512 + 512)
                b.emit("tensor", lambda e, half=half, sl=sl, f=f, par=par: e.matmul(
                    bsum[half][0:1, 0:512], c_ocb[:, 0:1], LNB2(par)[:, sl],
                    start=(f == 0), stop=(f == NF - 1)),
                    waits=[tb, bank_tok[id(bsum[half])]] if f == 0 else [tb],
                )
                tk_mm = b.emit("tensor", lambda e, half=half, sl=sl, f=f, par=par: e.matmul(
                    bsq[half][0:1, 0:512], c_ocb[:, 0:1], LNB1(par)[:, sl],
                    start=(f == 0), stop=(f == NF - 1)),
                    waits=[ts_, bank_tok[id(bsq[half])]] if f == 0 else [ts_],
                    inc=sT)
            mm_toks[f] = tk_mm
        tk_apply = None
        reps_tok = None
        for half in range(2):
            sl = slice(half * 512, half * 512 + 512)
            c1 = b.emit("vector", lambda e, half=half: e.tensor_copy(
                rowb[0:1, 0:512], bsum[half][0:1, 0:512]), waits=[tk_mm, reps_tok], inc=sV)
            c2 = b.emit("vector", lambda e, half=half: e.tensor_copy(
                rowb[0:1, 512:1024], bsq[half][0:1, 0:512]), waits=[tk_mm, reps_tok], inc=sV)
            if half == 1:
                for bk in rowbanks:
                    bank_done(bk, c2)
            t1 = mm_group(REPS, slice(0, 512), [(c_orb[0:1, :], rowb[0:1, 0:512])], [c1, reps_tok], inc=sT)
            em = b.emit("vector", lambda e: e.tensor_scalar(rep2[:, :], REPS[:, 0:512], C_INVD, None, ALU.mult),
                        waits=[t1, tk_c], inc=sV)
            bank_done(REPS, em)
            t2 = mm_group(REPS, slice(0, 512), [(c_orb[0:1, :], rowb[0:1, 512:1024])], [c2], inc=sT)
            reps_tok = t2
            tms = b.emit("vector", lambda e: e.tensor_tensor(S1(), rep2[:, :], rep2[:, :], ALU.mult),
                         waits=[em], inc=sV)
            ev = b.emit("vector", lambda e: e.scalar_tensor_tensor(S2(), REPS[:, 0:512], C_INVD, S1(),
                                                                   ALU.mult, ALU.subtract), waits=[t2, tms], inc=sV)
            bank_done(REPS, ev)
            tv = b.emit("vector", lambda e: e.tensor_scalar(S1(), S2(), C_EPS, None, ALU.add),
                        waits=[ev], inc=sV)
            tq = b.emit("scalar", lambda e: e.activation(S2(), S1(), AF.Sqrt), waits=[tv], inc=sS)
            ti = b.emit("vector", lambda e: e.reciprocal(rep1[:, :], S2()), waits=[tq], inc=sV)
            tmi = b.emit("vector", lambda e: e.tensor_tensor(S1(), rep2[:, :], rep1[:, :], ALU.mult),
                         waits=[ti], inc=sV)
            for f in range(NF):
                ta1 = b.emit("vector", lambda e, f=f, sl=sl: e.tensor_tensor(
                    S2(), src(f)[:, sl], rep1[:, :], ALU.mult), waits=[tmi] + list(wait_apply), inc=sV)
                ta2 = b.emit("vector", lambda e: e.tensor_sub(rep2[:, :], S2(), S1()), waits=[ta1], inc=sV)
                tk_apply = b.emit("vector", lambda e, f=f, sl=sl: e.tensor_scalar(
                    dst(f)[:, sl], rep2[:, :], cmod(f), smod(f), ALU.mult, ALU.add), waits=[ta2], inc=sV)
        return tk_apply

    tk_h1 = emit_ln(
        src=lambda f: big_a[:, f * TL:(f + 1) * TL],
        dst=lambda f: big_b[:, f * TL:(f + 1) * TL],
        cmod=lambda f: mvec[:, 96 + f:97 + f],
        smod=lambda f: mvec[:, f:f + 1],
        wait_stats=lambda f: [tk_x],
        wait_apply=[tk_ca, tk_ga, tk_gb, tk_cb],
        rowbanks=(PROJ, ROWS, pb[3], pb[6]),
    )
    tk_h1st = b.emit("scalar", lambda e: e.dma_start(
        out=h1_loc.rearrange("(f p) t -> p f t", p=128),
        in_=big_b[:].rearrange("p (f t) -> p f t", f=NF)), waits=[tk_h1], inc=sd_h1b, amt=16)
    tk_ag2 = b.emit("gpsimd", lambda e: e.collective_compute(
        "AllGather", ALU.bypass, replica_groups=[list(range(NC))],
        ins=[h1_loc], outs=[h1_all]), waits=[tk_h1st], inc=cc)

    tk_wbc = None
    for hf in range(2):
        tk_wbc = stream_piece(
            lambda hf=hf: wb.rearrange("(a p) c -> p a c", p=128)[:, hf * 8:(hf + 1) * 8, :],
            lambda hf=hf: dwn_b[:, 12288 + hf * 128:12288 + (hf + 1) * 128],
            rows=128, ncols=128, a=8)
    tk_betmm = [None, None]
    for half in range(2):
        tk_betmm[half] = mm_group(
            [pb[3], pb[4]][half], (slice(0, H), slice(0, 512)),
            [(dwn_b[:, 12288 + f * 16:12288 + (f + 1) * 16],
              big_b[:, f * TL + half * 512: f * TL + half * 512 + 512]) for f in range(NF)],
            [tk_h1] + stream_sync(), inc=sT)
    tk_betst = None
    prev_betst = None
    for half in range(2):
        tk_bete = b.emit("scalar", lambda e, half=half: e.activation(
            lo16f[0:16, 0:512], [pb[3], pb[4]][half][0:H, 0:512], AF.Sigmoid),
            waits=[tk_betmm[half], prev_betst], inc=sS)
        bank_done([pb[3], pb[4]][half], tk_bete)
        tk_betst = b.emit("scalar", lambda e, half=half: e.dma_start(
            out=bet_loc[:, half * 512: half * 512 + 512], in_=lo16f[0:16, :]),
            waits=[tk_bete], inc=sd_bet, amt=16)
        prev_betst = tk_betst
    tk_a2a_b = b.emit("gpsimd", lambda e: e.collective_compute(
        "AllToAll", ALU.bypass, replica_groups=[list(range(NC))],
        ins=[bet_loc], outs=[bet_all]), waits=[tk_betst], inc=cc)

    def emit_done(extra):
        b.emit("gpsimd", lambda e: e.nop(),
               waits=[(sd_out, sd_out.n), (cc, cc.n), (sd_d[0], sd_d[0].n), (sd_d[1], sd_d[1].n),
                      (sd_h1b, sd_h1b.n), (sd_wr, sd_wr.n)] + list(extra))

    dump_prevs = [None, None]

    def dump_f32(dst_dram, src_tiles, nt, wait):
        prevs = dump_prevs
        i = 0
        for f in range(nt):
            for half in range(2):
                par = i % 2
                i += 1
                t1 = b.emit("vector", lambda e, f=f, half=half, par=par: e.tensor_copy(
                    rep_s[:, par * 512: par * 512 + 512], src_tiles(f)[:, half * 512:half * 512 + 512]),
                    waits=[wait, prevs[par]], inc=sV)
                prevs[par] = b.emit("scalar", lambda e, f=f, half=half, par=par: e.dma_start(
                    out=dst_dram[f * 128:(f + 1) * 128, half * 512:half * 512 + 512],
                    in_=rep_s[:, par * 512: par * 512 + 512]), waits=[t1], inc=sd_d[par], amt=16)

    def zero_out():
        t9 = b.emit("vector", lambda e: e.memset(LNF(), 0.0),
                    waits=[(sd_out, sd_out.n), (sd_d[0], sd_d[0].n), (sd_d[1], sd_d[1].n)], inc=sV)
        for m_ in range(NF):
            for half in range(2):
                b.emit("scalar", lambda e, m_=m_, half=half: e.dma_start(
                    out=outT[m_ * 128:(m_ + 1) * 128, half * 512:half * 512 + 512],
                    in_=LNF()), waits=[t9], inc=sd_out, amt=16)

    if stage == "h1":
        dump_f32(dbg["h1T_dbg"], lambda f: big_b[:, f * TL:(f + 1) * TL], NF, tk_h1)
        tbl = b.emit("sync", lambda e: e.dma_start(out=lo16f[0:16, :], in_=bet_loc[:, 0:512]),
                     waits=[tk_betst], inc=sd_out, amt=16)
        b.emit("sync", lambda e: e.dma_start(out=dbg["bet_dbg"][:, 0:512], in_=lo16f[0:16, :]),
               waits=[tbl], inc=sd_out, amt=16)
        tbl2 = b.emit("sync", lambda e: e.dma_start(out=lo16f[0:16, :], in_=bet_loc[:, 512:1024]),
                      waits=[(sd_out, sd_out.n)], inc=sd_out, amt=16)
        b.emit("sync", lambda e: e.dma_start(out=dbg["bet_dbg"][:, 512:1024], in_=lo16f[0:16, :]),
               waits=[tbl2], inc=sd_out, amt=16)
        b.emit("sync", lambda e: e.dma_start(out=dbg["mod_dbg"][:, :], in_=mvec[:]),
               waits=[tk_ga, tk_gb, tk_ca, tk_cb], inc=sd_out, amt=16)
        zero_out()
        emit_done([])
        return finish(nc, ctx, b), dbg

    WQKV = lambda f, proj, m: dwn_b[:, f * 768 + proj * 256 + m * 128: f * 768 + proj * 256 + (m + 1) * 128]
    tk_wqkv_all = None
    for proj, wsrc in [(0, wq), (1, wk), (2, wv)]:
        for m in range(2):
            for kh in range(2):
                tk_wqkv_all = stream_piece(
                    lambda wsrc=wsrc, m=m, kh=kh: wsrc.rearrange("(f p) c -> p f c", p=128)[
                        :, kh * 8:(kh + 1) * 8, m * 128:(m + 1) * 128],
                    lambda proj=proj, m=m, kh=kh: dwn_b[:, 0:12288].rearrange(
                        "p (f c) -> p f c", f=NF)[:, kh * 8:(kh + 1) * 8,
                                                  proj * 256 + m * 128: proj * 256 + (m + 1) * 128],
                    rows=128, ncols=1024, a=8)
    tk_wqkv_sync = stream_sync()

    tk_s0 = b.emit("vector", lambda e: e.memset(S_m[:], 0.0), inc=sV)
    tk_sb0 = b.emit("vector", lambda e: e.memset(S_b[:], 0.0), waits=[tk_s0], inc=sV)
    brow_guard = [tk_a2a_b]

    hin_tok = [None, None]
    hin_free = [None, None]
    prev_tbr = None
    qkv_free = [None, None]
    sb_tok = [tk_sb0, tk_sb0]
    sm_tok = [tk_s0, tk_s0]
    last_oT = None
    first_oT = True

    NSC_LIM = NSC
    for s in range(NSC_LIM):
        sp = s % 2
        jblk = s // 4
        col0 = 256 * (s % 4)
        hin_tok[sp] = b.emit("sync", lambda e, sp=sp, jblk=jblk, col0=col0: e.dma_start(
            out=big_c[:, sp * 4096: sp * 4096 + 4096].rearrange("p (f t) -> p f t", f=NF),
            in_=h1_all.rearrange("(j f p) t -> p j f t", p=128, f=NF)[:, jblk, :, col0:col0 + SC]),
            waits=[tk_ag2, hin_free[sp]], inc=sd_hin[sp], amt=16)
        tk_brow = None
        for hs in range(2):
            tk_brow = b.emit("sync", lambda e, hs=hs, jblk=jblk, col0=col0: e.dma_start(
                out=lo16f[0:1, hs * 256:(hs + 1) * 256],
                in_=bet_all[2 * jblk + hs:2 * jblk + hs + 1, col0:col0 + SC]),
                waits=brow_guard, inc=sd_bcol, amt=16)
        brow_consumers = []

        ev = []
        prev_trm_l2 = None
        for gi, (proj, m) in enumerate([(0, 0), (0, 1), (1, 0), (1, 1), (2, 0), (2, 1)]):
            grp = mm_group(PROJ, slice(0, 256),
                           [(WQKV(k, proj, m), big_c[:, sp * 4096 + k * 256: sp * 4096 + (k + 1) * 256])
                            for k in range(NF)],
                           [hin_tok[sp], tk_wqkv_sync[0], tk_wqkv_sync[1], qkv_free[sp]], inc=sT)
            qidx = proj * 2 + m
            if proj < 2:
                tsq = b.emit("scalar", lambda e: e.activation(big_d[:, SQ2], PROJ[:, 0:256], AF.Square),
                             waits=[grp], inc=sS)
                tss = mm_group(ROWS, (slice(0, 1), slice(0, 256)), [(c_ocb[:, 0:1], big_d[:, SQ2])],
                               [tsq], inc=sT)
                trb = b.emit("vector", lambda e: e.tensor_copy(rowb[0:1, 0:256], ROWS[0:1, 0:256]),
                             waits=[tss], inc=sV)
                bank_done(ROWS, trb)
                trm = mm_group(REPS, slice(0, 256), [(c_orb[0:1, :], rowb[0:1, 0:256])], [trb], inc=sT)
                prev_trm_l2 = trm
                tra = b.emit("scalar", lambda e: e.activation(L2A(), REPS[:, 0:256], AF.Sqrt),
                             waits=[trm], inc=sS)
                bank_done(REPS, tra)
                tmx = b.emit("vector", lambda e: e.tensor_scalar(L2B(), L2A(), C_TINY, None, ALU.max),
                             waits=[tra, tk_c], inc=sV)
                trr = b.emit("vector", lambda e: e.reciprocal(L2A(), L2B()), waits=[tmx], inc=sV)
                if proj == 0:
                    trr = b.emit("vector", lambda e: e.tensor_scalar(L2B(), L2A(), C_QKS, None, ALU.mult),
                                 waits=[trr], inc=sV)
                    fin = L2B
                else:
                    fin = L2A
                tev = b.emit("vector", lambda e, sp=sp, qidx=qidx, fin=fin: e.tensor_tensor(
                    big_d[:, QB(sp, qidx)], PROJ[:, 0:256], fin(), ALU.mult),
                    waits=[trr], inc=sV)
            else:
                tev = b.emit("scalar", lambda e, sp=sp, qidx=qidx: e.activation(
                    big_d[:, QB(sp, qidx)], PROJ[:, 0:256], AF.Silu), waits=[grp], inc=sS)
            ev.append(tev)
            bank_done(PROJ, tev)
        hin_free[sp] = ev[-1]

        for ci in range(CPS):
            c = s * CPS + ci
            tc = slice(ci * 128, (ci + 1) * 128)
            for hs in range(2):
                SCB, SCB2 = SCB_[hs]
                Qf = big_d[:, QB(sp, 0 + hs)][:, tc]
                Kf = big_d[:, QB(sp, 2 + hs)][:, tc]
                Vf = big_d[:, QB(sp, 4 + hs)][:, tc]
                brsl = lo16f[0:1, hs * 256 + ci * 128: hs * 256 + ci * 128 + 128]
                bcol = bcols[:, hs * 2 + sp: hs * 2 + sp + 1]

                tbc = mm_group(ROWS, (slice(0, 128), slice(448, 449)), [(brsl, c_orf[0:1, 0:1])],
                               [tk_brow], inc=sT)
                tbce = b.emit("vector", lambda e, hs=hs, sp=sp: e.tensor_copy(
                    bcols[:, hs * 2 + sp: hs * 2 + sp + 1], ROWS[:, 448:449]), waits=[tbc], inc=sV)
                bank_done(ROWS, tbce)
                tbrb = b.emit("vector", lambda e, brsl=brsl: e.tensor_copy(rowb[0:1, 384:512], brsl),
                              waits=[tk_brow, prev_tbr], inc=sV)
                brow_consumers.append(tbrb)
                tbr = mm_group(REPS, slice(0, 128), [(c_orb[0:1, :], rowb[0:1, 384:512])], [tbrb], inc=sT)
                tbm = b.emit("vector", lambda e, hs=hs: e.tensor_tensor(
                    rep_s[:, bm_sl(hs)], REPS[:, 0:128], c_msu[:], ALU.mult), waits=[tbr], inc=sV)
                bank_done(REPS, tbm)
                prev_tbr = tbr
                brow_consumers.append(tbc)

                tG = mm_group(SCB, slice(0, 128), [(Kf, Kf)], [ev[2 + hs]], inc=sT)
                tA = b.emit("vector", lambda e, hs=hs, bcol=bcol, SCB=SCB: e.scalar_tensor_tensor(
                    big_d[:, SCR(hs, SL_A)], SCB[:, 0:128], bcol, c_msl[:], ALU.mult, ALU.mult),
                    waits=[tG, tbce], inc=sV)
                tB = b.emit("vector", lambda e, hs=hs, SCB=SCB: e.tensor_tensor(
                    big_d[:, SCR(hs, SL_B)], SCB[:, 0:128], rep_s[:, bm_sl(hs)], ALU.mult),
                    waits=[tbm, tA], inc=sV)
                bank_done(SCB, tB)

                tKt = tr(ptb, slice(0, 128), Kf, c_idb[:], [ev[2 + hs]], inc=sT)
                tKtok = b.emit("scalar", lambda e, hs=hs: e.activation(big_d[:, SCR(hs, SL_KT)], ptb[:, 0:128], AF.Copy),
                               waits=[tKt], inc=sS)
                tKb = b.emit("vector", lambda e, hs=hs, bcol=bcol: e.tensor_scalar(
                    big_d[:, SCR(hs, SL_KB)], ptb[:, 0:128], bcol, None, ALU.mult),
                    waits=[tKt, tbce], inc=sV)
                bank_done(ptb, tKb)
                tVt = tr(ptb, slice(128, 256), Vf, c_idb[:], [ev[4 + hs], tKtok], inc=sT)
                tVb = b.emit("vector", lambda e, hs=hs, bcol=bcol: e.tensor_scalar(
                    big_d[:, SCR(hs, SL_VB)], ptb[:, 128:256], bcol, None, ALU.mult),
                    waits=[tVt, tbce], inc=sV)
                bank_done(ptb, tVb)

                tX = b.emit("vector", lambda e, hs=hs: e.tensor_sub(
                    big_d[:, SCR(hs, SL_MT)], c_idb[:], big_d[:, SCR(hs, SL_B)]),
                    waits=[tB], inc=sV)

                aslot = {1: SL_A, 2: SL_A2, 4: SL_A4, 8: SL_A8, 16: SL_A16, 32: SL_A32, 64: SL_A64}
                atok = {1: tA}
                prevB, tokB = SL_B, tB
                bslots = [SL_X, SL_B]
                bs_i = 0
                pw = 1
                while pw < 64:
                    npw = 2 * pw
                    t1 = mm_group(SCB, slice(0, 128),
                                  [(big_d[:, SCR(hs, prevB)], big_d[:, SCR(hs, aslot[pw])])],
                                  [atok[pw], tokB], inc=sT)
                    ta_n = b.emit("scalar", lambda e, hs=hs, d=aslot[npw], SCB=SCB: e.activation(
                        big_d[:, SCR(hs, d)], SCB[:, 0:128], AF.Copy), waits=[t1], inc=sS)
                    bank_done(SCB, ta_n)
                    atok[npw] = ta_n
                    if npw < 64:
                        t2 = mm_group(SCB2, slice(0, 128),
                                      [(big_d[:, SCR(hs, aslot[pw])], big_d[:, SCR(hs, prevB)])],
                                      [atok[pw], tokB, tX], inc=sT)
                        nbs = bslots[bs_i % 2]
                        bs_i += 1
                        tokB = b.emit("scalar", lambda e, hs=hs, nbs=nbs, SCB2=SCB2: e.activation(
                            big_d[:, SCR(hs, nbs)], SCB2[:, 0:128], AF.Copy), waits=[t2], inc=sS)
                        bank_done(SCB2, tokB)
                        prevB = nbs
                    pw = npw
                for jp in [2, 4, 8, 16, 32, 64]:
                    tj = mm_group(SCB, slice(0, 128),
                                  [(c_idb[:], big_d[:, SCR(hs, SL_MT)]),
                                   (big_d[:, SCR(hs, aslot[jp])], big_d[:, SCR(hs, SL_MT)])],
                                  [tX, atok[jp]], inc=sT)
                    tX = b.emit("scalar", lambda e, hs=hs, SCB=SCB: e.activation(
                        big_d[:, SCR(hs, SL_MT)], SCB[:, 0:128], AF.Copy), waits=[tj], inc=sS)
                    bank_done(SCB, tX)
                tMt = tX

                tw = mm_group(SCB2, slice(0, 128),
                              [(big_d[:, SCR(hs, SL_KB)], big_d[:, SCR(hs, SL_MT)])],
                              [tKb, tMt], inc=sT)
                tW0 = b.emit("scalar", lambda e, hs=hs, SCB2=SCB2: e.mul(big_d[:, SCR(hs, SL_W0)], SCB2[:, 0:128], -1.0),
                             waits=[tw], inc=sS)
                bank_done(SCB2, tW0)

                tt1 = mm_group(SCB, slice(0, 128), [(Kf, Qf)], [ev[0 + hs], ev[2 + hs]], inc=sT)
                tTt = b.emit("vector", lambda e, hs=hs, SCB=SCB: e.tensor_tensor(
                    big_d[:, SCR(hs, SL_TT)], SCB[:, 0:128], c_mul[:], ALU.mult), waits=[tt1], inc=sV)
                bank_done(SCB, tTt)

                tu = mm_group(SCB2, slice(0, 128),
                              [(big_d[:, SCR(hs, SL_MT)], big_d[:, SCR(hs, SL_VB)]),
                               (big_d[:, SCR(hs, SL_W0)], S_b[:, hs * DH:(hs + 1) * DH])],
                              [tMt, tVb, tW0, sb_tok[hs]], inc=sT)
                tU = b.emit("scalar", lambda e, hs=hs, SCB2=SCB2: e.activation(big_d[:, SCR(hs, SL_U)], SCB2[:, 0:128], AF.Copy),
                            waits=[tu], inc=sS)
                bank_done(SCB2, tU)

                to = mm_group(SCB, slice(0, 128),
                              [(S_b[:, hs * DH:(hs + 1) * DH], Qf),
                               (big_d[:, SCR(hs, SL_U)], big_d[:, SCR(hs, SL_TT)])],
                              [sb_tok[hs], ev[0 + hs], tU, tTt], inc=sT)
                tosq = b.emit("scalar", lambda e, SCB=SCB: e.activation(big_d[:, SQ1], SCB[:, 0:128], AF.Square),
                              waits=[to], inc=sS)
                tss2 = mm_group(ROWS, (slice(0, 1), slice(256, 384)), [(c_ocb[:, 0:1], big_d[:, SQ1])],
                                [tosq], inc=sT)
                tonb = b.emit("vector", lambda e: e.tensor_copy(rowb[0:1, 256:384], ROWS[0:1, 256:384]),
                              waits=[tss2], inc=sV)
                bank_done(ROWS, tonb)
                trep = mm_group(REPS, slice(128, 256), [(c_orb[0:1, :], rowb[0:1, 256:384])], [tonb], inc=sT)
                tia = b.emit("vector", lambda e: e.tensor_scalar(ON1(), REPS[:, 128:256],
                                                                 C_INVDH, C_EPS, ALU.mult, ALU.add),
                             waits=[trep, tk_c], inc=sV)
                bank_done(REPS, tia)
                tib = b.emit("scalar", lambda e: e.activation(ON2(), ON1(), AF.Sqrt), waits=[tia], inc=sS)
                trepe = b.emit("vector", lambda e: e.reciprocal(ON1(), ON2()), waits=[tib], inc=sV)
                w_o = [trepe]
                if first_oT:
                    w_o += [tk_h1st, tk_betmm[0], tk_betmm[1]]
                    first_oT = False
                tOe = b.emit("vector", lambda e, hs=hs, c=c, SCB=SCB: e.scalar_tensor_tensor(
                    big_b[:, oT_col(hs, c)], SCB[:, 0:128], c_onw[:, 0:1], ON1(),
                    ALU.mult, ALU.mult), waits=w_o, inc=sV)
                bank_done(SCB, tOe)
                last_oT = tOe

                tsu = mm_group(SCB2, slice(0, 128),
                               [(big_d[:, SCR(hs, SL_KT)], big_d[:, SCR(hs, SL_U)])],
                               [tKtok, tU], inc=sT)
                tsm = b.emit("vector", lambda e, hs=hs, SCB2=SCB2: e.tensor_add(
                    S_m[:, hs * DH:(hs + 1) * DH], SCB2[:, 0:128], S_m[:, hs * DH:(hs + 1) * DH]),
                    waits=[tsu, sm_tok[hs]], inc=sV)
                tsb = b.emit("scalar", lambda e, hs=hs: e.activation(
                    S_b[:, hs * DH:(hs + 1) * DH], S_m[:, hs * DH:(hs + 1) * DH], AF.Copy),
                    waits=[tsm], inc=sS)
                bank_done(SCB2, tsb)
                sm_tok[hs] = tsm
                sb_tok[hs] = tsb
            if SCAN_DBG and s == 0 and ci == 0:
                break
        qkv_free[sp] = last_oT
        brow_guard = brow_consumers[-2:]
        if SCAN_DBG and s == 0:
            break

    if stage == "scan" and SCAN_DBG:
        allw = [(sS, sS.n), (sV, sV.n), (sT, sT.n)]
        t1 = b.emit("sync", lambda e: e.dma_start(out=dbg["scr_dbg"][:, 0:1536], in_=big_d[:, 0:1536]),
                    waits=allw, inc=sd_out, amt=16)
        t1 = b.emit("sync", lambda e: e.dma_start(out=dbg["scr_dbg"][:, 4608:9088], in_=big_d[:, 4608:9088]),
                    waits=allw, inc=sd_out, amt=16)
        t2 = b.emit("sync", lambda e: e.dma_start(out=dbg["sm_dbg"][:, :], in_=S_m[:]),
                    waits=allw, inc=sd_d[0], amt=16)
        t3 = b.emit("sync", lambda e: e.dma_start(out=dbg["bc_dbg"][:, 0:1], in_=bcols[:, 0:1], allow_slow_non_contiguous=True),
                    waits=allw, inc=sd_d[1], amt=16)
        t3 = b.emit("sync", lambda e: e.dma_start(out=dbg["bc_dbg"][:, 2:3], in_=bcols[:, 2:3], allow_slow_non_contiguous=True),
                    waits=allw, inc=sd_d[1], amt=16)
        prev_dd = None
        for hs in range(2):
            for blk in range(1):
                t9 = b.emit("vector", lambda e, hs=hs, blk=blk: e.tensor_copy(
                    LNF(), big_b[:, hs * 8192 + blk * 512: hs * 8192 + (blk + 1) * 512]),
                    waits=[last_oT, prev_dd, t1], inc=sV)
                prev_dd = b.emit("scalar", lambda e, hs=hs, blk=blk: e.dma_start(
                    out=dbg["o_dbg"][hs * 128:(hs + 1) * 128, blk * 512:(blk + 1) * 512],
                    in_=LNF()), waits=[t9], inc=sd_out, amt=16)
        zero_out()
        emit_done([])
        return finish(nc, ctx, b), dbg

    tk_ost = None
    for hs in range(2):
        tk_ost = b.emit("scalar", lambda e, hs=hs: e.dma_start(
            out=o_loc.rearrange("(r q p) t -> q p r t", p=128, q=2)[hs],
            in_=big_b[:, hs * 8192:(hs + 1) * 8192].rearrange("p (r t) -> p r t", r=8)),
            waits=[last_oT], inc=sd_o, amt=16)
    tk_a2a_o = b.emit("gpsimd", lambda e: e.collective_compute(
        "AllToAll", ALU.bypass, replica_groups=[list(range(NC))],
        ins=[o_loc], outs=[o_all]), waits=[tk_ost], inc=cc)

    if stage == "scan":
        prev_dd = None
        for hs in range(2):
            for blk in range(16):
                t1 = b.emit("vector", lambda e, hs=hs, blk=blk: e.tensor_copy(
                    LNF(), big_b[:, hs * 8192 + blk * 512: hs * 8192 + (blk + 1) * 512]),
                    waits=[last_oT, prev_dd], inc=sV)
                prev_dd = b.emit("scalar", lambda e, hs=hs, blk=blk: e.dma_start(
                    out=dbg["o_dbg"][hs * 128:(hs + 1) * 128, blk * 512:(blk + 1) * 512],
                    in_=LNF()), waits=[t1], inc=sd_out, amt=16)
        zero_out()
        emit_done([])
        return finish(nc, ctx, b), dbg

    tk_oall = b.emit("sync", lambda e: e.dma_start(
        out=big_b[:].rearrange("p (f t) -> p f t", f=NF),
        in_=o_all.rearrange("(f p) t -> p f t", p=128)),
        waits=[tk_a2a_o, last_oT], inc=sd_oall, amt=16)

    wop_free = [None, None]
    x2_toks = [None] * NF
    for m in range(NF):
        slm = m % 2
        tcv = None
        for kh in range(2):
            tcv = stream_piece(
                lambda m=m, kh=kh: wo.rearrange("(f p) c -> p f c", p=128)[
                    :, kh * 8:(kh + 1) * 8, m * 128:(m + 1) * 128],
                lambda slm=slm, kh=kh: dwn_b[:, 12288 + slm * 2048: 12288 + (slm + 1) * 2048].rearrange(
                    "p (f c) -> p f c", f=NF)[:, kh * 8:(kh + 1) * 8, :],
                extra_cv_waits=[wop_free[slm]], rows=128, ncols=1024, a=8)
        tcv_sync = stream_sync()
        tev = None
        for half in range(2):
            grp = mm_group(PROJ, slice(0, 512),
                           [(dwn_b[:, 12288 + slm * 2048 + k * 128: 12288 + slm * 2048 + (k + 1) * 128],
                             big_b[:, k * TL + half * 512: k * TL + half * 512 + 512]) for k in range(NF)],
                           tcv_sync + [tk_oall], inc=sT)
            tev = b.emit("vector", lambda e, m=m, half=half: e.scalar_tensor_tensor(
                big_a[:, m * TL + half * 512: m * TL + half * 512 + 512], PROJ[:, 0:512],
                mvec[:, 32 + m:33 + m], big_a[:, m * TL + half * 512: m * TL + half * 512 + 512],
                ALU.mult, ALU.add), waits=[grp, tk_ga], inc=sV)
            bank_done(PROJ, tev)
        x2_toks[m] = tev
        wop_free[slm] = tev

    tk_x2 = x2_toks[-1]
    tk_h2 = emit_ln(
        src=lambda f: big_a[:, f * TL:(f + 1) * TL],
        dst=lambda f: big_b[:, f * TL:(f + 1) * TL],
        cmod=lambda f: mvec[:, 112 + f:113 + f],
        smod=lambda f: mvec[:, 48 + f:49 + f],
        wait_stats=lambda f: [x2_toks[f], tk_x2],
        wait_apply=[tk_cb],
        rowbanks=(pb[3], pb[4], pb[5], pb[6]),
    )

    tk_gw = stream_piece(
        lambda: gwT.rearrange("(f p) c -> p f c", p=128),
        lambda: gw_b[:, :], rows=128, ncols=NF * E, a=NF)
    tk_gac1 = stream_piece(lambda: gA[:, 0:1024], lambda: lo16[0:16, 0:1024], rows=16, ncols=1024)
    tk_gac = stream_piece(lambda: gA[:, 1024:2048], lambda: lo16[0:16, 1024:2048], rows=16, ncols=1024)
    tk_gbc = stream_piece(lambda: gBT[:, :], lambda: lo16[0:16, 3072:3080], rows=16, ncols=8)
    tk_weff = None
    tk_router_sync = stream_sync()
    for f in range(NF):
        t1 = mm_group(REPS, slice(0, 8), [(lo16[0:16, f * 128:(f + 1) * 128], lo16[0:16, 3072:3080])],
                      tk_router_sync, inc=sT)
        tk_weff = b.emit("vector", lambda e, f=f: e.scalar_tensor_tensor(
            weff[:, f * 8:(f + 1) * 8], REPS[:, 0:8], C_TWO, gw_b[:, f * 8:(f + 1) * 8], ALU.mult, ALU.add),
            waits=[t1, tk_gw], inc=sV)
        bank_done(REPS, tk_weff)
    lrow = lo16f
    tk_w = None
    RT = lambda a, b2: rep_s[:, a:b2]
    prev_router = [tk_betst]
    rt_guard = []
    neg_guard = []
    for half in range(2):
        grp = mm_group(PROJ, (slice(0, 8), slice(0, 512)),
                       [(weff[:, f * 8:(f + 1) * 8], big_b[:, f * TL + half * 512: f * TL + half * 512 + 512])
                        for f in range(NF)], [tk_weff, tk_h2], inc=sT)
        tk_lge = b.emit("vector", lambda e: e.tensor_copy(
            lrow[0:8, 0:512], PROJ[0:8, 0:512]), waits=[grp] + prev_router, inc=sV)
        bank_done(PROJ, tk_lge)
        for tt_ in range(4):
            tile = half * 4 + tt_
            tt = tr(pb[3], slice(0, 8), lrow[0:8, tt_ * 128:(tt_ + 1) * 128], c_idf[0:8, 0:8],
                    [tk_lge], inc=sT)
            tl = b.emit("vector", lambda e, tile=tile: e.tensor_copy(wtok[:, tile * 8:(tile + 1) * 8], pb[3][:, 0:8]),
                        waits=[tt] + rt_guard, inc=sV)
            bank_done(pb[3], tl)
            Lap = wtok[:, tile * 8:(tile + 1) * 8]
            r1 = b.emit("vector", lambda e, Lap=Lap: e.reduce_max(RT(384, 385), Lap, axis=AX.X),
                        waits=[tl], inc=sV)
            r2 = b.emit("vector", lambda e, Lap=Lap: e.tensor_scalar(RT(400, 408), Lap, RT(384, 385), None, ALU.is_ge),
                        waits=[r1], inc=sV)
            r3 = b.emit("vector", lambda e, Lap=Lap: e.scalar_tensor_tensor(RT(424, 432), RT(400, 408), C_NEG30, Lap, ALU.mult, ALU.add),
                        waits=[r2], inc=sV)
            r4 = b.emit("vector", lambda e: e.reduce_max(RT(385, 386), RT(424, 432), axis=AX.X),
                        waits=[r3], inc=sV)
            tnm = b.emit("vector", lambda e: e.tensor_scalar(RT(386, 387), RT(384, 385), C_NEG1, None, ALU.mult),
                         waits=[r1] + neg_guard, inc=sV)
            tp = b.emit("scalar", lambda e, Lap=Lap: e.activation(RT(408, 416), Lap, AF.Exp, bias=RT(386, 387)),
                        waits=[tnm, tl], inc=sS)
            te2 = b.emit("scalar", lambda e: e.activation(RT(387, 388), RT(385, 386), AF.Exp, bias=RT(386, 387)),
                         waits=[tnm, r4], inc=sS)
            r5 = b.emit("vector", lambda e: e.tensor_scalar(RT(388, 389), RT(387, 388), C_ONE, None, ALU.add),
                        waits=[te2], inc=sV)
            r6 = b.emit("vector", lambda e: e.reciprocal(RT(387, 388), RT(388, 389)), waits=[r5], inc=sV)
            r7 = b.emit("vector", lambda e, Lap=Lap: e.tensor_scalar(RT(416, 424), Lap, RT(385, 386), None, ALU.is_ge),
                        waits=[r4, tl], inc=sV)
            r8 = b.emit("vector", lambda e: e.tensor_tensor(RT(400, 408), RT(408, 416), RT(416, 424), ALU.mult),
                        waits=[tp, r7, r3], inc=sV)
            tw_ = b.emit("vector", lambda e, tile=tile: e.tensor_scalar(
                wtok[:, tile * 8:(tile + 1) * 8], RT(400, 408), RT(387, 388), None, ALU.mult),
                waits=[r8, r6], inc=sV)
            tt2 = tr(pb[3], (slice(0, 8), slice(0, 128)), wtok[:, tile * 8:(tile + 1) * 8], c_idf[:], [tw_], inc=sT)
            tk_w = b.emit("vector", lambda e, tile=tile: e.tensor_copy(
                wrowb[0:8, tile * 128:(tile + 1) * 128], pb[3][0:8, 0:128]), waits=[tt2], inc=sV)
            bank_done(pb[3], tk_w)
            rt_guard = [tw_]
            neg_guard = [tp, te2]
        prev_router = [tk_w]
    tk_wrb = tk_w

    tk_gal = None
    for kh in range(2):
        tk_gal = stream_piece(
            lambda kh=kh: guAT_all.rearrange("(f p) c -> p f c", p=128)[:, kh * 8:(kh + 1) * 8, :],
            lambda kh=kh: big_d[:, 0:2048].rearrange("p (f c) -> p f c", f=NF)[:, kh * 8:(kh + 1) * 8, :],
            extra_cv_waits=[last_oT], rows=128, ncols=1024, a=8)
    tk_gal_sync = stream_sync()
    tk_tal = None
    for half in range(2):
        grp = mm_group(PROJ, slice(0, 512),
                       [(big_d[:, f * 128:(f + 1) * 128], big_b[:, f * TL + half * 512: f * TL + half * 512 + 512])
                        for f in range(NF)], tk_gal_sync + [tk_h2], inc=sT)
        tk_tal = b.emit("scalar", lambda e, half=half: e.mul(
            big_d[:, 2048 + half * 512: 2048 + half * 512 + 512], PROJ[:, 0:512], 2.0), waits=[grp], inc=sS)
        bank_done(PROJ, tk_tal)
    tk_talst = b.emit("scalar", lambda e: e.dma_start(
        out=tal_d.rearrange("(g p) t -> p g t", p=128),
        in_=big_d[:, 2048:3072].rearrange("p (g t) -> p g t", g=1)), waits=[tk_tal], inc=sd_tal, amt=16)

    if stage == "attn":
        dump_f32(dbg["h2_dbg"], lambda f: big_b[:, f * TL:(f + 1) * TL], NF, tk_wrb)
        dump_f32(dbg["x2_dbg"], lambda f: big_a[:, f * TL:(f + 1) * TL], NF, tk_wrb)
        for half in range(2):
            tww = b.emit("vector", lambda e, half=half: e.tensor_copy(
                lo16f[0:8, 0:512], wrowb[0:8, half * 512: half * 512 + 512]),
                waits=[tk_w, (sd_out, sd_out.n)], inc=sV)
            b.emit("sync", lambda e, half=half: e.dma_start(
                out=dbg["w_dbg"][:, half * 512: half * 512 + 512], in_=lo16f[0:8, 0:512]),
                waits=[tww], inc=sd_out, amt=16)
        zero_out()
        emit_done([(sd_tal, sd_tal.n)])
        return finish(nc, ctx, b), dbg

    GUPW = lambda k, q2: big_d[:, k * 512 + q2: k * 512 + q2 + 128]
    DNW = lambda k, mcol: dwn_b[:, k * 2048 + mcol: k * 2048 + mcol + 128]
    HC = lambda k, half: big_c[:, k * 1024 + half * 512: k * 1024 + half * 512 + 512]

    gup_guard = [tk_tal, (sd_tal, sd_tal.n)]
    wrow_guard = [tk_wrb, tk_weff]
    last_wrep = None
    dwn_guard = [tk_x2]
    hc_guard = [tk_h2, tk_lge, tk_tal]
    Te_guard = [(sd_tal, sd_tal.n)]
    guB_guard = [tk_weff]
    dnB_guard = [tk_weff, tk_gbc]
    last_out = None
    gup_mm_last = None
    last_lora = None
    for e_i in range(E):
        tk_Te = b.emit("sync", lambda e, e_i=e_i: e.dma_start(
            out=lo16[0:16, 2048:3072].rearrange("r (a t) -> r a t", a=1),
            in_=tal_d.rearrange("(e2 r) t -> r e2 t", r=R)[:, e_i:e_i + 1, :]),
            waits=Te_guard, inc=sd_te, amt=16)
        guB_t = []
        for kh in range(2):
            guB_t.append(stream_piece(
                lambda e_i=e_i, kh=kh: guBT[e_i][:, kh * 1024:(kh + 1) * 1024],
                lambda kh=kh: lo16[0:16, kh * 1024:(kh + 1) * 1024],
                extra_cv_waits=guB_guard, rows=16, ncols=1024))
        tk_guB2 = guB_t
        dnB_t = []
        for kh in range(2):
            dnB_t.append(stream_piece(
                lambda e_i=e_i, kh=kh: dnBT[e_i][:, kh * 1024:(kh + 1) * 1024],
                lambda kh=kh: lo16[32:48, kh * 1024:(kh + 1) * 1024],
                extra_cv_waits=dnB_guard, rows=16, ncols=1024))
        tk_dnB2 = dnB_t
        tk_wrl = b.emit("scalar", lambda e, e_i=e_i: e.dma_start(
            out=lo16[0:1, 3072:4096], in_=wrowb[e_i:e_i + 1, :]),
            waits=wrow_guard, inc=sd_wr, amt=16)
        tk_dnA = stream_piece(
            lambda e_i=e_i: dnAT[e_i].rearrange("(k p) r -> p k r", p=128),
            lambda: dnA_s[:, :],
            extra_cv_waits=guB_guard, rows=128, ncols=128, a=8)

        dn_cv = None
        for piece in range(16):
            dn_cv = stream_piece(
                lambda e_i=e_i, piece=piece: dnw[e_i].rearrange("(k p) c -> p k c", p=128)[
                    :, :, piece * 128:(piece + 1) * 128],
                lambda piece=piece: dwn_b[:].rearrange("p (k c) -> p k c", k=8)[
                    :, :, piece * 128:(piece + 1) * 128],
                extra_cv_waits=(dwn_guard if (e_i == 0 and piece < 2) else
                                ([last_lora] if (piece < 2 and last_lora) else [])),
                rows=128, ncols=1024, a=8)
        tk_dwn_sync = stream_sync()

        for qb in range(4):
            cv = None
            for pi_ in range(4):
                j = pi_ % 2
                if pi_ < 2:
                    src_col = 256 * qb + 128 * j
                    q2 = 128 * j
                else:
                    src_col = 1024 + 256 * qb + 128 * j
                    q2 = 256 + 128 * j
                for kh in range(2):
                    exw = []
                    if qb == 0 and pi_ == 0:
                        exw = (gup_guard if e_i == 0 else []) + ([gup_mm_last] if gup_mm_last else [])
                    if qb > 0 and pi_ == 0:
                        exw = [gup_mm_last]
                    cv = stream_piece(
                        lambda e_i=e_i, src_col=src_col, kh=kh: gupw[e_i].rearrange(
                            "(f p) c -> p f c", p=128)[:, kh * 8:(kh + 1) * 8, src_col:src_col + 128],
                        lambda q2=q2, kh=kh: big_d[:, 0:8192].rearrange("p (k c) -> p k c", k=NF)[
                            :, kh * 8:(kh + 1) * 8, q2:q2 + 128],
                        extra_cv_waits=exw, rows=128, ncols=1024, a=8)
            tk_gup_sync = stream_sync()

            for half in range(2):
                hwsl = slice(half * 512, half * 512 + 512)
                trm = mm_group(REPS, slice(0, 512), [(c_orb[0:1, :], lo16[0:1, 3072 + half * 512:3072 + half * 512 + 512])],
                               [tk_wrl], inc=sT)
                last_wrep = trm
                glast = [gup_mm_last] if gup_mm_last else []
                trw = b.emit("vector", lambda e: e.tensor_copy(rep_s[:, 0:512], REPS[:, 0:512]),
                             waits=[trm] + glast, inc=sV)
                bank_done(REPS, trw)
                for mpl in range(2):
                    kt_h = qb * 2 + mpl
                    glast = [gup_mm_last] if gup_mm_last else []
                    gg = mm_group(PROJ, slice(0, 512),
                                  [(GUPW(k, mpl * 128), big_b[:, k * TL + half * 512: k * TL + half * 512 + 512])
                                   for k in range(NF)] +
                                  [(lo16[0:16, (256 * qb + mpl * 128):(256 * qb + (mpl + 1) * 128)],
                                    lo16[0:16, 2048 + half * 512: 2048 + half * 512 + 512])],
                                  tk_gup_sync + [tk_h2] + tk_guB2 + [tk_Te], inc=sT)
                    tsg = b.emit("scalar", lambda e: e.activation(rep1[:, :], PROJ[:, 0:512], AF.Silu),
                                 waits=[gg] + glast, inc=sS)
                    bank_done(PROJ, tsg)
                    gh = mm_group(pb[4], slice(0, 512),
                                  [(GUPW(k, 256 + mpl * 128), big_b[:, k * TL + half * 512: k * TL + half * 512 + 512])
                                   for k in range(NF)] +
                                  [(lo16[0:16, (1024 + 256 * qb + mpl * 128):(1024 + 256 * qb + (mpl + 1) * 128)],
                                    lo16[0:16, 2048 + half * 512: 2048 + half * 512 + 512])],
                                  tk_gup_sync + [tk_h2] + tk_guB2 + [tk_Te], inc=sT)
                    thw = b.emit("vector", lambda e: e.tensor_tensor(rep2[:, :], pb[4][:, 0:512],
                                                                     rep_s[:, 0:512], ALU.mult),
                                 waits=[gh, trw] + glast, inc=sV)
                    bank_done(pb[4], thw)
                    w10 = [thw, tsg] + (hc_guard if (e_i == 0 and qb == 0) else [])
                    gup_mm_last = b.emit("vector", lambda e, kt_h=kt_h, half=half: e.tensor_tensor(
                        HC(kt_h, half), rep1[:, :], rep2[:, :], ALU.mult), waits=w10, inc=sV)
        tk_hcap = gup_mm_last

        dn_last = None
        for half in range(2):
            grp = mm_group(ROWS, (slice(0, 16), slice(0, 512)),
                           [(dnA_s[:, k * 16:(k + 1) * 16], HC(k, half)) for k in range(8)],
                           [tk_dnA, tk_hcap], inc=sT)
            tk_T2 = b.emit("scalar", lambda e: e.mul(lo16[32:48, 2048:2560], ROWS[0:16, 0:512], 2.0),
                           waits=[grp], inc=sS)
            bank_done(ROWS, tk_T2)
            for m_ in range(NF):
                DB = pb[5] if m_ % 2 == 0 else pb[6]
                grp = mm_group(DB, slice(0, 512),
                               [(DNW(k, m_ * 128), HC(k, half)) for k in range(8)] +
                               [(lo16[32:48, m_ * 128:(m_ + 1) * 128], lo16[32:48, 2048:2560])],
                               tk_dwn_sync + [tk_hcap, tk_T2] + tk_dnB2, inc=sT)
                last_lora = grp
                tev = b.emit("vector", lambda e, m_=m_, half=half, DB=DB: e.scalar_tensor_tensor(
                    big_a[:, m_ * TL + half * 512: m_ * TL + half * 512 + 512], DB[:, 0:512],
                    mvec[:, 80 + m_:81 + m_], big_a[:, m_ * TL + half * 512: m_ * TL + half * 512 + 512],
                    ALU.mult, ALU.add), waits=[grp, tk_gb], inc=sV)
                bank_done(DB, tev)
                dn_last = tev
                if e_i == E - 1:
                    last_out = b.emit("scalar", lambda e, m_=m_, half=half: e.dma_start(
                        out=outT[m_ * 128:(m_ + 1) * 128, half * 512: half * 512 + 512],
                        in_=big_a[:, m_ * TL + half * 512: m_ * TL + half * 512 + 512]),
                        waits=[tev], inc=sd_out, amt=16)
        guB_guard = [gup_mm_last]
        dnB_guard = [last_lora]
        Te_guard = [gup_mm_last]
        wrow_guard = [last_wrep]

    emit_done([])
    return finish(nc, ctx, b), dbg


def finish(nc, ctx, b):
    with ctx:
        with nc.Block() as block:
            @block.sync
            def _(eng):
                b.run("sync", eng)

            @block.scalar
            def _(eng):
                b.run("scalar", eng)

            @block.vector
            def _(eng):
                b.run("vector", eng)

            @block.tensor
            def _(eng):
                b.run("tensor", eng)

            @block.gpsimd
            def _(eng):
                b.run("gpsimd", eng)
    return nc


def make_consts():
    ii = np.arange(128)
    ident = np.eye(128, dtype=np.float32)
    cst = np.zeros((128, 16), np.float32)
    for k, v in enumerate([0.0, 2.0, -1.0, 0.25, 1.0, 1e-5, 1e-12, QKS,
                           1.0 / D, 1.0 / DH, -1e30]):
        cst[:, k] = v
    return {
        "cstc": cst,
        "identf": ident,
        "identb": ident.astype(ml_dtypes.bfloat16),
        "ones_cb": np.ones((128, 1), ml_dtypes.bfloat16),
        "ones_rb": np.ones((1, 128), ml_dtypes.bfloat16),
        "ones_rf": np.ones((1, 128), np.float32),
        "mask_sl": (ii[None, :] < ii[:, None]).astype(ml_dtypes.bfloat16),
        "mask_su": (ii[None, :] > ii[:, None]).astype(ml_dtypes.bfloat16),
        "mask_ul": (ii[None, :] >= ii[:, None]).astype(ml_dtypes.bfloat16),
    }


def make_inmaps(inputs):
    f32 = np.float32
    x = np.ascontiguousarray(np.asarray(inputs["L0_attn_hidden_states"], f32)[0])
    c = np.asarray(inputs["L0_c"], f32)
    wmod_h = np.asarray(inputs["L0_wmod"], f32)
    Wq = np.asarray(inputs["L0_attn_Wq"], f32)
    Wk = np.asarray(inputs["L0_attn_Wk"], f32)
    Wv = np.asarray(inputs["L0_attn_Wv"], f32)
    Wb = np.ascontiguousarray(np.asarray(inputs["L0_attn_Wb"], f32))
    onw_h = np.ascontiguousarray(np.asarray(inputs["L0_attn_o_norm_weight"], f32).reshape(128, 1))
    Wo = np.ascontiguousarray(np.asarray(inputs["L0_attn_Wo"], f32))
    gw = np.asarray(inputs["L0_ffn_gate_weight"], f32)
    gA_h = np.ascontiguousarray(np.asarray(inputs["L0_ffn_gate_lora_A"], f32))
    gB_h = np.asarray(inputs["L0_ffn_gate_lora_B"], f32)
    gup_h = np.ascontiguousarray(np.asarray(inputs["L0_ffn_gate_up_proj"], f32))
    dwn_h = np.ascontiguousarray(np.asarray(inputs["L0_ffn_down_proj"], f32))
    guA_h = np.asarray(inputs["L0_ffn_gup_lora_A"], f32)
    guB_h = np.asarray(inputs["L0_ffn_gup_lora_B"], f32)
    dnA_h = np.asarray(inputs["L0_ffn_down_lora_A"], f32)
    dnB_h = np.asarray(inputs["L0_ffn_down_lora_B"], f32)
    consts = make_consts()
    guAT_all_h = np.ascontiguousarray(np.concatenate([guA_h[e] for e in range(E)], axis=0).T)
    shared = {
        "cT": np.ascontiguousarray(c.reshape(16, 128).T),
        "wb": Wb,
        "onw": onw_h,
        "wo": Wo,
        "gwT": np.ascontiguousarray(gw.T),
        "gA": gA_h,
        "gBT": np.ascontiguousarray(gB_h.T),
        "gupw": gup_h,
        "dnw": dwn_h,
        "guAT_all": guAT_all_h,
        "guBT": np.ascontiguousarray(guB_h.transpose(0, 2, 1)),
        "dnAT": np.ascontiguousarray(dnA_h.transpose(0, 2, 1)),
        "dnBT": np.ascontiguousarray(dnB_h.transpose(0, 2, 1)),
    }
    shared.update(consts)
    in_maps = []
    for i in range(NC):
        m = dict(shared)
        m["xT"] = np.ascontiguousarray(x[i * TL:(i + 1) * TL, :].T)
        m["wmod"] = np.ascontiguousarray(wmod_h[:, i * (6 * D // NC):(i + 1) * (6 * D // NC)])
        m["wq"] = np.ascontiguousarray(Wq[:, i * 256:(i + 1) * 256])
        m["wk"] = np.ascontiguousarray(Wk[:, i * 256:(i + 1) * 256])
        m["wv"] = np.ascontiguousarray(Wv[:, i * 256:(i + 1) * 256])
        in_maps.append(m)
    return in_maps


_NC_CACHE = {}


def run_stage(inputs, stage):
    if stage not in _NC_CACHE:
        _NC_CACHE[stage] = build_nc(stage)
    nc, dbg = _NC_CACHE[stage]
    in_maps = make_inmaps(inputs)
    return run_bass_kernel_spmd(nc, in_maps, core_ids=list(range(NC)))


def kernel(**inputs):
    res = run_stage(inputs, "full")
    out = np.empty((1, T, D), np.float32)
    for i in range(NC):
        out[0, i * TL:(i + 1) * TL, :] = res.results[i]["outT"].T
    return out


# version 02
# speedup vs baseline: 1.0205x; 1.0205x over previous
import os

os.environ.setdefault("TRN_ALLOW_ICI_ALL_TO_ALL", "1")

import numpy as np
import ml_dtypes
from contextlib import ExitStack

import concourse.bass as bass
import concourse.mybir as mybir
from concourse.bass_utils import run_bass_kernel_spmd

F32 = mybir.dt.float32
BF16 = mybir.dt.bfloat16
AF = mybir.ActivationFunctionType
ALU = mybir.AluOpType
AX = mybir.AxisListType

NC = 8
T = 8192
TL = T // NC
D = 2048
NF = 16
H = 16
DH = 128
E = 8
II = 1024
R = 16
CH = 128
SC = 256
NSC = T // SC
CPS = SC // CH
QKS = 0.08838834764831845

ENGINES = ("sync", "scalar", "vector", "tensor", "gpsimd")


class Sem:
    __slots__ = ("h", "n", "name")

    def __init__(self, h, name):
        self.h = h
        self.n = 0
        self.name = name


class Builder:
    def __init__(self, nc, ctx):
        self.nc = nc
        self.ctx = ctx
        self.q = {e: [] for e in ENGINES}

    def sem(self, name):
        return Sem(self.ctx.enter_context(self.nc.semaphore(name)), name)

    def emit(self, eng, fn, waits=(), inc=None, amt=1):
        best = {}
        for t in waits:
            if t is None:
                continue
            s, v = t
            assert v <= s.n, f"wait {s.name}>={v} but only {s.n} emitted"
            if best.get(s.name, (None, -1))[1] < v:
                best[s.name] = (s, v)
        tok = None
        if inc is not None:
            inc.n += amt
            tok = (inc, inc.n)
        self.q[eng].append((fn, list(best.values()), (inc, amt) if inc else None))
        return tok

    def run(self, eng_name, eng):
        n = 0
        for fn, waits, inc in self.q[eng_name]:
            for s, v in waits:
                eng.wait_ge(s.h, v)
            inst = fn(eng)
            if inc is not None:
                s, amt = inc
                inst.then_inc(s.h, amt)
            n += 1
        if n == 0:
            eng.nop()


def build_nc(stage="full"):
    nc = bass.Bass("TRN2", target_bir_lowering=False, debug=False)
    ctx = ExitStack()
    b = Builder(nc, ctx)

    def din(name, shape, dt=F32):
        return nc.dram_tensor(name, shape, dt, kind="ExternalInput").ap()

    xT = din("xT", [D, TL])
    cT_d = din("cT", [128, 16])
    wmod = din("wmod", [D, 6 * D // NC])
    wq = din("wq", [D, 2 * DH])
    wk = din("wk", [D, 2 * DH])
    wv = din("wv", [D, 2 * DH])
    wb = din("wb", [D, H])
    onw = din("onw", [128, 1])
    wo = din("wo", [D, D])
    gwT = din("gwT", [D, E])
    gA = din("gA", [R, D])
    gBT = din("gBT", [R, E])
    gupw = din("gupw", [E, D, 2 * II])
    dnw = din("dnw", [E, II, D])
    guAT_all = din("guAT_all", [D, E * R])
    guBT = din("guBT", [E, R, 2 * II])
    dnAT = din("dnAT", [E, II, R])
    dnBT = din("dnBT", [E, R, D])
    identf = din("identf", [128, 128])
    identb = din("identb", [128, 128], BF16)
    ones_cb = din("ones_cb", [128, 1], BF16)
    ones_rb = din("ones_rb", [1, 128], BF16)
    ones_rf = din("ones_rf", [1, 128])
    mask_sl = din("mask_sl", [128, 128], BF16)
    mask_su = din("mask_su", [128, 128], BF16)
    mask_ul = din("mask_ul", [128, 128], BF16)
    cstc = din("cstc", [128, 16])

    outT = nc.dram_tensor("outT", [D, TL], F32, kind="ExternalOutput").ap()
    dbg = {}
    if stage == "h1":
        dbg["h1T_dbg"] = nc.dram_tensor("h1T_dbg", [D, TL], F32, kind="ExternalOutput").ap()
        dbg["bet_dbg"] = nc.dram_tensor("bet_dbg", [H, TL], F32, kind="ExternalOutput").ap()
        dbg["mod_dbg"] = nc.dram_tensor("mod_dbg", [128, 128], F32, kind="ExternalOutput").ap()

    SCAN_DBG = False
    if stage == "scan":
        dbg["o_dbg"] = nc.dram_tensor("o_dbg", [2 * DH, T], F32, kind="ExternalOutput").ap()
        if SCAN_DBG:
            dbg["scr_dbg"] = nc.dram_tensor("scr_dbg", [128, 9088], BF16, kind="ExternalOutput").ap()
            dbg["sm_dbg"] = nc.dram_tensor("sm_dbg", [128, 2 * DH], F32, kind="ExternalOutput").ap()
            dbg["bc_dbg"] = nc.dram_tensor("bc_dbg", [128, 4], F32, kind="ExternalOutput").ap()
    if stage == "attn":
        dbg["x2_dbg"] = nc.dram_tensor("x2_dbg", [D, TL], F32, kind="ExternalOutput").ap()
        dbg["h2_dbg"] = nc.dram_tensor("h2_dbg", [D, TL], F32, kind="ExternalOutput").ap()
        dbg["w_dbg"] = nc.dram_tensor("w_dbg", [E, TL], F32, kind="ExternalOutput").ap()

    mod_loc = nc.dram_tensor("mod_loc", [1, 6 * D // NC], F32).ap()
    mod_all = nc.dram_tensor("mod_all", [NC, 6 * D // NC], F32).ap()
    h1_loc = nc.dram_tensor("h1_loc", [D, TL], BF16).ap()
    h1_all = nc.dram_tensor("h1_all", [NC * D, TL], BF16).ap()
    bet_loc = nc.dram_tensor("bet_loc", [H, TL], F32).ap()
    bet_all = nc.dram_tensor("bet_all", [H, TL], F32).ap()
    o_loc = nc.dram_tensor("o_loc", [2 * DH * NC, TL], BF16).ap()
    o_all = nc.dram_tensor("o_all", [2 * DH * NC, TL], BF16).ap()
    tal_d = nc.dram_tensor("tal_d", [E * R, TL], BF16).ap()

    def sb(name, shape, dt=F32):
        return ctx.enter_context(nc.sbuf_tensor(name, shape, dt))

    big_a = sb("big_a", [128, NF * TL], F32)
    big_b = sb("big_b", [128, NF * TL], BF16)
    big_c = sb("big_c", [128, 8192], BF16)
    big_d = sb("big_d", [128, 9088], BF16)
    dwn_b = sb("dwn_b", [128, 8 * D], BF16)
    stage_f = sb("stage_f", [128, 2048], F32)
    rep1 = sb("rep1", [128, 512], F32)
    rep2 = sb("rep2", [128, 512], F32)
    rep_s = sb("rep_s", [128, 1024], F32)
    rh = sb("rh", [128, 1536], F32)
    rowb = sb("rowb", [1, 1024], BF16)
    c_idf = sb("c_idf", [128, 128], F32)
    c_idb = sb("c_idb", [128, 128], BF16)
    c_ocb = sb("c_ocb", [128, 1], BF16)
    c_orb = sb("c_orb", [1, 128], BF16)
    c_orf = sb("c_orf", [1, 128], F32)
    c_msl = sb("c_msl", [128, 128], BF16)
    c_msu = sb("c_msu", [128, 128], BF16)
    c_mul = sb("c_mul", [128, 128], BF16)
    c_onw = sb("c_onw", [128, 1], F32)
    cT_s = sb("cT_s", [128, 16], F32)
    cS = sb("cS", [128, 16], F32)
    mvec = sb("mvec", [128, 128], F32)
    bcols = sb("bcols", [128, 4], F32)
    
    S_m = sb("S_m", [128, 2 * DH], F32)
    S_b = sb("S_b", [128, 2 * DH], BF16)
    lo16 = sb("lo16", [128, 4096], BF16)
    lo16f = sb("lo16f", [64, 512], F32)
    weff = sb("weff", [128, NF * E], BF16)
    gw_b = sb("gw_b", [128, NF * E], BF16)
    wtok = sb("wtok", [128, 8 * E], F32)
    wrowb = sb("wrowb", [E, TL], BF16)
    dnA_s = sb("dnA_s", [128, 8 * R], BF16)

    R_MOD = lambda: rh[96:97, 0:1536]
    S1 = lambda: rep_s[:, 0:512]
    S2 = lambda: rep_s[:, 512:1024]
    L2A = lambda: rep_s[:, 640:896]
    L2B = lambda: rep_s[:, 0:256]
    ON1 = lambda: rep_s[:, 256:384]
    ON2 = lambda: rep_s[:, 896:1024]

    LNB1 = lambda par: big_c[:, par * 2048: par * 2048 + 1024]
    LNB2 = lambda par: big_c[:, par * 2048 + 1024: par * 2048 + 2048]
    LNF = lambda: rep_s[:, 0:512]
    QB = lambda sp, idx: slice(sp * 2304 + idx * 256, sp * 2304 + (idx + 1) * 256)
    SCR = lambda hs, slot: slice(4608 + hs * 2048 + slot * 128, 4608 + hs * 2048 + (slot + 1) * 128)
    SQ2 = slice(8704, 8960)
    SQ1 = slice(8960, 9088)
    (SL_A, SL_A2, SL_A4, SL_A8, SL_A16, SL_A32, SL_A64, SL_B, SL_MT, SL_W0,
     SL_U, SL_TT, SL_KT, SL_KB, SL_VB, SL_X) = range(16)
    oT_col = lambda hs, c: slice(hs * 8192 + c * 128, hs * 8192 + (c + 1) * 128)
    bm_sl = lambda hs: slice(384 + hs * 128, 384 + (hs + 1) * 128)

    cst = sb("cst", [128, 16], F32)
    C_ZERO = cst[:, 0:1]
    C_TWO = cst[:, 1:2]
    C_NEG1 = cst[:, 2:3]
    C_QTR = cst[:, 3:4]
    C_ONE = cst[:, 4:5]
    C_EPS = cst[:, 5:6]
    C_TINY = cst[:, 6:7]
    C_QKS = cst[:, 7:8]
    C_INVD = cst[:, 8:9]
    C_INVDH = cst[:, 9:10]
    C_NEG30 = cst[:, 10:11]
    assert nc.sbuf_bytes_remaining >= 0

    def pst(name, shape, dt=F32):
        return ctx.enter_context(nc.psum_tensor(name, shape, dt))

    pb = [pst(f"pb{i}", [128, 512]) for i in range(7)]
    ptb = pst("ptb", [128, 1024], BF16)
    PROJ, ROWS, REPS = pb[0], pb[1], pb[2]
    SCB_ = [(pb[3], pb[4]), (pb[5], pb[6])]

    sT = b.sem("sT")
    sV = b.sem("sV")
    sS = b.sem("sS")
    cc = b.sem("cc")
    sd_c = b.sem("sd_c")
    sd_x = b.sem("sd_x")
    sd_w = [b.sem("sd_w0"), b.sem("sd_w1")]
    sd_h1 = b.sem("sd_h1")
    sd_bet = b.sem("sd_bet")
    sd_mvec = b.sem("sd_mvec")
    sd_hin = [b.sem("sd_hin0"), b.sem("sd_hin1")]
    sd_bcol = b.sem("sd_bcol")
    sd_o = b.sem("sd_o")
    sd_oall = b.sem("sd_oall")
    sd_tal = b.sem("sd_tal")
    sd_te = b.sem("sd_te")
    sd_h1b = b.sem("sd_h1b")
    sd_wr = b.sem("sd_wr")
    sd_d = [b.sem("sd_d0"), b.sem("sd_d1")]
    sd_out = b.sem("sd_out")

    bank_tok = {id(p): None for p in pb}
    bank_tok[id(ptb)] = None

    def mm_group(bank, reg, pairs, waits, inc=None):
        n = len(pairs)
        tok = None
        for k, (lh, rhs_) in enumerate(pairs):
            w = []
            if k == 0:
                w = list(waits)
                if bank_tok[id(bank)] is not None:
                    w.append(bank_tok[id(bank)])
            ap = bank[reg[0], reg[1]] if isinstance(reg, tuple) else bank[:, reg]
            tok = b.emit("tensor", lambda e, lh=lh, rhs_=rhs_, k=k, ap=ap: e.matmul(
                ap, lh, rhs_, start=(k == 0), stop=(k == n - 1)),
                waits=w, inc=inc if k == n - 1 else None)
        return tok

    def bank_done(bank, tok):
        bank_tok[id(bank)] = tok

    def tr(bank, reg, in_, ident, waits, inc=None):
        w = list(waits)
        if bank_tok[id(bank)] is not None:
            w.append(bank_tok[id(bank)])
        ap = bank[reg[0], reg[1]] if isinstance(reg, tuple) else bank[:, reg]
        return b.emit("tensor", lambda e, ap=ap: e.transpose(ap, in_, ident), waits=w, inc=inc)

    stage_free = [None, None]
    stream_i = [0]
    stream_last = {"vector": None, "scalar": None}

    def stream_sync():
        return [stream_last["vector"], stream_last["scalar"]]

    def stream_piece(src_fn, dst_fn, extra_load_waits=(), extra_cv_waits=(), rows=128, ncols=1024, a=None):
        i = stream_i[0]
        stream_i[0] += 1
        sl = i % 2
        so = sl * 1024
        if a is None:
            out_ap = stage_f[0:rows, so:so + ncols]
        else:
            out_ap = stage_f[0:rows, so:so + ncols].rearrange("p (a c) -> p a c", a=a)
        tload = b.emit("sync", lambda e, out_ap=out_ap: e.dma_start(out=out_ap, in_=src_fn()),
                       waits=[stage_free[sl]] + list(extra_load_waits), inc=sd_w[sl], amt=16)
        eng = "vector" if i % 2 == 0 else "scalar"
        src_sb = stage_f[0:rows, so:so + ncols]
        if eng == "vector":
            fn = lambda e, src_sb=src_sb: e.tensor_copy(dst_fn(), src_sb)
        else:
            fn = lambda e, src_sb=src_sb: e.activation(dst_fn(), src_sb, AF.Copy)
        cv = b.emit(eng, fn, waits=[tload] + list(extra_cv_waits), inc=sV if eng == "vector" else sS)
        stage_free[sl] = cv
        stream_last[eng] = cv
        return cv

    tk_c = None
    for out, src in [(c_idf[:], identf), (c_idb[:], identb), (c_ocb[:], ones_cb),
                     (c_orb[:], ones_rb), (c_orf[:], ones_rf), (c_msl[:], mask_sl),
                     (c_msu[:], mask_su), (c_mul[:], mask_ul), (c_onw[:], onw),
                     (cT_s[:], cT_d), (cst[:], cstc)]:
        tk_c = b.emit("sync", lambda e, o=out, s=src: e.dma_start(out=o, in_=s[:, :]), inc=sd_c, amt=16)
    tk_x = b.emit("sync", lambda e: e.dma_start(
        out=big_a[:].rearrange("p (f t) -> p f t", f=NF),
        in_=xT.rearrange("(f p) t -> p f t", p=128)), inc=sd_x, amt=16)
    tk_cS = b.emit("scalar", lambda e: e.activation(cS[:], cT_s[:], AF.Silu), waits=[tk_c], inc=sS)

    MODB = [pb[3], pb[4], pb[5]]
    tk_modmm = None
    for f in range(16):
        for n in range(3):
            i = stream_i[0]
            stream_i[0] += 1
            sl = i % 2
            so = sl * 1024
            tload = b.emit("sync", lambda e, f=f, n=n, so=so: e.dma_start(
                out=stage_f[:, so:so + 512],
                in_=wmod[128 * f:128 * (f + 1), 512 * n:512 * (n + 1)]),
                waits=[stage_free[sl]], inc=sd_w[sl], amt=16)
            w = [tload]
            if f == 0:
                w += [tk_cS, bank_tok[id(MODB[n])]]
            tk_modmm = b.emit("tensor", lambda e, f=f, n=n, so=so: e.matmul(
                MODB[n][0:1, 0:512], cS[:, f:f + 1], stage_f[:, so:so + 512],
                start=(f == 0), stop=(f == 15)), waits=w, inc=sT)
            stage_free[sl] = tk_modmm
    tk_mode = None
    for n in range(3):
        tk_mode = b.emit("vector", lambda e, n=n: e.tensor_copy(
            R_MOD()[0:1, 512 * n:512 * (n + 1)], MODB[n][0:1, 0:512]), waits=[tk_modmm], inc=sV)
        bank_done(MODB[n], tk_mode)
    tk_modst = b.emit("scalar", lambda e: e.dma_start(out=mod_loc[:, :], in_=R_MOD()),
                      waits=[tk_mode], inc=sd_h1, amt=16)
    tk_ag1 = b.emit("gpsimd", lambda e: e.collective_compute(
        "AllGather", ALU.bypass, replica_groups=[list(range(NC))],
        ins=[mod_loc], outs=[mod_all]), waits=[tk_modst], inc=cc)
    tk_mvl = b.emit("sync", lambda e: e.dma_start(
        out=rep_s[0:96, 0:128],
        in_=mod_all.rearrange("a b -> (a b)").rearrange("(q r) -> q r", r=128)),
        waits=[tk_ag1], inc=sd_mvec, amt=16)
    tk_mvt = tr(REPS, slice(0, 96), rep_s[0:96, 0:128], c_idf[0:96, 0:96], [tk_mvl, tk_c], inc=sT)
    tk_mv = b.emit("vector", lambda e: e.tensor_copy(mvec[:, 0:96], REPS[:, 0:96]),
                   waits=[tk_mvt], inc=sV)
    bank_done(REPS, tk_mv)

    def clamp_emit(v, prev):
        sl = slice(v * 16, v * 16 + 16)
        t1 = b.emit("vector", lambda e: e.tensor_scalar(rep_s[:, 128:144], mvec[:, sl], C_ZERO, None, ALU.is_ge),
                    waits=[tk_mv, prev, tk_c], inc=sV)
        t2 = b.emit("vector", lambda e: e.tensor_scalar(rep_s[:, 160:176], rep_s[:, 128:144], C_TWO, C_NEG1, ALU.mult, ALU.add),
                    waits=[t1], inc=sV)
        t3 = b.emit("scalar", lambda e: e.activation(rep_s[:, 144:160], mvec[:, sl], AF.Abs), waits=[tk_mv, prev], inc=sS)
        t4 = b.emit("vector", lambda e: e.tensor_scalar(rep_s[:, 176:192], rep_s[:, 144:160], C_QTR, None, ALU.max),
                    waits=[t3], inc=sV)
        return b.emit("vector", lambda e: e.tensor_tensor(mvec[:, sl], rep_s[:, 160:176], rep_s[:, 176:192], ALU.mult),
                      waits=[t2, t4], inc=sV)

    tk_ga = clamp_emit(2, None)
    tk_gb = clamp_emit(5, tk_ga)
    tk_ca = b.emit("vector", lambda e: e.tensor_scalar(mvec[:, 96:112], mvec[:, 16:32], C_ONE, None, ALU.add),
                   waits=[tk_mv, tk_c], inc=sV)
    tk_cb = b.emit("vector", lambda e: e.tensor_scalar(mvec[:, 112:128], mvec[:, 64:80], C_ONE, None, ALU.add),
                   waits=[tk_mv, tk_c], inc=sV)

    def emit_ln(src, dst, cmod, smod, wait_stats, wait_apply, rowbanks):
        bsum = [rowbanks[0], rowbanks[1]]
        bsq = [rowbanks[2], rowbanks[3]]
        tk_mm = None
        mm_toks = [None] * NF
        for f in range(NF):
            par = f % 2
            w_ov = [mm_toks[f - 2]] if f >= 2 else []
            tb = b.emit("vector", lambda e, f=f, par=par: e.tensor_copy(LNB2(par), src(f)),
                        waits=wait_stats(f) + w_ov, inc=sV)
            ts_ = b.emit("scalar", lambda e, f=f, par=par: e.activation(LNB1(par), src(f), AF.Square),
                         waits=wait_stats(f) + w_ov, inc=sS)
            for half in range(2):
                sl = slice(half * 512, half * 512 + 512)
                b.emit("tensor", lambda e, half=half, sl=sl, f=f, par=par: e.matmul(
                    bsum[half][0:1, 0:512], c_ocb[:, 0:1], LNB2(par)[:, sl],
                    start=(f == 0), stop=(f == NF - 1)),
                    waits=[tb, bank_tok[id(bsum[half])]] if f == 0 else [tb],
                )
                tk_mm = b.emit("tensor", lambda e, half=half, sl=sl, f=f, par=par: e.matmul(
                    bsq[half][0:1, 0:512], c_ocb[:, 0:1], LNB1(par)[:, sl],
                    start=(f == 0), stop=(f == NF - 1)),
                    waits=[ts_, bank_tok[id(bsq[half])]] if f == 0 else [ts_],
                    inc=sT)
            mm_toks[f] = tk_mm
        tk_apply = None
        reps_tok = None
        for half in range(2):
            sl = slice(half * 512, half * 512 + 512)
            c1 = b.emit("vector", lambda e, half=half: e.tensor_copy(
                rowb[0:1, 0:512], bsum[half][0:1, 0:512]), waits=[tk_mm, reps_tok], inc=sV)
            c2 = b.emit("vector", lambda e, half=half: e.tensor_copy(
                rowb[0:1, 512:1024], bsq[half][0:1, 0:512]), waits=[tk_mm, reps_tok], inc=sV)
            if half == 1:
                for bk in rowbanks:
                    bank_done(bk, c2)
            t1 = mm_group(REPS, slice(0, 512), [(c_orb[0:1, :], rowb[0:1, 0:512])], [c1, reps_tok], inc=sT)
            em = b.emit("vector", lambda e: e.tensor_scalar(rep2[:, :], REPS[:, 0:512], C_INVD, None, ALU.mult),
                        waits=[t1, tk_c], inc=sV)
            bank_done(REPS, em)
            t2 = mm_group(REPS, slice(0, 512), [(c_orb[0:1, :], rowb[0:1, 512:1024])], [c2], inc=sT)
            reps_tok = t2
            tms = b.emit("vector", lambda e: e.tensor_tensor(S1(), rep2[:, :], rep2[:, :], ALU.mult),
                         waits=[em], inc=sV)
            ev = b.emit("vector", lambda e: e.scalar_tensor_tensor(S2(), REPS[:, 0:512], C_INVD, S1(),
                                                                   ALU.mult, ALU.subtract), waits=[t2, tms], inc=sV)
            bank_done(REPS, ev)
            tv = b.emit("vector", lambda e: e.tensor_scalar(S1(), S2(), C_EPS, None, ALU.add),
                        waits=[ev], inc=sV)
            tq = b.emit("scalar", lambda e: e.activation(S2(), S1(), AF.Sqrt), waits=[tv], inc=sS)
            ti = b.emit("vector", lambda e: e.reciprocal(rep1[:, :], S2()), waits=[tq], inc=sV)
            tmi = b.emit("vector", lambda e: e.tensor_tensor(S1(), rep2[:, :], rep1[:, :], ALU.mult),
                         waits=[ti], inc=sV)
            for f in range(NF):
                ta1 = b.emit("vector", lambda e, f=f, sl=sl: e.tensor_tensor(
                    S2(), src(f)[:, sl], rep1[:, :], ALU.mult), waits=[tmi] + list(wait_apply), inc=sV)
                ta2 = b.emit("vector", lambda e: e.tensor_sub(rep2[:, :], S2(), S1()), waits=[ta1], inc=sV)
                tk_apply = b.emit("vector", lambda e, f=f, sl=sl: e.tensor_scalar(
                    dst(f)[:, sl], rep2[:, :], cmod(f), smod(f), ALU.mult, ALU.add), waits=[ta2], inc=sV)
        return tk_apply

    tk_h1 = emit_ln(
        src=lambda f: big_a[:, f * TL:(f + 1) * TL],
        dst=lambda f: big_b[:, f * TL:(f + 1) * TL],
        cmod=lambda f: mvec[:, 96 + f:97 + f],
        smod=lambda f: mvec[:, f:f + 1],
        wait_stats=lambda f: [tk_x],
        wait_apply=[tk_ca, tk_ga, tk_gb, tk_cb],
        rowbanks=(PROJ, ROWS, pb[3], pb[6]),
    )
    tk_h1st = b.emit("scalar", lambda e: e.dma_start(
        out=h1_loc.rearrange("(f p) t -> p f t", p=128),
        in_=big_b[:].rearrange("p (f t) -> p f t", f=NF)), waits=[tk_h1], inc=sd_h1b, amt=16)
    tk_ag2 = b.emit("gpsimd", lambda e: e.collective_compute(
        "AllGather", ALU.bypass, replica_groups=[list(range(NC))],
        ins=[h1_loc], outs=[h1_all]), waits=[tk_h1st], inc=cc)

    tk_wbc = None
    for hf in range(2):
        tk_wbc = stream_piece(
            lambda hf=hf: wb.rearrange("(a p) c -> p a c", p=128)[:, hf * 8:(hf + 1) * 8, :],
            lambda hf=hf: dwn_b[:, 12288 + hf * 128:12288 + (hf + 1) * 128],
            rows=128, ncols=128, a=8)
    tk_betmm = [None, None]
    for half in range(2):
        tk_betmm[half] = mm_group(
            [pb[3], pb[4]][half], (slice(0, H), slice(0, 512)),
            [(dwn_b[:, 12288 + f * 16:12288 + (f + 1) * 16],
              big_b[:, f * TL + half * 512: f * TL + half * 512 + 512]) for f in range(NF)],
            [tk_h1] + stream_sync(), inc=sT)
    tk_betst = None
    prev_betst = None
    for half in range(2):
        tk_bete = b.emit("scalar", lambda e, half=half: e.activation(
            lo16f[0:16, 0:512], [pb[3], pb[4]][half][0:H, 0:512], AF.Sigmoid),
            waits=[tk_betmm[half], prev_betst], inc=sS)
        bank_done([pb[3], pb[4]][half], tk_bete)
        tk_betst = b.emit("scalar", lambda e, half=half: e.dma_start(
            out=bet_loc[:, half * 512: half * 512 + 512], in_=lo16f[0:16, :]),
            waits=[tk_bete], inc=sd_bet, amt=16)
        prev_betst = tk_betst
    tk_a2a_b = b.emit("gpsimd", lambda e: e.collective_compute(
        "AllToAll", ALU.bypass, replica_groups=[list(range(NC))],
        ins=[bet_loc], outs=[bet_all]), waits=[tk_betst], inc=cc)

    def emit_done(extra):
        b.emit("gpsimd", lambda e: e.nop(),
               waits=[(sd_out, sd_out.n), (cc, cc.n), (sd_d[0], sd_d[0].n), (sd_d[1], sd_d[1].n),
                      (sd_h1b, sd_h1b.n), (sd_wr, sd_wr.n)] + list(extra))

    dump_prevs = [None, None]

    def dump_f32(dst_dram, src_tiles, nt, wait):
        prevs = dump_prevs
        i = 0
        for f in range(nt):
            for half in range(2):
                par = i % 2
                i += 1
                t1 = b.emit("vector", lambda e, f=f, half=half, par=par: e.tensor_copy(
                    rep_s[:, par * 512: par * 512 + 512], src_tiles(f)[:, half * 512:half * 512 + 512]),
                    waits=[wait, prevs[par]], inc=sV)
                prevs[par] = b.emit("scalar", lambda e, f=f, half=half, par=par: e.dma_start(
                    out=dst_dram[f * 128:(f + 1) * 128, half * 512:half * 512 + 512],
                    in_=rep_s[:, par * 512: par * 512 + 512]), waits=[t1], inc=sd_d[par], amt=16)

    def zero_out():
        t9 = b.emit("vector", lambda e: e.memset(LNF(), 0.0),
                    waits=[(sd_out, sd_out.n), (sd_d[0], sd_d[0].n), (sd_d[1], sd_d[1].n)], inc=sV)
        for m_ in range(NF):
            for half in range(2):
                b.emit("scalar", lambda e, m_=m_, half=half: e.dma_start(
                    out=outT[m_ * 128:(m_ + 1) * 128, half * 512:half * 512 + 512],
                    in_=LNF()), waits=[t9], inc=sd_out, amt=16)

    if stage == "h1":
        dump_f32(dbg["h1T_dbg"], lambda f: big_b[:, f * TL:(f + 1) * TL], NF, tk_h1)
        tbl = b.emit("sync", lambda e: e.dma_start(out=lo16f[0:16, :], in_=bet_loc[:, 0:512]),
                     waits=[tk_betst], inc=sd_out, amt=16)
        b.emit("sync", lambda e: e.dma_start(out=dbg["bet_dbg"][:, 0:512], in_=lo16f[0:16, :]),
               waits=[tbl], inc=sd_out, amt=16)
        tbl2 = b.emit("sync", lambda e: e.dma_start(out=lo16f[0:16, :], in_=bet_loc[:, 512:1024]),
                      waits=[(sd_out, sd_out.n)], inc=sd_out, amt=16)
        b.emit("sync", lambda e: e.dma_start(out=dbg["bet_dbg"][:, 512:1024], in_=lo16f[0:16, :]),
               waits=[tbl2], inc=sd_out, amt=16)
        b.emit("sync", lambda e: e.dma_start(out=dbg["mod_dbg"][:, :], in_=mvec[:]),
               waits=[tk_ga, tk_gb, tk_ca, tk_cb], inc=sd_out, amt=16)
        zero_out()
        emit_done([])
        return finish(nc, ctx, b), dbg

    WQKV = lambda f, proj, m: dwn_b[:, f * 768 + proj * 256 + m * 128: f * 768 + proj * 256 + (m + 1) * 128]
    tk_wqkv_all = None
    for proj, wsrc in [(0, wq), (1, wk), (2, wv)]:
        for m in range(2):
            for kh in range(2):
                tk_wqkv_all = stream_piece(
                    lambda wsrc=wsrc, m=m, kh=kh: wsrc.rearrange("(f p) c -> p f c", p=128)[
                        :, kh * 8:(kh + 1) * 8, m * 128:(m + 1) * 128],
                    lambda proj=proj, m=m, kh=kh: dwn_b[:, 0:12288].rearrange(
                        "p (f c) -> p f c", f=NF)[:, kh * 8:(kh + 1) * 8,
                                                  proj * 256 + m * 128: proj * 256 + (m + 1) * 128],
                    rows=128, ncols=1024, a=8)
    tk_wqkv_sync = stream_sync()

    tk_s0 = b.emit("vector", lambda e: e.memset(S_m[:], 0.0), inc=sV)
    tk_sb0 = b.emit("vector", lambda e: e.memset(S_b[:], 0.0), waits=[tk_s0], inc=sV)
    brow_guard = [tk_a2a_b]

    hin_tok = [None, None]
    hin_free = [None, None]
    prev_tbr = None
    qkv_free = [None, None]
    sb_tok = [tk_sb0, tk_sb0]
    sm_tok = [tk_s0, tk_s0]
    last_oT = None
    first_oT = True

    NSC_LIM = NSC
    for s in range(NSC_LIM):
        sp = s % 2
        jblk = s // 4
        col0 = 256 * (s % 4)
        hin_tok[sp] = b.emit("sync", lambda e, sp=sp, jblk=jblk, col0=col0: e.dma_start(
            out=big_c[:, sp * 4096: sp * 4096 + 4096].rearrange("p (f t) -> p f t", f=NF),
            in_=h1_all.rearrange("(j f p) t -> p j f t", p=128, f=NF)[:, jblk, :, col0:col0 + SC]),
            waits=[tk_ag2, hin_free[sp]], inc=sd_hin[sp], amt=16)
        tk_brow = None
        for hs in range(2):
            tk_brow = b.emit("sync", lambda e, hs=hs, jblk=jblk, col0=col0: e.dma_start(
                out=lo16f[0:1, hs * 256:(hs + 1) * 256],
                in_=bet_all[2 * jblk + hs:2 * jblk + hs + 1, col0:col0 + SC]),
                waits=brow_guard, inc=sd_bcol, amt=16)
        brow_consumers = []

        ev = []
        prev_trm_l2 = None
        for gi, (proj, m) in enumerate([(0, 0), (0, 1), (1, 0), (1, 1), (2, 0), (2, 1)]):
            grp = mm_group(PROJ, slice(0, 256),
                           [(WQKV(k, proj, m), big_c[:, sp * 4096 + k * 256: sp * 4096 + (k + 1) * 256])
                            for k in range(NF)],
                           [hin_tok[sp], tk_wqkv_sync[0], tk_wqkv_sync[1], qkv_free[sp]], inc=sT)
            qidx = proj * 2 + m
            if proj < 2:
                tsq = b.emit("scalar", lambda e: e.activation(big_d[:, SQ2], PROJ[:, 0:256], AF.Square),
                             waits=[grp], inc=sS)
                tss = mm_group(ROWS, (slice(0, 1), slice(0, 256)), [(c_ocb[:, 0:1], big_d[:, SQ2])],
                               [tsq], inc=sT)
                trb = b.emit("vector", lambda e: e.tensor_copy(rowb[0:1, 0:256], ROWS[0:1, 0:256]),
                             waits=[tss], inc=sV)
                bank_done(ROWS, trb)
                trm = mm_group(REPS, slice(0, 256), [(c_orb[0:1, :], rowb[0:1, 0:256])], [trb], inc=sT)
                prev_trm_l2 = trm
                tra = b.emit("scalar", lambda e: e.activation(L2A(), REPS[:, 0:256], AF.Sqrt),
                             waits=[trm], inc=sS)
                bank_done(REPS, tra)
                tmx = b.emit("vector", lambda e: e.tensor_scalar(L2B(), L2A(), C_TINY, None, ALU.max),
                             waits=[tra, tk_c], inc=sV)
                trr = b.emit("vector", lambda e: e.reciprocal(L2A(), L2B()), waits=[tmx], inc=sV)
                if proj == 0:
                    trr = b.emit("vector", lambda e: e.tensor_scalar(L2B(), L2A(), C_QKS, None, ALU.mult),
                                 waits=[trr], inc=sV)
                    fin = L2B
                else:
                    fin = L2A
                tev = b.emit("vector", lambda e, sp=sp, qidx=qidx, fin=fin: e.tensor_tensor(
                    big_d[:, QB(sp, qidx)], PROJ[:, 0:256], fin(), ALU.mult),
                    waits=[trr], inc=sV)
            else:
                tev = b.emit("scalar", lambda e, sp=sp, qidx=qidx: e.activation(
                    big_d[:, QB(sp, qidx)], PROJ[:, 0:256], AF.Silu), waits=[grp], inc=sS)
            ev.append(tev)
            bank_done(PROJ, tev)
        hin_free[sp] = ev[-1]

        for ci in range(CPS):
            c = s * CPS + ci
            tc = slice(ci * 128, (ci + 1) * 128)
            for hs in range(2):
                SCB, SCB2 = SCB_[hs]
                Qf = big_d[:, QB(sp, 0 + hs)][:, tc]
                Kf = big_d[:, QB(sp, 2 + hs)][:, tc]
                Vf = big_d[:, QB(sp, 4 + hs)][:, tc]
                brsl = lo16f[0:1, hs * 256 + ci * 128: hs * 256 + ci * 128 + 128]
                bcol = bcols[:, hs * 2 + sp: hs * 2 + sp + 1]

                tbc = mm_group(ROWS, (slice(0, 128), slice(448, 449)), [(brsl, c_orf[0:1, 0:1])],
                               [tk_brow], inc=sT)
                tbce = b.emit("vector", lambda e, hs=hs, sp=sp: e.tensor_copy(
                    bcols[:, hs * 2 + sp: hs * 2 + sp + 1], ROWS[:, 448:449]), waits=[tbc], inc=sV)
                bank_done(ROWS, tbce)
                tbrb = b.emit("vector", lambda e, brsl=brsl: e.tensor_copy(rowb[0:1, 384:512], brsl),
                              waits=[tk_brow, prev_tbr], inc=sV)
                brow_consumers.append(tbrb)
                tbr = mm_group(REPS, slice(0, 128), [(c_orb[0:1, :], rowb[0:1, 384:512])], [tbrb], inc=sT)
                tbm = b.emit("vector", lambda e, hs=hs: e.tensor_tensor(
                    rep_s[:, bm_sl(hs)], REPS[:, 0:128], c_msu[:], ALU.mult), waits=[tbr], inc=sV)
                bank_done(REPS, tbm)
                prev_tbr = tbr
                brow_consumers.append(tbc)

                tG = mm_group(SCB, slice(0, 128), [(Kf, Kf)], [ev[2 + hs]], inc=sT)
                tA = b.emit("vector", lambda e, hs=hs, bcol=bcol, SCB=SCB: e.scalar_tensor_tensor(
                    big_d[:, SCR(hs, SL_A)], SCB[:, 0:128], bcol, c_msl[:], ALU.mult, ALU.mult),
                    waits=[tG, tbce], inc=sV)
                tB = b.emit("vector", lambda e, hs=hs, SCB=SCB: e.tensor_tensor(
                    big_d[:, SCR(hs, SL_B)], SCB[:, 0:128], rep_s[:, bm_sl(hs)], ALU.mult),
                    waits=[tbm, tA], inc=sV)
                bank_done(SCB, tB)

                tKt = tr(ptb, slice(0, 128), Kf, c_idb[:], [ev[2 + hs]], inc=sT)
                tKtok = b.emit("scalar", lambda e, hs=hs: e.activation(big_d[:, SCR(hs, SL_KT)], ptb[:, 0:128], AF.Copy),
                               waits=[tKt], inc=sS)
                tKb = b.emit("vector", lambda e, hs=hs, bcol=bcol: e.tensor_scalar(
                    big_d[:, SCR(hs, SL_KB)], ptb[:, 0:128], bcol, None, ALU.mult),
                    waits=[tKt, tbce], inc=sV)
                bank_done(ptb, tKb)
                tVt = tr(ptb, slice(128, 256), Vf, c_idb[:], [ev[4 + hs], tKtok], inc=sT)
                tVb = b.emit("vector", lambda e, hs=hs, bcol=bcol: e.tensor_scalar(
                    big_d[:, SCR(hs, SL_VB)], ptb[:, 128:256], bcol, None, ALU.mult),
                    waits=[tVt, tbce], inc=sV)
                bank_done(ptb, tVb)

                tX = b.emit("vector", lambda e, hs=hs: e.tensor_sub(
                    big_d[:, SCR(hs, SL_MT)], c_idb[:], big_d[:, SCR(hs, SL_B)]),
                    waits=[tB], inc=sV)

                aslot = {1: SL_A, 2: SL_A2, 4: SL_A4, 8: SL_A8, 16: SL_A16, 32: SL_A32, 64: SL_A64}
                atok = {1: tA}
                prevB, tokB = SL_B, tB
                bslots = [SL_X, SL_B]
                bs_i = 0
                pw = 1
                while pw < 64:
                    npw = 2 * pw
                    t1 = mm_group(SCB, slice(0, 128),
                                  [(big_d[:, SCR(hs, prevB)], big_d[:, SCR(hs, aslot[pw])])],
                                  [atok[pw], tokB], inc=sT)
                    ta_n = b.emit("scalar", lambda e, hs=hs, d=aslot[npw], SCB=SCB: e.activation(
                        big_d[:, SCR(hs, d)], SCB[:, 0:128], AF.Copy), waits=[t1], inc=sS)
                    bank_done(SCB, ta_n)
                    atok[npw] = ta_n
                    if npw < 64:
                        t2 = mm_group(SCB2, slice(0, 128),
                                      [(big_d[:, SCR(hs, aslot[pw])], big_d[:, SCR(hs, prevB)])],
                                      [atok[pw], tokB, tX], inc=sT)
                        nbs = bslots[bs_i % 2]
                        bs_i += 1
                        tokB = b.emit("scalar", lambda e, hs=hs, nbs=nbs, SCB2=SCB2: e.activation(
                            big_d[:, SCR(hs, nbs)], SCB2[:, 0:128], AF.Copy), waits=[t2], inc=sS)
                        bank_done(SCB2, tokB)
                        prevB = nbs
                    pw = npw
                for jp in [2, 4, 8, 16, 32, 64]:
                    tj = mm_group(SCB, slice(0, 128),
                                  [(c_idb[:], big_d[:, SCR(hs, SL_MT)]),
                                   (big_d[:, SCR(hs, aslot[jp])], big_d[:, SCR(hs, SL_MT)])],
                                  [tX, atok[jp]], inc=sT)
                    tX = b.emit("scalar", lambda e, hs=hs, SCB=SCB: e.activation(
                        big_d[:, SCR(hs, SL_MT)], SCB[:, 0:128], AF.Copy), waits=[tj], inc=sS)
                    bank_done(SCB, tX)
                tMt = tX

                tw = mm_group(SCB2, slice(0, 128),
                              [(big_d[:, SCR(hs, SL_KB)], big_d[:, SCR(hs, SL_MT)])],
                              [tKb, tMt], inc=sT)
                tW0 = b.emit("scalar", lambda e, hs=hs, SCB2=SCB2: e.mul(big_d[:, SCR(hs, SL_W0)], SCB2[:, 0:128], -1.0),
                             waits=[tw], inc=sS)
                bank_done(SCB2, tW0)

                tt1 = mm_group(SCB, slice(0, 128), [(Kf, Qf)], [ev[0 + hs], ev[2 + hs]], inc=sT)
                tTt = b.emit("vector", lambda e, hs=hs, SCB=SCB: e.tensor_tensor(
                    big_d[:, SCR(hs, SL_TT)], SCB[:, 0:128], c_mul[:], ALU.mult), waits=[tt1], inc=sV)
                bank_done(SCB, tTt)

                tu = mm_group(SCB2, slice(0, 128),
                              [(big_d[:, SCR(hs, SL_MT)], big_d[:, SCR(hs, SL_VB)]),
                               (big_d[:, SCR(hs, SL_W0)], S_b[:, hs * DH:(hs + 1) * DH])],
                              [tMt, tVb, tW0, sb_tok[hs]], inc=sT)
                tU = b.emit("scalar", lambda e, hs=hs, SCB2=SCB2: e.activation(big_d[:, SCR(hs, SL_U)], SCB2[:, 0:128], AF.Copy),
                            waits=[tu], inc=sS)
                bank_done(SCB2, tU)

                to = mm_group(SCB, slice(0, 128),
                              [(S_b[:, hs * DH:(hs + 1) * DH], Qf),
                               (big_d[:, SCR(hs, SL_U)], big_d[:, SCR(hs, SL_TT)])],
                              [sb_tok[hs], ev[0 + hs], tU, tTt], inc=sT)
                tosq = b.emit("scalar", lambda e, SCB=SCB: e.activation(big_d[:, SQ1], SCB[:, 0:128], AF.Square),
                              waits=[to], inc=sS)
                tss2 = mm_group(ROWS, (slice(0, 1), slice(256, 384)), [(c_ocb[:, 0:1], big_d[:, SQ1])],
                                [tosq], inc=sT)
                tonb = b.emit("vector", lambda e: e.tensor_copy(rowb[0:1, 256:384], ROWS[0:1, 256:384]),
                              waits=[tss2], inc=sV)
                bank_done(ROWS, tonb)
                trep = mm_group(REPS, slice(128, 256), [(c_orb[0:1, :], rowb[0:1, 256:384])], [tonb], inc=sT)
                tia = b.emit("vector", lambda e: e.tensor_scalar(ON1(), REPS[:, 128:256],
                                                                 C_INVDH, C_EPS, ALU.mult, ALU.add),
                             waits=[trep, tk_c], inc=sV)
                bank_done(REPS, tia)
                tib = b.emit("scalar", lambda e: e.activation(ON2(), ON1(), AF.Sqrt), waits=[tia], inc=sS)
                trepe = b.emit("vector", lambda e: e.reciprocal(ON1(), ON2()), waits=[tib], inc=sV)
                w_o = [trepe]
                if first_oT:
                    w_o += [tk_h1st, tk_betmm[0], tk_betmm[1]]
                    first_oT = False
                tOe = b.emit("vector", lambda e, hs=hs, c=c, SCB=SCB: e.scalar_tensor_tensor(
                    big_b[:, oT_col(hs, c)], SCB[:, 0:128], c_onw[:, 0:1], ON1(),
                    ALU.mult, ALU.mult), waits=w_o, inc=sV)
                bank_done(SCB, tOe)
                last_oT = tOe

                tsu = mm_group(SCB2, slice(0, 128),
                               [(big_d[:, SCR(hs, SL_KT)], big_d[:, SCR(hs, SL_U)])],
                               [tKtok, tU], inc=sT)
                tsm = b.emit("vector", lambda e, hs=hs, SCB2=SCB2: e.tensor_add(
                    S_m[:, hs * DH:(hs + 1) * DH], SCB2[:, 0:128], S_m[:, hs * DH:(hs + 1) * DH]),
                    waits=[tsu, sm_tok[hs]], inc=sV)
                tsb = b.emit("scalar", lambda e, hs=hs: e.activation(
                    S_b[:, hs * DH:(hs + 1) * DH], S_m[:, hs * DH:(hs + 1) * DH], AF.Copy),
                    waits=[tsm], inc=sS)
                bank_done(SCB2, tsb)
                sm_tok[hs] = tsm
                sb_tok[hs] = tsb
            if SCAN_DBG and s == 0 and ci == 0:
                break
        qkv_free[sp] = last_oT
        brow_guard = brow_consumers[-2:]
        if SCAN_DBG and s == 0:
            break

    if stage == "scan" and SCAN_DBG:
        allw = [(sS, sS.n), (sV, sV.n), (sT, sT.n)]
        t1 = b.emit("sync", lambda e: e.dma_start(out=dbg["scr_dbg"][:, 0:1536], in_=big_d[:, 0:1536]),
                    waits=allw, inc=sd_out, amt=16)
        t1 = b.emit("sync", lambda e: e.dma_start(out=dbg["scr_dbg"][:, 4608:9088], in_=big_d[:, 4608:9088]),
                    waits=allw, inc=sd_out, amt=16)
        t2 = b.emit("sync", lambda e: e.dma_start(out=dbg["sm_dbg"][:, :], in_=S_m[:]),
                    waits=allw, inc=sd_d[0], amt=16)
        t3 = b.emit("sync", lambda e: e.dma_start(out=dbg["bc_dbg"][:, 0:1], in_=bcols[:, 0:1], allow_slow_non_contiguous=True),
                    waits=allw, inc=sd_d[1], amt=16)
        t3 = b.emit("sync", lambda e: e.dma_start(out=dbg["bc_dbg"][:, 2:3], in_=bcols[:, 2:3], allow_slow_non_contiguous=True),
                    waits=allw, inc=sd_d[1], amt=16)
        prev_dd = None
        for hs in range(2):
            for blk in range(1):
                t9 = b.emit("vector", lambda e, hs=hs, blk=blk: e.tensor_copy(
                    LNF(), big_b[:, hs * 8192 + blk * 512: hs * 8192 + (blk + 1) * 512]),
                    waits=[last_oT, prev_dd, t1], inc=sV)
                prev_dd = b.emit("scalar", lambda e, hs=hs, blk=blk: e.dma_start(
                    out=dbg["o_dbg"][hs * 128:(hs + 1) * 128, blk * 512:(blk + 1) * 512],
                    in_=LNF()), waits=[t9], inc=sd_out, amt=16)
        zero_out()
        emit_done([])
        return finish(nc, ctx, b), dbg

    tk_ost = None
    for hs in range(2):
        tk_ost = b.emit("scalar", lambda e, hs=hs: e.dma_start(
            out=o_loc.rearrange("(r q p) t -> q p r t", p=128, q=2)[hs],
            in_=big_b[:, hs * 8192:(hs + 1) * 8192].rearrange("p (r t) -> p r t", r=8)),
            waits=[last_oT], inc=sd_o, amt=16)
    tk_a2a_o = b.emit("gpsimd", lambda e: e.collective_compute(
        "AllToAll", ALU.bypass, replica_groups=[list(range(NC))],
        ins=[o_loc], outs=[o_all]), waits=[tk_ost], inc=cc)

    if stage == "scan":
        prev_dd = None
        for hs in range(2):
            for blk in range(16):
                t1 = b.emit("vector", lambda e, hs=hs, blk=blk: e.tensor_copy(
                    LNF(), big_b[:, hs * 8192 + blk * 512: hs * 8192 + (blk + 1) * 512]),
                    waits=[last_oT, prev_dd], inc=sV)
                prev_dd = b.emit("scalar", lambda e, hs=hs, blk=blk: e.dma_start(
                    out=dbg["o_dbg"][hs * 128:(hs + 1) * 128, blk * 512:(blk + 1) * 512],
                    in_=LNF()), waits=[t1], inc=sd_out, amt=16)
        zero_out()
        emit_done([])
        return finish(nc, ctx, b), dbg

    tk_oall = b.emit("sync", lambda e: e.dma_start(
        out=big_b[:].rearrange("p (f t) -> p f t", f=NF),
        in_=o_all.rearrange("(f p) t -> p f t", p=128)),
        waits=[tk_a2a_o, last_oT], inc=sd_oall, amt=16)

    wop_free = [None, None]
    x2_toks = [None] * NF
    for m in range(NF):
        slm = m % 2
        tcv = None
        for kh in range(2):
            tcv = stream_piece(
                lambda m=m, kh=kh: wo.rearrange("(f p) c -> p f c", p=128)[
                    :, kh * 8:(kh + 1) * 8, m * 128:(m + 1) * 128],
                lambda slm=slm, kh=kh: dwn_b[:, 12288 + slm * 2048: 12288 + (slm + 1) * 2048].rearrange(
                    "p (f c) -> p f c", f=NF)[:, kh * 8:(kh + 1) * 8, :],
                extra_cv_waits=[wop_free[slm]], rows=128, ncols=1024, a=8)
        tcv_sync = stream_sync()
        tev = None
        for half in range(2):
            grp = mm_group(PROJ, slice(0, 512),
                           [(dwn_b[:, 12288 + slm * 2048 + k * 128: 12288 + slm * 2048 + (k + 1) * 128],
                             big_b[:, k * TL + half * 512: k * TL + half * 512 + 512]) for k in range(NF)],
                           tcv_sync + [tk_oall], inc=sT)
            tev = b.emit("vector", lambda e, m=m, half=half: e.scalar_tensor_tensor(
                big_a[:, m * TL + half * 512: m * TL + half * 512 + 512], PROJ[:, 0:512],
                mvec[:, 32 + m:33 + m], big_a[:, m * TL + half * 512: m * TL + half * 512 + 512],
                ALU.mult, ALU.add), waits=[grp, tk_ga], inc=sV)
            bank_done(PROJ, tev)
        x2_toks[m] = tev
        wop_free[slm] = tev

    tk_x2 = x2_toks[-1]
    tk_h2 = emit_ln(
        src=lambda f: big_a[:, f * TL:(f + 1) * TL],
        dst=lambda f: big_b[:, f * TL:(f + 1) * TL],
        cmod=lambda f: mvec[:, 112 + f:113 + f],
        smod=lambda f: mvec[:, 48 + f:49 + f],
        wait_stats=lambda f: [x2_toks[f], tk_x2],
        wait_apply=[tk_cb],
        rowbanks=(pb[3], pb[4], pb[5], pb[6]),
    )

    tk_gw = stream_piece(
        lambda: gwT.rearrange("(f p) c -> p f c", p=128),
        lambda: gw_b[:, :], rows=128, ncols=NF * E, a=NF)
    tk_gac1 = stream_piece(lambda: gA[:, 0:1024], lambda: lo16[0:16, 0:1024], rows=16, ncols=1024)
    tk_gac = stream_piece(lambda: gA[:, 1024:2048], lambda: lo16[0:16, 1024:2048], rows=16, ncols=1024)
    tk_gbc = stream_piece(lambda: gBT[:, :], lambda: lo16[0:16, 3072:3080], rows=16, ncols=8)
    tk_weff = None
    tk_router_sync = stream_sync()
    for f in range(NF):
        t1 = mm_group(REPS, slice(0, 8), [(lo16[0:16, f * 128:(f + 1) * 128], lo16[0:16, 3072:3080])],
                      tk_router_sync, inc=sT)
        tk_weff = b.emit("vector", lambda e, f=f: e.scalar_tensor_tensor(
            weff[:, f * 8:(f + 1) * 8], REPS[:, 0:8], C_TWO, gw_b[:, f * 8:(f + 1) * 8], ALU.mult, ALU.add),
            waits=[t1, tk_gw], inc=sV)
        bank_done(REPS, tk_weff)
    lrow = lo16f
    tk_w = None
    RT = lambda a, b2: rep_s[:, a:b2]
    prev_router = [tk_betst]
    rt_guard = []
    neg_guard = []
    for half in range(2):
        grp = mm_group(PROJ, (slice(0, 8), slice(0, 512)),
                       [(weff[:, f * 8:(f + 1) * 8], big_b[:, f * TL + half * 512: f * TL + half * 512 + 512])
                        for f in range(NF)], [tk_weff, tk_h2], inc=sT)
        tk_lge = b.emit("vector", lambda e: e.tensor_copy(
            lrow[0:8, 0:512], PROJ[0:8, 0:512]), waits=[grp] + prev_router, inc=sV)
        bank_done(PROJ, tk_lge)
        for tt_ in range(4):
            tile = half * 4 + tt_
            tt = tr(pb[3], slice(0, 8), lrow[0:8, tt_ * 128:(tt_ + 1) * 128], c_idf[0:8, 0:8],
                    [tk_lge], inc=sT)
            tl = b.emit("vector", lambda e, tile=tile: e.tensor_copy(wtok[:, tile * 8:(tile + 1) * 8], pb[3][:, 0:8]),
                        waits=[tt] + rt_guard, inc=sV)
            bank_done(pb[3], tl)
            Lap = wtok[:, tile * 8:(tile + 1) * 8]
            r1 = b.emit("vector", lambda e, Lap=Lap: e.reduce_max(RT(384, 385), Lap, axis=AX.X),
                        waits=[tl], inc=sV)
            r2 = b.emit("vector", lambda e, Lap=Lap: e.tensor_scalar(RT(400, 408), Lap, RT(384, 385), None, ALU.is_ge),
                        waits=[r1], inc=sV)
            r3 = b.emit("vector", lambda e, Lap=Lap: e.scalar_tensor_tensor(RT(424, 432), RT(400, 408), C_NEG30, Lap, ALU.mult, ALU.add),
                        waits=[r2], inc=sV)
            r4 = b.emit("vector", lambda e: e.reduce_max(RT(385, 386), RT(424, 432), axis=AX.X),
                        waits=[r3], inc=sV)
            tnm = b.emit("vector", lambda e: e.tensor_scalar(RT(386, 387), RT(384, 385), C_NEG1, None, ALU.mult),
                         waits=[r1] + neg_guard, inc=sV)
            tp = b.emit("scalar", lambda e, Lap=Lap: e.activation(RT(408, 416), Lap, AF.Exp, bias=RT(386, 387)),
                        waits=[tnm, tl], inc=sS)
            te2 = b.emit("scalar", lambda e: e.activation(RT(387, 388), RT(385, 386), AF.Exp, bias=RT(386, 387)),
                         waits=[tnm, r4], inc=sS)
            r5 = b.emit("vector", lambda e: e.tensor_scalar(RT(388, 389), RT(387, 388), C_ONE, None, ALU.add),
                        waits=[te2], inc=sV)
            r6 = b.emit("vector", lambda e: e.reciprocal(RT(387, 388), RT(388, 389)), waits=[r5], inc=sV)
            r7 = b.emit("vector", lambda e, Lap=Lap: e.tensor_scalar(RT(416, 424), Lap, RT(385, 386), None, ALU.is_ge),
                        waits=[r4, tl], inc=sV)
            r8 = b.emit("vector", lambda e: e.tensor_tensor(RT(400, 408), RT(408, 416), RT(416, 424), ALU.mult),
                        waits=[tp, r7, r3], inc=sV)
            tw_ = b.emit("vector", lambda e, tile=tile: e.tensor_scalar(
                wtok[:, tile * 8:(tile + 1) * 8], RT(400, 408), RT(387, 388), None, ALU.mult),
                waits=[r8, r6], inc=sV)
            tt2 = tr(pb[3], (slice(0, 8), slice(0, 128)), wtok[:, tile * 8:(tile + 1) * 8], c_idf[:], [tw_], inc=sT)
            tk_w = b.emit("vector", lambda e, tile=tile: e.tensor_copy(
                wrowb[0:8, tile * 128:(tile + 1) * 128], pb[3][0:8, 0:128]), waits=[tt2], inc=sV)
            bank_done(pb[3], tk_w)
            rt_guard = [tw_]
            neg_guard = [tp, te2]
        prev_router = [tk_w]
    tk_wrb = tk_w

    tk_gal = None
    for kh in range(2):
        tk_gal = stream_piece(
            lambda kh=kh: guAT_all.rearrange("(f p) c -> p f c", p=128)[:, kh * 8:(kh + 1) * 8, :],
            lambda kh=kh: big_d[:, 0:2048].rearrange("p (f c) -> p f c", f=NF)[:, kh * 8:(kh + 1) * 8, :],
            extra_cv_waits=[last_oT], rows=128, ncols=1024, a=8)
    tk_gal_sync = stream_sync()
    tk_tal = None
    for half in range(2):
        grp = mm_group(PROJ, slice(0, 512),
                       [(big_d[:, f * 128:(f + 1) * 128], big_b[:, f * TL + half * 512: f * TL + half * 512 + 512])
                        for f in range(NF)], tk_gal_sync + [tk_h2], inc=sT)
        tk_tal = b.emit("scalar", lambda e, half=half: e.mul(
            big_d[:, 2048 + half * 512: 2048 + half * 512 + 512], PROJ[:, 0:512], 2.0), waits=[grp], inc=sS)
        bank_done(PROJ, tk_tal)
    tk_talst = b.emit("scalar", lambda e: e.dma_start(
        out=tal_d.rearrange("(g p) t -> p g t", p=128),
        in_=big_d[:, 2048:3072].rearrange("p (g t) -> p g t", g=1)), waits=[tk_tal], inc=sd_tal, amt=16)

    if stage == "attn":
        dump_f32(dbg["h2_dbg"], lambda f: big_b[:, f * TL:(f + 1) * TL], NF, tk_wrb)
        dump_f32(dbg["x2_dbg"], lambda f: big_a[:, f * TL:(f + 1) * TL], NF, tk_wrb)
        for half in range(2):
            tww = b.emit("vector", lambda e, half=half: e.tensor_copy(
                lo16f[0:8, 0:512], wrowb[0:8, half * 512: half * 512 + 512]),
                waits=[tk_w, (sd_out, sd_out.n)], inc=sV)
            b.emit("sync", lambda e, half=half: e.dma_start(
                out=dbg["w_dbg"][:, half * 512: half * 512 + 512], in_=lo16f[0:8, 0:512]),
                waits=[tww], inc=sd_out, amt=16)
        zero_out()
        emit_done([(sd_tal, sd_tal.n)])
        return finish(nc, ctx, b), dbg

    GUPW = lambda k, q2: big_d[:, k * 512 + q2: k * 512 + q2 + 128]
    DNW = lambda k, mcol: dwn_b[:, k * 2048 + mcol: k * 2048 + mcol + 128]
    HC = lambda k, half: big_c[:, k * 1024 + half * 512: k * 1024 + half * 512 + 512]

    gup_guard = [tk_tal, (sd_tal, sd_tal.n)]
    wrow_guard = [tk_wrb, tk_weff]
    last_wrep = None
    dwn_guard = [tk_x2]
    hc_guard = [tk_h2, tk_lge, tk_tal]
    Te_guard = [(sd_tal, sd_tal.n)]
    guB_guard = [tk_weff]
    dnB_guard = [tk_weff, tk_gbc]
    last_out = None
    gup_mm_last = None
    last_lora = None
    for e_i in range(E):
        tk_Te = b.emit("sync", lambda e, e_i=e_i: e.dma_start(
            out=lo16[0:16, 2048:3072].rearrange("r (a t) -> r a t", a=1),
            in_=tal_d.rearrange("(e2 r) t -> r e2 t", r=R)[:, e_i:e_i + 1, :]),
            waits=Te_guard, inc=sd_te, amt=16)
        guB_t = []
        for kh in range(2):
            guB_t.append(stream_piece(
                lambda e_i=e_i, kh=kh: guBT[e_i][:, kh * 1024:(kh + 1) * 1024],
                lambda kh=kh: lo16[0:16, kh * 1024:(kh + 1) * 1024],
                extra_cv_waits=guB_guard, rows=16, ncols=1024))
        tk_guB2 = guB_t
        dnB_t = []
        for kh in range(2):
            dnB_t.append(stream_piece(
                lambda e_i=e_i, kh=kh: dnBT[e_i][:, kh * 1024:(kh + 1) * 1024],
                lambda kh=kh: lo16[32:48, kh * 1024:(kh + 1) * 1024],
                extra_cv_waits=dnB_guard, rows=16, ncols=1024))
        tk_dnB2 = dnB_t
        tk_wrl = b.emit("scalar", lambda e, e_i=e_i: e.dma_start(
            out=lo16[0:1, 3072:4096], in_=wrowb[e_i:e_i + 1, :]),
            waits=wrow_guard, inc=sd_wr, amt=16)
        tk_dnA = stream_piece(
            lambda e_i=e_i: dnAT[e_i].rearrange("(k p) r -> p k r", p=128),
            lambda: dnA_s[:, :],
            extra_cv_waits=guB_guard, rows=128, ncols=128, a=8)

        def emit_qb_pieces(qb, waits_per_pi):
            for pi_ in range(4):
                j = pi_ % 2
                src_col = (1024 if pi_ >= 2 else 0) + 256 * qb + 128 * j
                q2 = (256 if pi_ >= 2 else 0) + 128 * j
                for kh in range(2):
                    stream_piece(
                        lambda e_i=e_i, src_col=src_col, kh=kh: gupw[e_i].rearrange(
                            "(f p) c -> p f c", p=128)[:, kh * 8:(kh + 1) * 8, src_col:src_col + 128],
                        lambda q2=q2, kh=kh: big_d[:, 0:8192].rearrange("p (k c) -> p k c", k=NF)[
                            :, kh * 8:(kh + 1) * 8, q2:q2 + 128],
                        extra_cv_waits=waits_per_pi(pi_), rows=128, ncols=1024, a=8)
            return stream_sync()

        def emit_dn_pieces(lo, hi):
            for piece in range(lo, hi):
                stream_piece(
                    lambda e_i=e_i, piece=piece: dnw[e_i].rearrange("(k p) c -> p k c", p=128)[
                        :, :, piece * 128:(piece + 1) * 128],
                    lambda piece=piece: dwn_b[:].rearrange("p (k c) -> p k c", k=8)[
                        :, :, piece * 128:(piece + 1) * 128],
                    extra_cv_waits=(dwn_guard if (e_i == 0 and piece < 2) else
                                    ([last_lora] if (piece < 2 and last_lora) else [])),
                    rows=128, ncols=1024, a=8)
            return stream_sync()

        next_gup_sync = emit_qb_pieces(
            0, lambda pi_: (gup_guard if e_i == 0 else []) + ([gup_mm_last] if gup_mm_last else []))

        tk_dwn_sync = None
        gg_h1 = {}
        gh_h1 = {}
        for qb in range(4):
            tk_gup_sync = next_gup_sync

            for half in range(2):
                hwsl = slice(half * 512, half * 512 + 512)
                trm = mm_group(REPS, slice(0, 512), [(c_orb[0:1, :], lo16[0:1, 3072 + half * 512:3072 + half * 512 + 512])],
                               [tk_wrl], inc=sT)
                last_wrep = trm
                glast = [gup_mm_last] if gup_mm_last else []
                trw = b.emit("vector", lambda e: e.tensor_copy(rep_s[:, 0:512], REPS[:, 0:512]),
                             waits=[trm] + glast, inc=sV)
                bank_done(REPS, trw)
                for mpl in range(2):
                    kt_h = qb * 2 + mpl
                    glast = [gup_mm_last] if gup_mm_last else []
                    gg = mm_group(PROJ, slice(0, 512),
                                  [(GUPW(k, mpl * 128), big_b[:, k * TL + half * 512: k * TL + half * 512 + 512])
                                   for k in range(NF)] +
                                  [(lo16[0:16, (256 * qb + mpl * 128):(256 * qb + (mpl + 1) * 128)],
                                    lo16[0:16, 2048 + half * 512: 2048 + half * 512 + 512])],
                                  tk_gup_sync + [tk_h2] + tk_guB2 + [tk_Te], inc=sT)
                    tsg = b.emit("scalar", lambda e: e.activation(rep1[:, :], PROJ[:, 0:512], AF.Silu),
                                 waits=[gg] + glast, inc=sS)
                    bank_done(PROJ, tsg)
                    gh = mm_group(pb[4], slice(0, 512),
                                  [(GUPW(k, 256 + mpl * 128), big_b[:, k * TL + half * 512: k * TL + half * 512 + 512])
                                   for k in range(NF)] +
                                  [(lo16[0:16, (1024 + 256 * qb + mpl * 128):(1024 + 256 * qb + (mpl + 1) * 128)],
                                    lo16[0:16, 2048 + half * 512: 2048 + half * 512 + 512])],
                                  tk_gup_sync + [tk_h2] + tk_guB2 + [tk_Te], inc=sT)
                    thw = b.emit("vector", lambda e: e.tensor_tensor(rep2[:, :], pb[4][:, 0:512],
                                                                     rep_s[:, 0:512], ALU.mult),
                                 waits=[gh, trw] + glast, inc=sV)
                    bank_done(pb[4], thw)
                    w10 = [thw, tsg] + (hc_guard if (e_i == 0 and qb == 0) else [])
                    gup_mm_last = b.emit("vector", lambda e, kt_h=kt_h, half=half: e.tensor_tensor(
                        HC(kt_h, half), rep1[:, :], rep2[:, :], ALU.mult), waits=w10, inc=sV)
                    if half == 1:
                        gg_h1[mpl] = gg
                        gh_h1[mpl] = gh

            if qb == 0:
                tk_dwn_sync = emit_dn_pieces(0, 4)
            if qb < 3:
                next_gup_sync = emit_qb_pieces(
                    qb + 1, lambda pi_: [(gg_h1 if pi_ < 2 else gh_h1)[pi_ % 2]])
            if qb == 1:
                tk_dwn_sync = emit_dn_pieces(4, 10)
            if qb == 2:
                tk_dwn_sync = emit_dn_pieces(10, 16)
        tk_hcap = gup_mm_last

        dn_last = None
        for half in range(2):
            grp = mm_group(ROWS, (slice(0, 16), slice(0, 512)),
                           [(dnA_s[:, k * 16:(k + 1) * 16], HC(k, half)) for k in range(8)],
                           [tk_dnA, tk_hcap], inc=sT)
            tk_T2 = b.emit("scalar", lambda e: e.mul(lo16[32:48, 2048:2560], ROWS[0:16, 0:512], 2.0),
                           waits=[grp], inc=sS)
            bank_done(ROWS, tk_T2)
            for m_ in range(NF):
                DB = pb[5] if m_ % 2 == 0 else pb[6]
                grp = mm_group(DB, slice(0, 512),
                               [(DNW(k, m_ * 128), HC(k, half)) for k in range(8)] +
                               [(lo16[32:48, m_ * 128:(m_ + 1) * 128], lo16[32:48, 2048:2560])],
                               tk_dwn_sync + [tk_hcap, tk_T2] + tk_dnB2, inc=sT)
                last_lora = grp
                tev = b.emit("vector", lambda e, m_=m_, half=half, DB=DB: e.scalar_tensor_tensor(
                    big_a[:, m_ * TL + half * 512: m_ * TL + half * 512 + 512], DB[:, 0:512],
                    mvec[:, 80 + m_:81 + m_], big_a[:, m_ * TL + half * 512: m_ * TL + half * 512 + 512],
                    ALU.mult, ALU.add), waits=[grp, tk_gb], inc=sV)
                bank_done(DB, tev)
                dn_last = tev
                if e_i == E - 1:
                    last_out = b.emit("scalar", lambda e, m_=m_, half=half: e.dma_start(
                        out=outT[m_ * 128:(m_ + 1) * 128, half * 512: half * 512 + 512],
                        in_=big_a[:, m_ * TL + half * 512: m_ * TL + half * 512 + 512]),
                        waits=[tev], inc=sd_out, amt=16)
        guB_guard = [gup_mm_last]
        dnB_guard = [last_lora]
        Te_guard = [gup_mm_last]
        wrow_guard = [last_wrep]

    emit_done([])
    return finish(nc, ctx, b), dbg


def finish(nc, ctx, b):
    with ctx:
        with nc.Block() as block:
            @block.sync
            def _(eng):
                b.run("sync", eng)

            @block.scalar
            def _(eng):
                b.run("scalar", eng)

            @block.vector
            def _(eng):
                b.run("vector", eng)

            @block.tensor
            def _(eng):
                b.run("tensor", eng)

            @block.gpsimd
            def _(eng):
                b.run("gpsimd", eng)
    return nc


def make_consts():
    ii = np.arange(128)
    ident = np.eye(128, dtype=np.float32)
    cst = np.zeros((128, 16), np.float32)
    for k, v in enumerate([0.0, 2.0, -1.0, 0.25, 1.0, 1e-5, 1e-12, QKS,
                           1.0 / D, 1.0 / DH, -1e30]):
        cst[:, k] = v
    return {
        "cstc": cst,
        "identf": ident,
        "identb": ident.astype(ml_dtypes.bfloat16),
        "ones_cb": np.ones((128, 1), ml_dtypes.bfloat16),
        "ones_rb": np.ones((1, 128), ml_dtypes.bfloat16),
        "ones_rf": np.ones((1, 128), np.float32),
        "mask_sl": (ii[None, :] < ii[:, None]).astype(ml_dtypes.bfloat16),
        "mask_su": (ii[None, :] > ii[:, None]).astype(ml_dtypes.bfloat16),
        "mask_ul": (ii[None, :] >= ii[:, None]).astype(ml_dtypes.bfloat16),
    }


def make_inmaps(inputs):
    f32 = np.float32
    x = np.ascontiguousarray(np.asarray(inputs["L0_attn_hidden_states"], f32)[0])
    c = np.asarray(inputs["L0_c"], f32)
    wmod_h = np.asarray(inputs["L0_wmod"], f32)
    Wq = np.asarray(inputs["L0_attn_Wq"], f32)
    Wk = np.asarray(inputs["L0_attn_Wk"], f32)
    Wv = np.asarray(inputs["L0_attn_Wv"], f32)
    Wb = np.ascontiguousarray(np.asarray(inputs["L0_attn_Wb"], f32))
    onw_h = np.ascontiguousarray(np.asarray(inputs["L0_attn_o_norm_weight"], f32).reshape(128, 1))
    Wo = np.ascontiguousarray(np.asarray(inputs["L0_attn_Wo"], f32))
    gw = np.asarray(inputs["L0_ffn_gate_weight"], f32)
    gA_h = np.ascontiguousarray(np.asarray(inputs["L0_ffn_gate_lora_A"], f32))
    gB_h = np.asarray(inputs["L0_ffn_gate_lora_B"], f32)
    gup_h = np.ascontiguousarray(np.asarray(inputs["L0_ffn_gate_up_proj"], f32))
    dwn_h = np.ascontiguousarray(np.asarray(inputs["L0_ffn_down_proj"], f32))
    guA_h = np.asarray(inputs["L0_ffn_gup_lora_A"], f32)
    guB_h = np.asarray(inputs["L0_ffn_gup_lora_B"], f32)
    dnA_h = np.asarray(inputs["L0_ffn_down_lora_A"], f32)
    dnB_h = np.asarray(inputs["L0_ffn_down_lora_B"], f32)
    consts = make_consts()
    guAT_all_h = np.ascontiguousarray(np.concatenate([guA_h[e] for e in range(E)], axis=0).T)
    shared = {
        "cT": np.ascontiguousarray(c.reshape(16, 128).T),
        "wb": Wb,
        "onw": onw_h,
        "wo": Wo,
        "gwT": np.ascontiguousarray(gw.T),
        "gA": gA_h,
        "gBT": np.ascontiguousarray(gB_h.T),
        "gupw": gup_h,
        "dnw": dwn_h,
        "guAT_all": guAT_all_h,
        "guBT": np.ascontiguousarray(guB_h.transpose(0, 2, 1)),
        "dnAT": np.ascontiguousarray(dnA_h.transpose(0, 2, 1)),
        "dnBT": np.ascontiguousarray(dnB_h.transpose(0, 2, 1)),
    }
    shared.update(consts)
    in_maps = []
    for i in range(NC):
        m = dict(shared)
        m["xT"] = np.ascontiguousarray(x[i * TL:(i + 1) * TL, :].T)
        m["wmod"] = np.ascontiguousarray(wmod_h[:, i * (6 * D // NC):(i + 1) * (6 * D // NC)])
        m["wq"] = np.ascontiguousarray(Wq[:, i * 256:(i + 1) * 256])
        m["wk"] = np.ascontiguousarray(Wk[:, i * 256:(i + 1) * 256])
        m["wv"] = np.ascontiguousarray(Wv[:, i * 256:(i + 1) * 256])
        in_maps.append(m)
    return in_maps


_NC_CACHE = {}


def run_stage(inputs, stage):
    if stage not in _NC_CACHE:
        _NC_CACHE[stage] = build_nc(stage)
    nc, dbg = _NC_CACHE[stage]
    in_maps = make_inmaps(inputs)
    return run_bass_kernel_spmd(nc, in_maps, core_ids=list(range(NC)))


def kernel(**inputs):
    res = run_stage(inputs, "full")
    out = np.empty((1, T, D), np.float32)
    for i in range(NC):
        out[0, i * TL:(i + 1) * TL, :] = res.results[i]["outT"].T
    return out


# version 05 (final)
# speedup vs baseline: 1.0646x; 1.0432x over previous
import os

os.environ.setdefault("TRN_ALLOW_ICI_ALL_TO_ALL", "1")

import numpy as np
import ml_dtypes
from contextlib import ExitStack

import concourse.bass as bass
import concourse.mybir as mybir
from concourse.bass_utils import run_bass_kernel_spmd

F32 = mybir.dt.float32
BF16 = mybir.dt.bfloat16
AF = mybir.ActivationFunctionType
ALU = mybir.AluOpType
AX = mybir.AxisListType

NC = 8
T = 8192
TL = T // NC
D = 2048
NF = 16
H = 16
DH = 128
E = 8
II = 1024
R = 16
CH = 128
SC = 256
NSC = T // SC
CPS = SC // CH
QKS = 0.08838834764831845

ENGINES = ("sync", "scalar", "vector", "tensor", "gpsimd")


class Sem:
    __slots__ = ("h", "n", "name")

    def __init__(self, h, name):
        self.h = h
        self.n = 0
        self.name = name


class Builder:
    def __init__(self, nc, ctx):
        self.nc = nc
        self.ctx = ctx
        self.q = {e: [] for e in ENGINES}

    def sem(self, name):
        return Sem(self.ctx.enter_context(self.nc.semaphore(name)), name)

    def emit(self, eng, fn, waits=(), inc=None, amt=1):
        best = {}
        for t in waits:
            if t is None:
                continue
            s, v = t
            assert v <= s.n, f"wait {s.name}>={v} but only {s.n} emitted"
            if best.get(s.name, (None, -1))[1] < v:
                best[s.name] = (s, v)
        tok = None
        if inc is not None:
            inc.n += amt
            tok = (inc, inc.n)
        self.q[eng].append((fn, list(best.values()), (inc, amt) if inc else None))
        return tok

    def run(self, eng_name, eng):
        n = 0
        for fn, waits, inc in self.q[eng_name]:
            for s, v in waits:
                eng.wait_ge(s.h, v)
            inst = fn(eng)
            if inc is not None:
                s, amt = inc
                inst.then_inc(s.h, amt)
            n += 1
        if n == 0:
            eng.nop()


def build_nc(stage="full"):
    nc = bass.Bass("TRN2", target_bir_lowering=False, debug=False)
    ctx = ExitStack()
    b = Builder(nc, ctx)

    def din(name, shape, dt=F32):
        return nc.dram_tensor(name, shape, dt, kind="ExternalInput").ap()

    xT = din("xT", [D, TL])
    cT_d = din("cT", [128, 16])
    wmod = din("wmod", [D, 6 * D // NC])
    wq = din("wq", [D, 2 * DH])
    wk = din("wk", [D, 2 * DH])
    wv = din("wv", [D, 2 * DH])
    wb = din("wb", [D, H])
    onw = din("onw", [128, 1])
    wo = din("wo", [D, D])
    gwT = din("gwT", [D, E])
    gA = din("gA", [R, D])
    gBT = din("gBT", [R, E])
    gupw = din("gupw", [E, D, 2 * II])
    dnw = din("dnw", [E, II, D])
    guAT_all = din("guAT_all", [D, E * R])
    guBT = din("guBT", [E, R, 2 * II])
    dnAT = din("dnAT", [E, II, R])
    dnBT = din("dnBT", [E, R, D])
    identf = din("identf", [128, 128])
    identb = din("identb", [128, 128], BF16)
    ones_cb = din("ones_cb", [128, 1], BF16)
    ones_rb = din("ones_rb", [1, 128], BF16)
    ones_rf = din("ones_rf", [1, 128])
    mask_sl = din("mask_sl", [128, 128], BF16)
    mask_su = din("mask_su", [128, 128], BF16)
    mask_ul = din("mask_ul", [128, 128], BF16)
    cstc = din("cstc", [128, 16])

    outT = nc.dram_tensor("outT", [D, TL], F32, kind="ExternalOutput").ap()
    dbg = {}
    if stage == "h1":
        dbg["h1T_dbg"] = nc.dram_tensor("h1T_dbg", [D, TL], F32, kind="ExternalOutput").ap()
        dbg["bet_dbg"] = nc.dram_tensor("bet_dbg", [H, TL], F32, kind="ExternalOutput").ap()
        dbg["mod_dbg"] = nc.dram_tensor("mod_dbg", [128, 128], F32, kind="ExternalOutput").ap()

    SCAN_DBG = False
    if stage == "scan":
        dbg["o_dbg"] = nc.dram_tensor("o_dbg", [2 * DH, T], F32, kind="ExternalOutput").ap()
        if SCAN_DBG:
            dbg["scr_dbg"] = nc.dram_tensor("scr_dbg", [128, 9088], BF16, kind="ExternalOutput").ap()
            dbg["sm_dbg"] = nc.dram_tensor("sm_dbg", [128, 2 * DH], F32, kind="ExternalOutput").ap()
            dbg["bc_dbg"] = nc.dram_tensor("bc_dbg", [128, 4], F32, kind="ExternalOutput").ap()
    if stage == "attn":
        dbg["x2_dbg"] = nc.dram_tensor("x2_dbg", [D, TL], F32, kind="ExternalOutput").ap()
        dbg["h2_dbg"] = nc.dram_tensor("h2_dbg", [D, TL], F32, kind="ExternalOutput").ap()
        dbg["w_dbg"] = nc.dram_tensor("w_dbg", [E, TL], F32, kind="ExternalOutput").ap()

    mod_loc = nc.dram_tensor("mod_loc", [1, 6 * D // NC], F32).ap()
    mod_all = nc.dram_tensor("mod_all", [NC, 6 * D // NC], F32).ap()
    h1_loc = nc.dram_tensor("h1_loc", [D, TL], BF16).ap()
    h1_all = nc.dram_tensor("h1_all", [NC * D, TL], BF16).ap()
    bet_loc = nc.dram_tensor("bet_loc", [H, TL], F32).ap()
    bet_all = nc.dram_tensor("bet_all", [H, TL], F32).ap()
    o_loc = nc.dram_tensor("o_loc", [2 * DH * NC, TL], BF16).ap()
    o_all = nc.dram_tensor("o_all", [2 * DH * NC, TL], BF16).ap()
    tal_d = nc.dram_tensor("tal_d", [E * R, TL], BF16).ap()

    def sb(name, shape, dt=F32):
        return ctx.enter_context(nc.sbuf_tensor(name, shape, dt))

    big_a = sb("big_a", [128, NF * TL], F32)
    big_b = sb("big_b", [128, NF * TL], BF16)
    big_c = sb("big_c", [128, 8192], BF16)
    big_d = sb("big_d", [128, 10752], BF16)
    dwn_b = sb("dwn_b", [128, 8 * D], BF16)
    stage_f = sb("stage_f", [128, 2048], F32)
    rep1 = sb("rep1", [128, 512], F32)
    rep2 = sb("rep2", [128, 512], F32)
    rep_s = sb("rep_s", [128, 1024], F32)
    rh = sb("rh", [128, 1536], F32)
    rowb = sb("rowb", [1, 1024], BF16)
    c_idf = sb("c_idf", [128, 128], F32)
    c_idb = sb("c_idb", [128, 128], BF16)
    c_ocb = sb("c_ocb", [128, 1], BF16)
    c_orb = sb("c_orb", [1, 128], BF16)
    c_orf = sb("c_orf", [1, 128], F32)
    c_msl = sb("c_msl", [128, 128], BF16)
    c_msu = sb("c_msu", [128, 128], BF16)
    c_mul = sb("c_mul", [128, 128], BF16)
    c_onw = sb("c_onw", [128, 1], F32)
    cT_s = sb("cT_s", [128, 16], F32)
    cS = sb("cS", [128, 16], F32)
    mvec = sb("mvec", [128, 128], F32)
    bcols = sb("bcols", [128, 4], F32)
    
    S_m = sb("S_m", [128, 2 * DH], F32)
    S_b = sb("S_b", [128, 2 * DH], BF16)
    lo16 = sb("lo16", [128, 4096], BF16)
    lo16f = sb("lo16f", [64, 512], F32)
    weff = sb("weff", [128, NF * E], BF16)
    gw_b = sb("gw_b", [128, NF * E], BF16)
    wtok = sb("wtok", [128, 8 * E], F32)
    wrowb = sb("wrowb", [E, TL], BF16)
    dnA_s = sb("dnA_s", [128, 8 * R], BF16)

    R_MOD = lambda: rh[96:97, 0:1536]
    S1 = lambda: rep_s[:, 0:512]
    S2 = lambda: rep_s[:, 512:1024]
    L2A = lambda: rep1[:, 0:512]
    L2B = lambda: rep2[:, 0:512]
    ON1 = lambda hs: rep_s[:, 512 + hs * 128: 640 + hs * 128]
    ON2 = lambda hs: rep_s[:, 768 + hs * 128: 896 + hs * 128]

    LNB1 = lambda par: big_c[:, par * 2048: par * 2048 + 1024]
    LNB2 = lambda par: big_c[:, par * 2048 + 1024: par * 2048 + 2048]
    LNF = lambda: rep_s[:, 0:512]
    QBP = lambda pp, idx: slice(pp * 3072 + idx * 512, pp * 3072 + (idx + 1) * 512)
    SCR = lambda hs, slot: slice(6144 + hs * 2048 + slot * 128, 6144 + hs * 2048 + (slot + 1) * 128)
    SQ2 = slice(10240, 10752)
    (SL_A, SL_A2, SL_A4, SL_A8, SL_A16, SL_A32, SL_A64, SL_B, SL_MT, SL_W0,
     SL_U, SL_TT, SL_KT, SL_KB, SL_VB, SL_X) = range(16)
    oT_col = lambda hs, c: slice(hs * 8192 + c * 128, hs * 8192 + (c + 1) * 128)
    bm_sl = lambda hs: slice(256 + hs * 128, 384 + hs * 128)
    rb_beta = lambda hs: rowb[0:1, 512 + hs * 256: 640 + hs * 256]
    rb_o = lambda hs: rowb[0:1, 640 + hs * 256: 768 + hs * 256]

    cst = sb("cst", [128, 16], F32)
    C_ZERO = cst[:, 0:1]
    C_TWO = cst[:, 1:2]
    C_NEG1 = cst[:, 2:3]
    C_QTR = cst[:, 3:4]
    C_ONE = cst[:, 4:5]
    C_EPS = cst[:, 5:6]
    C_TINY = cst[:, 6:7]
    C_QKS = cst[:, 7:8]
    C_INVD = cst[:, 8:9]
    C_INVDH = cst[:, 9:10]
    C_NEG30 = cst[:, 10:11]
    assert nc.sbuf_bytes_remaining >= 0

    def pst(name, shape, dt=F32):
        return ctx.enter_context(nc.psum_tensor(name, shape, dt))

    pb = [pst(f"pb{i}", [128, 512]) for i in range(7)]
    ptb = pst("ptb", [128, 1024], BF16)
    PROJ, ROWS, REPS = pb[0], pb[1], pb[2]
    SCB_ = [(pb[3], pb[4]), (pb[5], pb[6])]

    sT = b.sem("sT")
    sV = b.sem("sV")
    sS = b.sem("sS")
    cc = b.sem("cc")
    sd_c = b.sem("sd_c")
    sd_x = b.sem("sd_x")
    sd_w = [b.sem("sd_w0"), b.sem("sd_w1")]
    sd_h1 = b.sem("sd_h1")
    sd_bet = b.sem("sd_bet")
    sd_mvec = b.sem("sd_mvec")
    sd_hin = [b.sem("sd_hin0"), b.sem("sd_hin1")]
    sd_bcol = [b.sem("sd_bcol0"), b.sem("sd_bcol1")]
    sd_o = b.sem("sd_o")
    sd_oall = b.sem("sd_oall")
    sd_tal = b.sem("sd_tal")
    sd_te = b.sem("sd_te")
    sd_h1b = b.sem("sd_h1b")
    sd_wr = b.sem("sd_wr")
    sd_d = [b.sem("sd_d0"), b.sem("sd_d1")]
    sd_out = b.sem("sd_out")

    bank_tok = {}

    def mm_group(bank, reg, pairs, waits, inc=None, key=None):
        n = len(pairs)
        tok = None
        bk = key if key is not None else id(bank)
        for k, (lh, rhs_) in enumerate(pairs):
            w = []
            if k == 0:
                w = list(waits)
                if bank_tok.get(bk) is not None:
                    w.append(bank_tok[bk])
            ap = bank[reg[0], reg[1]] if isinstance(reg, tuple) else bank[:, reg]
            tok = b.emit("tensor", lambda e, lh=lh, rhs_=rhs_, k=k, ap=ap: e.matmul(
                ap, lh, rhs_, start=(k == 0), stop=(k == n - 1)),
                waits=w, inc=inc if k == n - 1 else None)
        return tok

    def bank_done(bank, tok, key=None):
        bank_tok[key if key is not None else id(bank)] = tok

    def tr(bank, reg, in_, ident, waits, inc=None, key=None):
        w = list(waits)
        bk = key if key is not None else id(bank)
        if bank_tok.get(bk) is not None:
            w.append(bank_tok[bk])
        ap = bank[reg[0], reg[1]] if isinstance(reg, tuple) else bank[:, reg]
        return b.emit("tensor", lambda e, ap=ap, in_=in_, ident=ident: e.transpose(ap, in_, ident), waits=w, inc=inc)

    stage_free = [None, None]
    stream_i = [0]
    stream_last = {"vector": None, "scalar": None}

    def stream_sync():
        return [stream_last["vector"], stream_last["scalar"]]

    def stream_piece(src_fn, dst_fn, extra_load_waits=(), extra_cv_waits=(), rows=128, ncols=1024, a=None):
        i = stream_i[0]
        stream_i[0] += 1
        sl = i % 2
        so = sl * 1024
        if a is None:
            out_ap = stage_f[0:rows, so:so + ncols]
        else:
            out_ap = stage_f[0:rows, so:so + ncols].rearrange("p (a c) -> p a c", a=a)
        tload = b.emit("sync", lambda e, out_ap=out_ap: e.dma_start(out=out_ap, in_=src_fn()),
                       waits=[stage_free[sl]] + list(extra_load_waits), inc=sd_w[sl], amt=16)
        eng = "vector" if i % 2 == 0 else "scalar"
        src_sb = stage_f[0:rows, so:so + ncols]
        if eng == "vector":
            fn = lambda e, src_sb=src_sb: e.tensor_copy(dst_fn(), src_sb)
        else:
            fn = lambda e, src_sb=src_sb: e.activation(dst_fn(), src_sb, AF.Copy)
        cv = b.emit(eng, fn, waits=[tload] + list(extra_cv_waits), inc=sV if eng == "vector" else sS)
        stage_free[sl] = cv
        stream_last[eng] = cv
        return cv

    tk_c = None
    for out, src in [(c_idf[:], identf), (c_idb[:], identb), (c_ocb[:], ones_cb),
                     (c_orb[:], ones_rb), (c_orf[:], ones_rf), (c_msl[:], mask_sl),
                     (c_msu[:], mask_su), (c_mul[:], mask_ul), (c_onw[:], onw),
                     (cT_s[:], cT_d), (cst[:], cstc)]:
        tk_c = b.emit("sync", lambda e, o=out, s=src: e.dma_start(out=o, in_=s[:, :]), inc=sd_c, amt=16)
    tk_x = b.emit("sync", lambda e: e.dma_start(
        out=big_a[:].rearrange("p (f t) -> p f t", f=NF),
        in_=xT.rearrange("(f p) t -> p f t", p=128)), inc=sd_x, amt=16)
    tk_cS = b.emit("scalar", lambda e: e.activation(cS[:], cT_s[:], AF.Silu), waits=[tk_c], inc=sS)

    MODB = [pb[3], pb[4], pb[5]]
    tk_modmm = None
    for f in range(16):
        for n in range(3):
            i = stream_i[0]
            stream_i[0] += 1
            sl = i % 2
            so = sl * 1024
            tload = b.emit("sync", lambda e, f=f, n=n, so=so: e.dma_start(
                out=stage_f[:, so:so + 512],
                in_=wmod[128 * f:128 * (f + 1), 512 * n:512 * (n + 1)]),
                waits=[stage_free[sl]], inc=sd_w[sl], amt=16)
            w = [tload]
            if f == 0:
                w += [tk_cS, bank_tok.get(id(MODB[n]))]
            tk_modmm = b.emit("tensor", lambda e, f=f, n=n, so=so: e.matmul(
                MODB[n][0:1, 0:512], cS[:, f:f + 1], stage_f[:, so:so + 512],
                start=(f == 0), stop=(f == 15)), waits=w, inc=sT)
            stage_free[sl] = tk_modmm
    tk_mode = None
    for n in range(3):
        tk_mode = b.emit("vector", lambda e, n=n: e.tensor_copy(
            R_MOD()[0:1, 512 * n:512 * (n + 1)], MODB[n][0:1, 0:512]), waits=[tk_modmm], inc=sV)
        bank_done(MODB[n], tk_mode)
    tk_modst = b.emit("scalar", lambda e: e.dma_start(out=mod_loc[:, :], in_=R_MOD()),
                      waits=[tk_mode], inc=sd_h1, amt=16)
    tk_ag1 = b.emit("gpsimd", lambda e: e.collective_compute(
        "AllGather", ALU.bypass, replica_groups=[list(range(NC))],
        ins=[mod_loc], outs=[mod_all]), waits=[tk_modst], inc=cc)
    tk_mvl = b.emit("sync", lambda e: e.dma_start(
        out=rep_s[0:96, 0:128],
        in_=mod_all.rearrange("a b -> (a b)").rearrange("(q r) -> q r", r=128)),
        waits=[tk_ag1], inc=sd_mvec, amt=16)
    tk_mvt = tr(REPS, slice(0, 96), rep_s[0:96, 0:128], c_idf[0:96, 0:96], [tk_mvl, tk_c], inc=sT)
    tk_mv = b.emit("vector", lambda e: e.tensor_copy(mvec[:, 0:96], REPS[:, 0:96]),
                   waits=[tk_mvt], inc=sV)
    bank_done(REPS, tk_mv)

    def clamp_emit(v, prev):
        sl = slice(v * 16, v * 16 + 16)
        t1 = b.emit("vector", lambda e: e.tensor_scalar(rep_s[:, 128:144], mvec[:, sl], C_ZERO, None, ALU.is_ge),
                    waits=[tk_mv, prev, tk_c], inc=sV)
        t2 = b.emit("vector", lambda e: e.tensor_scalar(rep_s[:, 160:176], rep_s[:, 128:144], C_TWO, C_NEG1, ALU.mult, ALU.add),
                    waits=[t1], inc=sV)
        t3 = b.emit("scalar", lambda e: e.activation(rep_s[:, 144:160], mvec[:, sl], AF.Abs), waits=[tk_mv, prev], inc=sS)
        t4 = b.emit("vector", lambda e: e.tensor_scalar(rep_s[:, 176:192], rep_s[:, 144:160], C_QTR, None, ALU.max),
                    waits=[t3], inc=sV)
        return b.emit("vector", lambda e: e.tensor_tensor(mvec[:, sl], rep_s[:, 160:176], rep_s[:, 176:192], ALU.mult),
                      waits=[t2, t4], inc=sV)

    tk_ga = clamp_emit(2, None)
    tk_gb = clamp_emit(5, tk_ga)
    tk_ca = b.emit("vector", lambda e: e.tensor_scalar(mvec[:, 96:112], mvec[:, 16:32], C_ONE, None, ALU.add),
                   waits=[tk_mv, tk_c], inc=sV)
    tk_cb = b.emit("vector", lambda e: e.tensor_scalar(mvec[:, 112:128], mvec[:, 64:80], C_ONE, None, ALU.add),
                   waits=[tk_mv, tk_c], inc=sV)

    def emit_ln(src, dst, cmod, smod, wait_stats, wait_apply, rowbanks):
        bsum = [rowbanks[0], rowbanks[1]]
        bsq = [rowbanks[2], rowbanks[3]]
        tk_mm = None
        mm_toks = [None] * NF
        for f in range(NF):
            par = f % 2
            w_ov = [mm_toks[f - 2]] if f >= 2 else []
            tb = b.emit("vector", lambda e, f=f, par=par: e.tensor_copy(LNB2(par), src(f)),
                        waits=wait_stats(f) + w_ov, inc=sV)
            ts_ = b.emit("scalar", lambda e, f=f, par=par: e.activation(LNB1(par), src(f), AF.Square),
                         waits=wait_stats(f) + w_ov, inc=sS)
            for half in range(2):
                sl = slice(half * 512, half * 512 + 512)
                b.emit("tensor", lambda e, half=half, sl=sl, f=f, par=par: e.matmul(
                    bsum[half][0:1, 0:512], c_ocb[:, 0:1], LNB2(par)[:, sl],
                    start=(f == 0), stop=(f == NF - 1)),
                    waits=[tb, bank_tok.get(id(bsum[half]))] if f == 0 else [tb],
                )
                tk_mm = b.emit("tensor", lambda e, half=half, sl=sl, f=f, par=par: e.matmul(
                    bsq[half][0:1, 0:512], c_ocb[:, 0:1], LNB1(par)[:, sl],
                    start=(f == 0), stop=(f == NF - 1)),
                    waits=[ts_, bank_tok.get(id(bsq[half]))] if f == 0 else [ts_],
                    inc=sT)
            mm_toks[f] = tk_mm
        tk_apply = None
        reps_tok = None
        for half in range(2):
            sl = slice(half * 512, half * 512 + 512)
            c1 = b.emit("vector", lambda e, half=half: e.tensor_copy(
                rowb[0:1, 0:512], bsum[half][0:1, 0:512]), waits=[tk_mm, reps_tok], inc=sV)
            c2 = b.emit("vector", lambda e, half=half: e.tensor_copy(
                rowb[0:1, 512:1024], bsq[half][0:1, 0:512]), waits=[tk_mm, reps_tok], inc=sV)
            if half == 1:
                for bk in rowbanks:
                    bank_done(bk, c2)
            t1 = mm_group(REPS, slice(0, 512), [(c_orb[0:1, :], rowb[0:1, 0:512])], [c1, reps_tok], inc=sT)
            em = b.emit("vector", lambda e: e.tensor_scalar(rep2[:, :], REPS[:, 0:512], C_INVD, None, ALU.mult),
                        waits=[t1, tk_c], inc=sV)
            bank_done(REPS, em)
            t2 = mm_group(REPS, slice(0, 512), [(c_orb[0:1, :], rowb[0:1, 512:1024])], [c2], inc=sT)
            reps_tok = t2
            tms = b.emit("vector", lambda e: e.tensor_tensor(S1(), rep2[:, :], rep2[:, :], ALU.mult),
                         waits=[em], inc=sV)
            ev = b.emit("vector", lambda e: e.scalar_tensor_tensor(S2(), REPS[:, 0:512], C_INVD, S1(),
                                                                   ALU.mult, ALU.subtract), waits=[t2, tms], inc=sV)
            bank_done(REPS, ev)
            tv = b.emit("vector", lambda e: e.tensor_scalar(S1(), S2(), C_EPS, None, ALU.add),
                        waits=[ev], inc=sV)
            tq = b.emit("scalar", lambda e: e.activation(S2(), S1(), AF.Sqrt), waits=[tv], inc=sS)
            ti = b.emit("vector", lambda e: e.reciprocal(rep1[:, :], S2()), waits=[tq], inc=sV)
            tmi = b.emit("vector", lambda e: e.tensor_tensor(S1(), rep2[:, :], rep1[:, :], ALU.mult),
                         waits=[ti], inc=sV)
            for f in range(NF):
                ta1 = b.emit("vector", lambda e, f=f, sl=sl: e.tensor_tensor(
                    S2(), src(f)[:, sl], rep1[:, :], ALU.mult), waits=[tmi] + list(wait_apply), inc=sV)
                ta2 = b.emit("vector", lambda e: e.tensor_sub(rep2[:, :], S2(), S1()), waits=[ta1], inc=sV)
                tk_apply = b.emit("vector", lambda e, f=f, sl=sl: e.tensor_scalar(
                    dst(f)[:, sl], rep2[:, :], cmod(f), smod(f), ALU.mult, ALU.add), waits=[ta2], inc=sV)
        return tk_apply

    tk_h1 = emit_ln(
        src=lambda f: big_a[:, f * TL:(f + 1) * TL],
        dst=lambda f: big_b[:, f * TL:(f + 1) * TL],
        cmod=lambda f: mvec[:, 96 + f:97 + f],
        smod=lambda f: mvec[:, f:f + 1],
        wait_stats=lambda f: [tk_x],
        wait_apply=[tk_ca, tk_ga, tk_gb, tk_cb],
        rowbanks=(PROJ, ROWS, pb[3], pb[6]),
    )
    tk_h1st = b.emit("scalar", lambda e: e.dma_start(
        out=h1_loc.rearrange("(f p) t -> p f t", p=128),
        in_=big_b[:].rearrange("p (f t) -> p f t", f=NF)), waits=[tk_h1], inc=sd_h1b, amt=16)
    tk_ag2 = b.emit("gpsimd", lambda e: e.collective_compute(
        "AllGather", ALU.bypass, replica_groups=[list(range(NC))],
        ins=[h1_loc], outs=[h1_all]), waits=[tk_h1st], inc=cc)

    tk_wbc = None
    for hf in range(2):
        tk_wbc = stream_piece(
            lambda hf=hf: wb.rearrange("(a p) c -> p a c", p=128)[:, hf * 8:(hf + 1) * 8, :],
            lambda hf=hf: dwn_b[:, 12288 + hf * 128:12288 + (hf + 1) * 128],
            rows=128, ncols=128, a=8)
    tk_betmm = [None, None]
    for half in range(2):
        tk_betmm[half] = mm_group(
            [pb[3], pb[4]][half], (slice(0, H), slice(0, 512)),
            [(dwn_b[:, 12288 + f * 16:12288 + (f + 1) * 16],
              big_b[:, f * TL + half * 512: f * TL + half * 512 + 512]) for f in range(NF)],
            [tk_h1] + stream_sync(), inc=sT)
    tk_betst = None
    prev_betst = None
    for half in range(2):
        tk_bete = b.emit("scalar", lambda e, half=half: e.activation(
            lo16f[0:16, 0:512], [pb[3], pb[4]][half][0:H, 0:512], AF.Sigmoid),
            waits=[tk_betmm[half], prev_betst], inc=sS)
        bank_done([pb[3], pb[4]][half], tk_bete)
        tk_betst = b.emit("scalar", lambda e, half=half: e.dma_start(
            out=bet_loc[:, half * 512: half * 512 + 512], in_=lo16f[0:16, :]),
            waits=[tk_bete], inc=sd_bet, amt=16)
        prev_betst = tk_betst
    tk_a2a_b = b.emit("gpsimd", lambda e: e.collective_compute(
        "AllToAll", ALU.bypass, replica_groups=[list(range(NC))],
        ins=[bet_loc], outs=[bet_all]), waits=[tk_betst], inc=cc)

    def emit_done(extra):
        b.emit("gpsimd", lambda e: e.nop(),
               waits=[(sd_out, sd_out.n), (cc, cc.n), (sd_d[0], sd_d[0].n), (sd_d[1], sd_d[1].n),
                      (sd_h1b, sd_h1b.n), (sd_wr, sd_wr.n)] + list(extra))

    dump_prevs = [None, None]

    def dump_f32(dst_dram, src_tiles, nt, wait):
        prevs = dump_prevs
        i = 0
        for f in range(nt):
            for half in range(2):
                par = i % 2
                i += 1
                t1 = b.emit("vector", lambda e, f=f, half=half, par=par: e.tensor_copy(
                    rep_s[:, par * 512: par * 512 + 512], src_tiles(f)[:, half * 512:half * 512 + 512]),
                    waits=[wait, prevs[par]], inc=sV)
                prevs[par] = b.emit("scalar", lambda e, f=f, half=half, par=par: e.dma_start(
                    out=dst_dram[f * 128:(f + 1) * 128, half * 512:half * 512 + 512],
                    in_=rep_s[:, par * 512: par * 512 + 512]), waits=[t1], inc=sd_d[par], amt=16)

    def zero_out():
        t9 = b.emit("vector", lambda e: e.memset(LNF(), 0.0),
                    waits=[(sd_out, sd_out.n), (sd_d[0], sd_d[0].n), (sd_d[1], sd_d[1].n)], inc=sV)
        for m_ in range(NF):
            for half in range(2):
                b.emit("scalar", lambda e, m_=m_, half=half: e.dma_start(
                    out=outT[m_ * 128:(m_ + 1) * 128, half * 512:half * 512 + 512],
                    in_=LNF()), waits=[t9], inc=sd_out, amt=16)

    if stage == "h1":
        dump_f32(dbg["h1T_dbg"], lambda f: big_b[:, f * TL:(f + 1) * TL], NF, tk_h1)
        tbl = b.emit("sync", lambda e: e.dma_start(out=lo16f[0:16, :], in_=bet_loc[:, 0:512]),
                     waits=[tk_betst], inc=sd_out, amt=16)
        b.emit("sync", lambda e: e.dma_start(out=dbg["bet_dbg"][:, 0:512], in_=lo16f[0:16, :]),
               waits=[tbl], inc=sd_out, amt=16)
        tbl2 = b.emit("sync", lambda e: e.dma_start(out=lo16f[0:16, :], in_=bet_loc[:, 512:1024]),
                      waits=[(sd_out, sd_out.n)], inc=sd_out, amt=16)
        b.emit("sync", lambda e: e.dma_start(out=dbg["bet_dbg"][:, 512:1024], in_=lo16f[0:16, :]),
               waits=[tbl2], inc=sd_out, amt=16)
        b.emit("sync", lambda e: e.dma_start(out=dbg["mod_dbg"][:, :], in_=mvec[:]),
               waits=[tk_ga, tk_gb, tk_ca, tk_cb], inc=sd_out, amt=16)
        zero_out()
        emit_done([])
        return finish(nc, ctx, b), dbg

    WQKV = lambda f, proj, m: dwn_b[:, f * 768 + proj * 256 + m * 128: f * 768 + proj * 256 + (m + 1) * 128]
    tk_wqkv_all = None
    for proj, wsrc in [(0, wq), (1, wk), (2, wv)]:
        for m in range(2):
            for kh in range(2):
                tk_wqkv_all = stream_piece(
                    lambda wsrc=wsrc, m=m, kh=kh: wsrc.rearrange("(f p) c -> p f c", p=128)[
                        :, kh * 8:(kh + 1) * 8, m * 128:(m + 1) * 128],
                    lambda proj=proj, m=m, kh=kh: dwn_b[:, 0:12288].rearrange(
                        "p (f c) -> p f c", f=NF)[:, kh * 8:(kh + 1) * 8,
                                                  proj * 256 + m * 128: proj * 256 + (m + 1) * 128],
                    rows=128, ncols=1024, a=8)
    tk_wqkv_sync = stream_sync()

    tk_s0 = b.emit("vector", lambda e: e.memset(S_m[:], 0.0), inc=sV)
    tk_sb0 = b.emit("vector", lambda e: e.memset(S_b[:], 0.0), waits=[tk_s0], inc=sV)

    RB = [pb[1], pb[2]]
    st = {
        "sb": [tk_sb0, tk_sb0], "sm": [tk_s0, tk_s0],
        "bcol": [None, None], "rbb": [None, None], "rbo": [None, None],
        "oe": [None, None], "last_to": [None, None],
        "brow_c": [[tk_a2a_b], [tk_a2a_b]],
        "last_oT": None, "first_oT": True,
    }
    hin_free = None
    qkv_free = {0: [], 1: []}

    def chunk_head(pr, pp, ci, hs, ev, tk_brow):
        c = pr * 4 + ci
        tc = slice(ci * 128, (ci + 1) * 128)
        SCB, SCB2 = SCB_[hs]
        RBank = RB[hs]
        pkey = ("ptb", hs)
        pKsl = slice(hs * 256, hs * 256 + 128)
        pVsl = slice(hs * 256 + 128, hs * 256 + 256)
        Qf = big_d[:, QBP(pp, 0 + hs)][:, tc]
        Kf = big_d[:, QBP(pp, 2 + hs)][:, tc]
        Vf = big_d[:, QBP(pp, 4 + hs)][:, tc]
        bbase = 32 * hs
        brsl = lo16f[bbase:bbase + 1, ci * 128:(ci + 1) * 128]
        bcol = bcols[:, hs:hs + 1]

        tbc = mm_group(RBank, (slice(0, 128), slice(448, 449)),
                       [(brsl, cst[bbase:bbase + 1, 4:5])], [tk_brow], inc=sT)
        yield
        tbce = b.emit("vector", lambda e, hs=hs: e.tensor_copy(
            bcols[:, hs:hs + 1], RB[hs][:, 448:449]),
            waits=[tbc, st["bcol"][hs]], inc=sV)
        bank_done(RBank, tbce)
        yield
        tbrb = b.emit("vector", lambda e, hs=hs, brsl=brsl: e.tensor_copy(rb_beta(hs), brsl),
                      waits=[tk_brow, st["rbb"][hs]], inc=sV)
        yield
        tbr = mm_group(RBank, slice(0, 128), [(c_orb[0:1, :], rb_beta(hs))], [tbrb], inc=sT)
        st["rbb"][hs] = tbr
        yield
        tbm = b.emit("vector", lambda e, hs=hs: e.tensor_tensor(
            rep_s[:, bm_sl(hs)], RB[hs][:, 0:128], c_msu[:], ALU.mult), waits=[tbr], inc=sV)
        bank_done(RBank, tbm)
        yield

        tG = mm_group(SCB, slice(0, 128), [(Kf, Kf)], [ev[2 + hs]], inc=sT)
        yield
        tA = b.emit("vector", lambda e, hs=hs, bcol=bcol, SCB=SCB: e.scalar_tensor_tensor(
            big_d[:, SCR(hs, SL_A)], SCB[:, 0:128], bcol, c_msl[:], ALU.mult, ALU.mult),
            waits=[tG, tbce], inc=sV)
        yield
        tB = b.emit("vector", lambda e, hs=hs, SCB=SCB: e.tensor_tensor(
            big_d[:, SCR(hs, SL_B)], SCB[:, 0:128], rep_s[:, bm_sl(hs)], ALU.mult),
            waits=[tbm, tA], inc=sV)
        bank_done(SCB, tB)
        yield

        tKt = tr(ptb, pKsl, Kf, c_idb[:], [ev[2 + hs]], inc=sT, key=pkey)
        yield
        tKtok = b.emit("scalar", lambda e, hs=hs, pKsl=pKsl: e.activation(
            big_d[:, SCR(hs, SL_KT)], ptb[:, pKsl], AF.Copy), waits=[tKt], inc=sS)
        yield
        tKb = b.emit("vector", lambda e, hs=hs, bcol=bcol, pKsl=pKsl: e.tensor_scalar(
            big_d[:, SCR(hs, SL_KB)], ptb[:, pKsl], bcol, None, ALU.mult),
            waits=[tKt, tbce], inc=sV)
        bank_done(ptb, tKb, key=pkey)
        yield
        tVt = tr(ptb, pVsl, Vf, c_idb[:], [ev[4 + hs], tKtok], inc=sT, key=pkey)
        yield
        tVb = b.emit("vector", lambda e, hs=hs, bcol=bcol, pVsl=pVsl: e.tensor_scalar(
            big_d[:, SCR(hs, SL_VB)], ptb[:, pVsl], bcol, None, ALU.mult),
            waits=[tVt, tbce], inc=sV)
        bank_done(ptb, tVb, key=pkey)
        st["bcol"][hs] = tVb
        yield

        tX = b.emit("vector", lambda e, hs=hs: e.tensor_sub(
            big_d[:, SCR(hs, SL_MT)], c_idb[:], big_d[:, SCR(hs, SL_B)]),
            waits=[tB], inc=sV)
        yield

        aslot = {1: SL_A, 2: SL_A2, 4: SL_A4, 8: SL_A8, 16: SL_A16, 32: SL_A32, 64: SL_A64}
        atok = {1: tA}
        prevB, tokB = SL_B, tB
        bslots = [SL_X, SL_B]
        bs_i = 0
        pw = 1
        while pw < 64:
            npw = 2 * pw
            t1 = mm_group(SCB, slice(0, 128),
                          [(big_d[:, SCR(hs, prevB)], big_d[:, SCR(hs, aslot[pw])])],
                          [atok[pw], tokB], inc=sT)
            yield
            ta_n = b.emit("scalar", lambda e, hs=hs, d=aslot[npw], SCB=SCB: e.activation(
                big_d[:, SCR(hs, d)], SCB[:, 0:128], AF.Copy), waits=[t1], inc=sS)
            bank_done(SCB, ta_n)
            atok[npw] = ta_n
            yield
            if npw < 64:
                t2 = mm_group(SCB2, slice(0, 128),
                              [(big_d[:, SCR(hs, aslot[pw])], big_d[:, SCR(hs, prevB)])],
                              [atok[pw], tokB, tX], inc=sT)
                yield
                nbs = bslots[bs_i % 2]
                bs_i += 1
                tokB = b.emit("scalar", lambda e, hs=hs, nbs=nbs, SCB2=SCB2: e.activation(
                    big_d[:, SCR(hs, nbs)], SCB2[:, 0:128], AF.Copy), waits=[t2], inc=sS)
                bank_done(SCB2, tokB)
                prevB = nbs
                yield
            pw = npw

        for jp in [2, 4, 8, 16, 32, 64]:
            tj = mm_group(SCB, slice(0, 128),
                          [(c_idb[:], big_d[:, SCR(hs, SL_MT)]),
                           (big_d[:, SCR(hs, aslot[jp])], big_d[:, SCR(hs, SL_MT)])],
                          [tX, atok[jp]], inc=sT)
            yield
            tX = b.emit("scalar", lambda e, hs=hs, SCB=SCB: e.activation(
                big_d[:, SCR(hs, SL_MT)], SCB[:, 0:128], AF.Copy), waits=[tj], inc=sS)
            bank_done(SCB, tX)
            yield
        tMt = tX
        MTF = SL_MT

        tw = mm_group(SCB2, slice(0, 128),
                      [(big_d[:, SCR(hs, SL_KB)], big_d[:, SCR(hs, MTF)])],
                      [tKb, tMt], inc=sT)
        yield
        tW0 = b.emit("scalar", lambda e, hs=hs, SCB2=SCB2: e.mul(big_d[:, SCR(hs, SL_W0)], SCB2[:, 0:128], -1.0),
                     waits=[tw], inc=sS)
        bank_done(SCB2, tW0)
        yield

        tt1 = mm_group(SCB, slice(0, 128), [(Kf, Qf)], [ev[0 + hs], ev[2 + hs]], inc=sT)
        yield
        tTt = b.emit("vector", lambda e, hs=hs, SCB=SCB: e.tensor_tensor(
            big_d[:, SCR(hs, SL_TT)], SCB[:, 0:128], c_mul[:], ALU.mult), waits=[tt1], inc=sV)
        bank_done(SCB, tTt)
        yield

        tu = mm_group(SCB2, slice(0, 128),
                      [(big_d[:, SCR(hs, MTF)], big_d[:, SCR(hs, SL_VB)]),
                       (big_d[:, SCR(hs, SL_W0)], S_b[:, hs * DH:(hs + 1) * DH])],
                      [tMt, tVb, tW0, st["sb"][hs]], inc=sT)
        yield
        tU = b.emit("scalar", lambda e, hs=hs, SCB2=SCB2: e.activation(big_d[:, SCR(hs, SL_U)], SCB2[:, 0:128], AF.Copy),
                    waits=[tu], inc=sS)
        bank_done(SCB2, tU)
        yield

        to = mm_group(SCB, slice(0, 128),
                      [(S_b[:, hs * DH:(hs + 1) * DH], Qf),
                       (big_d[:, SCR(hs, SL_U)], big_d[:, SCR(hs, SL_TT)])],
                      [st["sb"][hs], ev[0 + hs], tU, tTt], inc=sT)
        st["last_to"][hs] = to
        yield
        tosq = b.emit("scalar", lambda e, hs=hs, SCB=SCB: e.activation(
            big_d[:, SCR(hs, SL_A)], SCB[:, 0:128], AF.Square), waits=[to], inc=sS)
        yield
        tss2 = mm_group(RBank, (slice(0, 1), slice(256, 384)),
                        [(c_ocb[:, 0:1], big_d[:, SCR(hs, SL_A)])], [tosq], inc=sT)
        yield
        tonb = b.emit("vector", lambda e, hs=hs: e.tensor_copy(rb_o(hs), RB[hs][0:1, 256:384]),
                      waits=[tss2, st["rbo"][hs]], inc=sV)
        bank_done(RBank, tonb)
        yield
        trep = mm_group(RBank, slice(128, 256), [(c_orb[0:1, :], rb_o(hs))], [tonb], inc=sT)
        st["rbo"][hs] = trep
        yield
        tia = b.emit("vector", lambda e, hs=hs: e.tensor_scalar(ON1(hs), RB[hs][:, 128:256],
                                                                C_INVDH, C_EPS, ALU.mult, ALU.add),
                     waits=[trep, tk_c, st["oe"][hs]], inc=sV)
        bank_done(RBank, tia)
        yield
        tib = b.emit("scalar", lambda e, hs=hs: e.activation(ON2(hs), ON1(hs), AF.Sqrt), waits=[tia], inc=sS)
        yield
        trepe = b.emit("vector", lambda e, hs=hs: e.reciprocal(ON1(hs), ON2(hs)), waits=[tib], inc=sV)
        yield
        w_o = [trepe]
        if st["first_oT"]:
            w_o += [tk_h1st, tk_betmm[0], tk_betmm[1]]
            st["first_oT"] = False
        tOe = b.emit("vector", lambda e, hs=hs, c=c, SCB=SCB: e.scalar_tensor_tensor(
            big_b[:, oT_col(hs, c)], SCB[:, 0:128], c_onw[:, 0:1], ON1(hs),
            ALU.mult, ALU.mult), waits=w_o, inc=sV)
        bank_done(SCB, tOe)
        st["last_oT"] = tOe
        st["oe"][hs] = tOe
        yield

        tsu = mm_group(SCB2, slice(0, 128),
                       [(big_d[:, SCR(hs, SL_KT)], big_d[:, SCR(hs, SL_U)])],
                       [tKtok, tU], inc=sT)
        yield
        tsm = b.emit("vector", lambda e, hs=hs, SCB2=SCB2: e.tensor_add(
            S_m[:, hs * DH:(hs + 1) * DH], SCB2[:, 0:128], S_m[:, hs * DH:(hs + 1) * DH]),
            waits=[tsu, st["sm"][hs]], inc=sV)
        yield
        tsb = b.emit("scalar", lambda e, hs=hs: e.activation(
            S_b[:, hs * DH:(hs + 1) * DH], S_m[:, hs * DH:(hs + 1) * DH], AF.Copy),
            waits=[tsm], inc=sS)
        bank_done(SCB2, tsb)
        st["sm"][hs] = tsm
        st["sb"][hs] = tsb
        if ci == 3:
            st["brow_c"][hs] = [tbc, tbrb]

    NSC_LIM = NSC
    NPAIRS = max(1, NSC_LIM // 2)
    for pr in range(NPAIRS):
        pp = pr % 2
        jblk = pr // 2
        col0 = 512 * (pr % 2)
        hin_tok_cur = b.emit("sync", lambda e, jblk=jblk, col0=col0: e.dma_start(
            out=big_c[:, 0:8192].rearrange("p (f t) -> p f t", f=NF),
            in_=h1_all.rearrange("(j f p) t -> p j f t", p=128, f=NF)[:, jblk, :, col0:col0 + 512]),
            waits=[tk_ag2, hin_free], inc=sd_hin[pp], amt=16)
        tk_brow2 = [None, None]
        for hs in range(2):
            bbase = 32 * hs
            tk_brow2[hs] = b.emit("sync", lambda e, hs=hs, bbase=bbase, jblk=jblk, col0=col0: e.dma_start(
                out=lo16f[bbase:bbase + 1, 0:512],
                in_=bet_all[2 * jblk + hs:2 * jblk + hs + 1, col0:col0 + 512]),
                waits=st["brow_c"][hs], inc=sd_bcol[hs], amt=16)

        ev = []
        for gi, (proj, m) in enumerate([(0, 0), (0, 1), (1, 0), (1, 1), (2, 0), (2, 1)]):
            grp = mm_group(PROJ, slice(0, 512),
                           [(WQKV(k, proj, m), big_c[:, k * 512:(k + 1) * 512])
                            for k in range(NF)],
                           [hin_tok_cur, tk_wqkv_sync[0], tk_wqkv_sync[1]] + qkv_free[pp], inc=sT)
            qidx = proj * 2 + m
            if proj < 2:
                tsq = b.emit("scalar", lambda e: e.activation(big_d[:, SQ2], PROJ[:, 0:512], AF.Square),
                             waits=[grp], inc=sS)
                tss = mm_group(ROWS, (slice(0, 1), slice(0, 512)), [(c_ocb[:, 0:1], big_d[:, SQ2])],
                               [tsq], inc=sT)
                trb = b.emit("vector", lambda e: e.tensor_copy(rowb[0:1, 0:512], ROWS[0:1, 0:512]),
                             waits=[tss], inc=sV)
                bank_done(ROWS, trb)
                trm = mm_group(REPS, slice(0, 512), [(c_orb[0:1, :], rowb[0:1, 0:512])], [trb], inc=sT)
                tra = b.emit("scalar", lambda e: e.activation(L2A(), REPS[:, 0:512], AF.Sqrt),
                             waits=[trm], inc=sS)
                bank_done(REPS, tra)
                tmx = b.emit("vector", lambda e: e.tensor_scalar(L2B(), L2A(), C_TINY, None, ALU.max),
                             waits=[tra, tk_c], inc=sV)
                trr = b.emit("vector", lambda e: e.reciprocal(L2A(), L2B()), waits=[tmx], inc=sV)
                if proj == 0:
                    trr = b.emit("vector", lambda e: e.tensor_scalar(L2B(), L2A(), C_QKS, None, ALU.mult),
                                 waits=[trr], inc=sV)
                    fin = L2B
                else:
                    fin = L2A
                tev = b.emit("vector", lambda e, pp=pp, qidx=qidx, fin=fin: e.tensor_tensor(
                    big_d[:, QBP(pp, qidx)], PROJ[:, 0:512], fin(), ALU.mult),
                    waits=[trr], inc=sV)
            else:
                tev = b.emit("scalar", lambda e, pp=pp, qidx=qidx: e.activation(
                    big_d[:, QBP(pp, qidx)], PROJ[:, 0:512], AF.Silu), waits=[grp], inc=sS)
            ev.append(tev)
            bank_done(PROJ, tev)
        hin_free = grp

        for ci in range(4):
            gens = [chunk_head(pr, pp, ci, 0, ev, tk_brow2[0]),
                    chunk_head(pr, pp, ci, 1, ev, tk_brow2[1])]
            for g in gens:
                for _ in g:
                    pass
        qkv_free[pp] = [st["last_to"][0], st["last_to"][1]]

    last_oT = st["last_oT"]

    tk_ost = None
    for hs in range(2):
        tk_ost = b.emit("scalar", lambda e, hs=hs: e.dma_start(
            out=o_loc.rearrange("(r q p) t -> q p r t", p=128, q=2)[hs],
            in_=big_b[:, hs * 8192:(hs + 1) * 8192].rearrange("p (r t) -> p r t", r=8)),
            waits=[last_oT], inc=sd_o, amt=16)
    tk_a2a_o = b.emit("gpsimd", lambda e: e.collective_compute(
        "AllToAll", ALU.bypass, replica_groups=[list(range(NC))],
        ins=[o_loc], outs=[o_all]), waits=[tk_ost], inc=cc)

    if stage == "scan":
        prev_dd = None
        for hs in range(2):
            for blk in range(16):
                t1 = b.emit("vector", lambda e, hs=hs, blk=blk: e.tensor_copy(
                    LNF(), big_b[:, hs * 8192 + blk * 512: hs * 8192 + (blk + 1) * 512]),
                    waits=[last_oT, prev_dd], inc=sV)
                prev_dd = b.emit("scalar", lambda e, hs=hs, blk=blk: e.dma_start(
                    out=dbg["o_dbg"][hs * 128:(hs + 1) * 128, blk * 512:(blk + 1) * 512],
                    in_=LNF()), waits=[t1], inc=sd_out, amt=16)
        zero_out()
        emit_done([])
        return finish(nc, ctx, b), dbg

    tk_oall = b.emit("sync", lambda e: e.dma_start(
        out=big_b[:].rearrange("p (f t) -> p f t", f=NF),
        in_=o_all.rearrange("(f p) t -> p f t", p=128)),
        waits=[tk_a2a_o, last_oT], inc=sd_oall, amt=16)

    wop_free = [None, None]
    x2_toks = [None] * NF
    for m in range(NF):
        slm = m % 2
        tcv = None
        for kh in range(2):
            tcv = stream_piece(
                lambda m=m, kh=kh: wo.rearrange("(f p) c -> p f c", p=128)[
                    :, kh * 8:(kh + 1) * 8, m * 128:(m + 1) * 128],
                lambda slm=slm, kh=kh: dwn_b[:, 12288 + slm * 2048: 12288 + (slm + 1) * 2048].rearrange(
                    "p (f c) -> p f c", f=NF)[:, kh * 8:(kh + 1) * 8, :],
                extra_cv_waits=[wop_free[slm]], rows=128, ncols=1024, a=8)
        tcv_sync = stream_sync()
        tev = None
        for half in range(2):
            grp = mm_group(PROJ, slice(0, 512),
                           [(dwn_b[:, 12288 + slm * 2048 + k * 128: 12288 + slm * 2048 + (k + 1) * 128],
                             big_b[:, k * TL + half * 512: k * TL + half * 512 + 512]) for k in range(NF)],
                           tcv_sync + [tk_oall], inc=sT)
            tev = b.emit("vector", lambda e, m=m, half=half: e.scalar_tensor_tensor(
                big_a[:, m * TL + half * 512: m * TL + half * 512 + 512], PROJ[:, 0:512],
                mvec[:, 32 + m:33 + m], big_a[:, m * TL + half * 512: m * TL + half * 512 + 512],
                ALU.mult, ALU.add), waits=[grp, tk_ga], inc=sV)
            bank_done(PROJ, tev)
        x2_toks[m] = tev
        wop_free[slm] = tev

    tk_x2 = x2_toks[-1]
    tk_h2 = emit_ln(
        src=lambda f: big_a[:, f * TL:(f + 1) * TL],
        dst=lambda f: big_b[:, f * TL:(f + 1) * TL],
        cmod=lambda f: mvec[:, 112 + f:113 + f],
        smod=lambda f: mvec[:, 48 + f:49 + f],
        wait_stats=lambda f: [x2_toks[f], tk_x2],
        wait_apply=[tk_cb],
        rowbanks=(pb[3], pb[4], pb[5], pb[6]),
    )

    tk_gw = stream_piece(
        lambda: gwT.rearrange("(f p) c -> p f c", p=128),
        lambda: gw_b[:, :], rows=128, ncols=NF * E, a=NF)
    tk_gac1 = stream_piece(lambda: gA[:, 0:1024], lambda: lo16[0:16, 0:1024], rows=16, ncols=1024)
    tk_gac = stream_piece(lambda: gA[:, 1024:2048], lambda: lo16[0:16, 1024:2048], rows=16, ncols=1024)
    tk_gbc = stream_piece(lambda: gBT[:, :], lambda: lo16[0:16, 3072:3080], rows=16, ncols=8)
    tk_weff = None
    tk_router_sync = stream_sync()
    for f in range(NF):
        t1 = mm_group(REPS, slice(0, 8), [(lo16[0:16, f * 128:(f + 1) * 128], lo16[0:16, 3072:3080])],
                      tk_router_sync, inc=sT)
        tk_weff = b.emit("vector", lambda e, f=f: e.scalar_tensor_tensor(
            weff[:, f * 8:(f + 1) * 8], REPS[:, 0:8], C_TWO, gw_b[:, f * 8:(f + 1) * 8], ALU.mult, ALU.add),
            waits=[t1, tk_gw], inc=sV)
        bank_done(REPS, tk_weff)
    lrow = lo16f
    tk_w = None
    RT = lambda a, b2: rep_s[:, a:b2]
    prev_router = [tk_betst]
    rt_guard = []
    neg_guard = []
    for half in range(2):
        grp = mm_group(PROJ, (slice(0, 8), slice(0, 512)),
                       [(weff[:, f * 8:(f + 1) * 8], big_b[:, f * TL + half * 512: f * TL + half * 512 + 512])
                        for f in range(NF)], [tk_weff, tk_h2], inc=sT)
        tk_lge = b.emit("vector", lambda e: e.tensor_copy(
            lrow[0:8, 0:512], PROJ[0:8, 0:512]), waits=[grp] + prev_router, inc=sV)
        bank_done(PROJ, tk_lge)
        for tt_ in range(4):
            tile = half * 4 + tt_
            tt = tr(pb[3], slice(0, 8), lrow[0:8, tt_ * 128:(tt_ + 1) * 128], c_idf[0:8, 0:8],
                    [tk_lge], inc=sT)
            tl = b.emit("vector", lambda e, tile=tile: e.tensor_copy(wtok[:, tile * 8:(tile + 1) * 8], pb[3][:, 0:8]),
                        waits=[tt] + rt_guard, inc=sV)
            bank_done(pb[3], tl)
            Lap = wtok[:, tile * 8:(tile + 1) * 8]
            r1 = b.emit("vector", lambda e, Lap=Lap: e.reduce_max(RT(384, 385), Lap, axis=AX.X),
                        waits=[tl], inc=sV)
            r2 = b.emit("vector", lambda e, Lap=Lap: e.tensor_scalar(RT(400, 408), Lap, RT(384, 385), None, ALU.is_ge),
                        waits=[r1], inc=sV)
            r3 = b.emit("vector", lambda e, Lap=Lap: e.scalar_tensor_tensor(RT(424, 432), RT(400, 408), C_NEG30, Lap, ALU.mult, ALU.add),
                        waits=[r2], inc=sV)
            r4 = b.emit("vector", lambda e: e.reduce_max(RT(385, 386), RT(424, 432), axis=AX.X),
                        waits=[r3], inc=sV)
            tnm = b.emit("vector", lambda e: e.tensor_scalar(RT(386, 387), RT(384, 385), C_NEG1, None, ALU.mult),
                         waits=[r1] + neg_guard, inc=sV)
            tp = b.emit("scalar", lambda e, Lap=Lap: e.activation(RT(408, 416), Lap, AF.Exp, bias=RT(386, 387)),
                        waits=[tnm, tl], inc=sS)
            te2 = b.emit("scalar", lambda e: e.activation(RT(387, 388), RT(385, 386), AF.Exp, bias=RT(386, 387)),
                         waits=[tnm, r4], inc=sS)
            r5 = b.emit("vector", lambda e: e.tensor_scalar(RT(388, 389), RT(387, 388), C_ONE, None, ALU.add),
                        waits=[te2], inc=sV)
            r6 = b.emit("vector", lambda e: e.reciprocal(RT(387, 388), RT(388, 389)), waits=[r5], inc=sV)
            r7 = b.emit("vector", lambda e, Lap=Lap: e.tensor_scalar(RT(416, 424), Lap, RT(385, 386), None, ALU.is_ge),
                        waits=[r4, tl], inc=sV)
            r8 = b.emit("vector", lambda e: e.tensor_tensor(RT(400, 408), RT(408, 416), RT(416, 424), ALU.mult),
                        waits=[tp, r7, r3], inc=sV)
            tw_ = b.emit("vector", lambda e, tile=tile: e.tensor_scalar(
                wtok[:, tile * 8:(tile + 1) * 8], RT(400, 408), RT(387, 388), None, ALU.mult),
                waits=[r8, r6], inc=sV)
            tt2 = tr(pb[3], (slice(0, 8), slice(0, 128)), wtok[:, tile * 8:(tile + 1) * 8], c_idf[:], [tw_], inc=sT)
            tk_w = b.emit("vector", lambda e, tile=tile: e.tensor_copy(
                wrowb[0:8, tile * 128:(tile + 1) * 128], pb[3][0:8, 0:128]), waits=[tt2], inc=sV)
            bank_done(pb[3], tk_w)
            rt_guard = [tw_]
            neg_guard = [tp, te2]
        prev_router = [tk_w]
    tk_wrb = tk_w

    tk_gal = None
    for kh in range(2):
        tk_gal = stream_piece(
            lambda kh=kh: guAT_all.rearrange("(f p) c -> p f c", p=128)[:, kh * 8:(kh + 1) * 8, :],
            lambda kh=kh: big_d[:, 0:2048].rearrange("p (f c) -> p f c", f=NF)[:, kh * 8:(kh + 1) * 8, :],
            extra_cv_waits=[last_oT], rows=128, ncols=1024, a=8)
    tk_gal_sync = stream_sync()
    tk_tal = None
    for half in range(2):
        grp = mm_group(PROJ, slice(0, 512),
                       [(big_d[:, f * 128:(f + 1) * 128], big_b[:, f * TL + half * 512: f * TL + half * 512 + 512])
                        for f in range(NF)], tk_gal_sync + [tk_h2], inc=sT)
        tk_tal = b.emit("scalar", lambda e, half=half: e.mul(
            big_d[:, 2048 + half * 512: 2048 + half * 512 + 512], PROJ[:, 0:512], 2.0), waits=[grp], inc=sS)
        bank_done(PROJ, tk_tal)
    tk_talst = b.emit("scalar", lambda e: e.dma_start(
        out=tal_d.rearrange("(g p) t -> p g t", p=128),
        in_=big_d[:, 2048:3072].rearrange("p (g t) -> p g t", g=1)), waits=[tk_tal], inc=sd_tal, amt=16)

    if stage == "attn":
        dump_f32(dbg["h2_dbg"], lambda f: big_b[:, f * TL:(f + 1) * TL], NF, tk_wrb)
        dump_f32(dbg["x2_dbg"], lambda f: big_a[:, f * TL:(f + 1) * TL], NF, tk_wrb)
        for half in range(2):
            tww = b.emit("vector", lambda e, half=half: e.tensor_copy(
                lo16f[0:8, 0:512], wrowb[0:8, half * 512: half * 512 + 512]),
                waits=[tk_w, (sd_out, sd_out.n)], inc=sV)
            b.emit("sync", lambda e, half=half: e.dma_start(
                out=dbg["w_dbg"][:, half * 512: half * 512 + 512], in_=lo16f[0:8, 0:512]),
                waits=[tww], inc=sd_out, amt=16)
        zero_out()
        emit_done([(sd_tal, sd_tal.n)])
        return finish(nc, ctx, b), dbg

    GUPW = lambda k, q2: big_d[:, k * 512 + q2: k * 512 + q2 + 128]
    DNW = lambda k, mcol: dwn_b[:, k * 2048 + mcol: k * 2048 + mcol + 128]
    HC = lambda k, half: big_c[:, k * 1024 + half * 512: k * 1024 + half * 512 + 512]

    gup_guard = [tk_tal, (sd_tal, sd_tal.n)]
    wrow_guard = [tk_wrb, tk_weff]
    last_wrep = None
    dwn_guard = [tk_x2]
    hc_guard = [tk_h2, tk_lge, tk_tal]
    Te_guard = [(sd_tal, sd_tal.n)]
    guB_guard = [tk_weff]
    dnB_guard = [tk_weff, tk_gbc]
    last_out = None
    gup_mm_last = None
    last_lora = None
    for e_i in range(E):
        tk_Te = b.emit("sync", lambda e, e_i=e_i: e.dma_start(
            out=lo16[0:16, 2048:3072].rearrange("r (a t) -> r a t", a=1),
            in_=tal_d.rearrange("(e2 r) t -> r e2 t", r=R)[:, e_i:e_i + 1, :]),
            waits=Te_guard, inc=sd_te, amt=16)
        guB_t = []
        for kh in range(2):
            guB_t.append(stream_piece(
                lambda e_i=e_i, kh=kh: guBT[e_i][:, kh * 1024:(kh + 1) * 1024],
                lambda kh=kh: lo16[0:16, kh * 1024:(kh + 1) * 1024],
                extra_cv_waits=guB_guard, rows=16, ncols=1024))
        tk_guB2 = guB_t
        dnB_t = []
        for kh in range(2):
            dnB_t.append(stream_piece(
                lambda e_i=e_i, kh=kh: dnBT[e_i][:, kh * 1024:(kh + 1) * 1024],
                lambda kh=kh: lo16[32:48, kh * 1024:(kh + 1) * 1024],
                extra_cv_waits=dnB_guard, rows=16, ncols=1024))
        tk_dnB2 = dnB_t
        tk_wrl = b.emit("scalar", lambda e, e_i=e_i: e.dma_start(
            out=lo16[0:1, 3072:4096], in_=wrowb[e_i:e_i + 1, :]),
            waits=wrow_guard, inc=sd_wr, amt=16)
        tk_dnA = stream_piece(
            lambda e_i=e_i: dnAT[e_i].rearrange("(k p) r -> p k r", p=128),
            lambda: dnA_s[:, :],
            extra_cv_waits=guB_guard, rows=128, ncols=128, a=8)

        def emit_qb_pieces(qb, waits_per_pi):
            for pi_ in range(4):
                j = pi_ % 2
                src_col = (1024 if pi_ >= 2 else 0) + 256 * qb + 128 * j
                q2 = (256 if pi_ >= 2 else 0) + 128 * j
                for kh in range(2):
                    stream_piece(
                        lambda e_i=e_i, src_col=src_col, kh=kh: gupw[e_i].rearrange(
                            "(f p) c -> p f c", p=128)[:, kh * 8:(kh + 1) * 8, src_col:src_col + 128],
                        lambda q2=q2, kh=kh: big_d[:, 0:8192].rearrange("p (k c) -> p k c", k=NF)[
                            :, kh * 8:(kh + 1) * 8, q2:q2 + 128],
                        extra_cv_waits=waits_per_pi(pi_), rows=128, ncols=1024, a=8)
            return stream_sync()

        def emit_dn_pieces(lo, hi):
            for piece in range(lo, hi):
                stream_piece(
                    lambda e_i=e_i, piece=piece: dnw[e_i].rearrange("(k p) c -> p k c", p=128)[
                        :, :, piece * 128:(piece + 1) * 128],
                    lambda piece=piece: dwn_b[:].rearrange("p (k c) -> p k c", k=8)[
                        :, :, piece * 128:(piece + 1) * 128],
                    extra_cv_waits=(dwn_guard if (e_i == 0 and piece < 2) else
                                    ([last_lora] if (piece < 2 and last_lora) else [])),
                    rows=128, ncols=1024, a=8)
            return stream_sync()

        next_gup_sync = emit_qb_pieces(
            0, lambda pi_: (gup_guard if e_i == 0 else []) + ([gup_mm_last] if gup_mm_last else []))

        tk_dwn_sync = None
        gg_h1 = {}
        gh_h1 = {}
        for qb in range(4):
            tk_gup_sync = next_gup_sync

            for half in range(2):
                hwsl = slice(half * 512, half * 512 + 512)
                trm = mm_group(REPS, slice(0, 512), [(c_orb[0:1, :], lo16[0:1, 3072 + half * 512:3072 + half * 512 + 512])],
                               [tk_wrl], inc=sT)
                last_wrep = trm
                glast = [gup_mm_last] if gup_mm_last else []
                trw = b.emit("vector", lambda e: e.tensor_copy(rep_s[:, 0:512], REPS[:, 0:512]),
                             waits=[trm] + glast, inc=sV)
                bank_done(REPS, trw)
                for mpl in range(2):
                    kt_h = qb * 2 + mpl
                    glast = [gup_mm_last] if gup_mm_last else []
                    gg = mm_group(PROJ, slice(0, 512),
                                  [(GUPW(k, mpl * 128), big_b[:, k * TL + half * 512: k * TL + half * 512 + 512])
                                   for k in range(NF)] +
                                  [(lo16[0:16, (256 * qb + mpl * 128):(256 * qb + (mpl + 1) * 128)],
                                    lo16[0:16, 2048 + half * 512: 2048 + half * 512 + 512])],
                                  tk_gup_sync + [tk_h2] + tk_guB2 + [tk_Te], inc=sT)
                    tsg = b.emit("scalar", lambda e: e.activation(rep1[:, :], PROJ[:, 0:512], AF.Silu),
                                 waits=[gg] + glast, inc=sS)
                    bank_done(PROJ, tsg)
                    gh = mm_group(pb[4], slice(0, 512),
                                  [(GUPW(k, 256 + mpl * 128), big_b[:, k * TL + half * 512: k * TL + half * 512 + 512])
                                   for k in range(NF)] +
                                  [(lo16[0:16, (1024 + 256 * qb + mpl * 128):(1024 + 256 * qb + (mpl + 1) * 128)],
                                    lo16[0:16, 2048 + half * 512: 2048 + half * 512 + 512])],
                                  tk_gup_sync + [tk_h2] + tk_guB2 + [tk_Te], inc=sT)
                    thw = b.emit("vector", lambda e: e.tensor_tensor(rep2[:, :], pb[4][:, 0:512],
                                                                     rep_s[:, 0:512], ALU.mult),
                                 waits=[gh, trw] + glast, inc=sV)
                    bank_done(pb[4], thw)
                    w10 = [thw, tsg] + (hc_guard if (e_i == 0 and qb == 0) else [])
                    gup_mm_last = b.emit("vector", lambda e, kt_h=kt_h, half=half: e.tensor_tensor(
                        HC(kt_h, half), rep1[:, :], rep2[:, :], ALU.mult), waits=w10, inc=sV)
                    if half == 1:
                        gg_h1[mpl] = gg
                        gh_h1[mpl] = gh

            if qb == 0:
                tk_dwn_sync = emit_dn_pieces(0, 4)
            if qb < 3:
                next_gup_sync = emit_qb_pieces(
                    qb + 1, lambda pi_: [(gg_h1 if pi_ < 2 else gh_h1)[pi_ % 2]])
            if qb == 1:
                tk_dwn_sync = emit_dn_pieces(4, 10)
            if qb == 2:
                tk_dwn_sync = emit_dn_pieces(10, 16)
        tk_hcap = gup_mm_last

        dn_last = None
        for half in range(2):
            grp = mm_group(ROWS, (slice(0, 16), slice(0, 512)),
                           [(dnA_s[:, k * 16:(k + 1) * 16], HC(k, half)) for k in range(8)],
                           [tk_dnA, tk_hcap], inc=sT)
            tk_T2 = b.emit("scalar", lambda e: e.mul(lo16[32:48, 2048:2560], ROWS[0:16, 0:512], 2.0),
                           waits=[grp], inc=sS)
            bank_done(ROWS, tk_T2)
            for m_ in range(NF):
                DB = pb[5] if m_ % 2 == 0 else pb[6]
                grp = mm_group(DB, slice(0, 512),
                               [(DNW(k, m_ * 128), HC(k, half)) for k in range(8)] +
                               [(lo16[32:48, m_ * 128:(m_ + 1) * 128], lo16[32:48, 2048:2560])],
                               tk_dwn_sync + [tk_hcap, tk_T2] + tk_dnB2, inc=sT)
                last_lora = grp
                tev = b.emit("vector", lambda e, m_=m_, half=half, DB=DB: e.scalar_tensor_tensor(
                    big_a[:, m_ * TL + half * 512: m_ * TL + half * 512 + 512], DB[:, 0:512],
                    mvec[:, 80 + m_:81 + m_], big_a[:, m_ * TL + half * 512: m_ * TL + half * 512 + 512],
                    ALU.mult, ALU.add), waits=[grp, tk_gb], inc=sV)
                bank_done(DB, tev)
                dn_last = tev
                if e_i == E - 1:
                    last_out = b.emit("scalar", lambda e, m_=m_, half=half: e.dma_start(
                        out=outT[m_ * 128:(m_ + 1) * 128, half * 512: half * 512 + 512],
                        in_=big_a[:, m_ * TL + half * 512: m_ * TL + half * 512 + 512]),
                        waits=[tev], inc=sd_out, amt=16)
        guB_guard = [gup_mm_last]
        dnB_guard = [last_lora]
        Te_guard = [gup_mm_last]
        wrow_guard = [last_wrep]

    emit_done([])
    return finish(nc, ctx, b), dbg


def finish(nc, ctx, b):
    with ctx:
        with nc.Block() as block:
            @block.sync
            def _(eng):
                b.run("sync", eng)

            @block.scalar
            def _(eng):
                b.run("scalar", eng)

            @block.vector
            def _(eng):
                b.run("vector", eng)

            @block.tensor
            def _(eng):
                b.run("tensor", eng)

            @block.gpsimd
            def _(eng):
                b.run("gpsimd", eng)
    return nc


def make_consts():
    ii = np.arange(128)
    ident = np.eye(128, dtype=np.float32)
    cst = np.zeros((128, 16), np.float32)
    for k, v in enumerate([0.0, 2.0, -1.0, 0.25, 1.0, 1e-5, 1e-12, QKS,
                           1.0 / D, 1.0 / DH, -1e30]):
        cst[:, k] = v
    return {
        "cstc": cst,
        "identf": ident,
        "identb": ident.astype(ml_dtypes.bfloat16),
        "ones_cb": np.ones((128, 1), ml_dtypes.bfloat16),
        "ones_rb": np.ones((1, 128), ml_dtypes.bfloat16),
        "ones_rf": np.ones((1, 128), np.float32),
        "mask_sl": (ii[None, :] < ii[:, None]).astype(ml_dtypes.bfloat16),
        "mask_su": (ii[None, :] > ii[:, None]).astype(ml_dtypes.bfloat16),
        "mask_ul": (ii[None, :] >= ii[:, None]).astype(ml_dtypes.bfloat16),
    }


def make_inmaps(inputs):
    f32 = np.float32
    x = np.ascontiguousarray(np.asarray(inputs["L0_attn_hidden_states"], f32)[0])
    c = np.asarray(inputs["L0_c"], f32)
    wmod_h = np.asarray(inputs["L0_wmod"], f32)
    Wq = np.asarray(inputs["L0_attn_Wq"], f32)
    Wk = np.asarray(inputs["L0_attn_Wk"], f32)
    Wv = np.asarray(inputs["L0_attn_Wv"], f32)
    Wb = np.ascontiguousarray(np.asarray(inputs["L0_attn_Wb"], f32))
    onw_h = np.ascontiguousarray(np.asarray(inputs["L0_attn_o_norm_weight"], f32).reshape(128, 1))
    Wo = np.ascontiguousarray(np.asarray(inputs["L0_attn_Wo"], f32))
    gw = np.asarray(inputs["L0_ffn_gate_weight"], f32)
    gA_h = np.ascontiguousarray(np.asarray(inputs["L0_ffn_gate_lora_A"], f32))
    gB_h = np.asarray(inputs["L0_ffn_gate_lora_B"], f32)
    gup_h = np.ascontiguousarray(np.asarray(inputs["L0_ffn_gate_up_proj"], f32))
    dwn_h = np.ascontiguousarray(np.asarray(inputs["L0_ffn_down_proj"], f32))
    guA_h = np.asarray(inputs["L0_ffn_gup_lora_A"], f32)
    guB_h = np.asarray(inputs["L0_ffn_gup_lora_B"], f32)
    dnA_h = np.asarray(inputs["L0_ffn_down_lora_A"], f32)
    dnB_h = np.asarray(inputs["L0_ffn_down_lora_B"], f32)
    consts = make_consts()
    guAT_all_h = np.ascontiguousarray(np.concatenate([guA_h[e] for e in range(E)], axis=0).T)
    shared = {
        "cT": np.ascontiguousarray(c.reshape(16, 128).T),
        "wb": Wb,
        "onw": onw_h,
        "wo": Wo,
        "gwT": np.ascontiguousarray(gw.T),
        "gA": gA_h,
        "gBT": np.ascontiguousarray(gB_h.T),
        "gupw": gup_h,
        "dnw": dwn_h,
        "guAT_all": guAT_all_h,
        "guBT": np.ascontiguousarray(guB_h.transpose(0, 2, 1)),
        "dnAT": np.ascontiguousarray(dnA_h.transpose(0, 2, 1)),
        "dnBT": np.ascontiguousarray(dnB_h.transpose(0, 2, 1)),
    }
    shared.update(consts)
    in_maps = []
    for i in range(NC):
        m = dict(shared)
        m["xT"] = np.ascontiguousarray(x[i * TL:(i + 1) * TL, :].T)
        m["wmod"] = np.ascontiguousarray(wmod_h[:, i * (6 * D // NC):(i + 1) * (6 * D // NC)])
        m["wq"] = np.ascontiguousarray(Wq[:, i * 256:(i + 1) * 256])
        m["wk"] = np.ascontiguousarray(Wk[:, i * 256:(i + 1) * 256])
        m["wv"] = np.ascontiguousarray(Wv[:, i * 256:(i + 1) * 256])
        in_maps.append(m)
    return in_maps


_NC_CACHE = {}


def run_stage(inputs, stage):
    if stage not in _NC_CACHE:
        _NC_CACHE[stage] = build_nc(stage)
    nc, dbg = _NC_CACHE[stage]
    in_maps = make_inmaps(inputs)
    return run_bass_kernel_spmd(nc, in_maps, core_ids=list(range(NC)))


def kernel(**inputs):
    res = run_stage(inputs, "full")
    out = np.empty((1, T, D), np.float32)
    for i in range(NC):
        out[0, i * TL:(i + 1) * TL, :] = res.results[i]["outT"].T
    return out
